# version 13
# baseline (speedup 1.0000x reference)
import numpy as np

import concourse.bass as bass
import concourse.mybir as mybir
from concourse.bass_utils import run_bass_kernel_spmd

F32 = mybir.dt.float32
F16 = mybir.dt.float16
AF = mybir.ActivationFunctionType
ALU = mybir.AluOpType

D = 1024
S = 2048
NSEQ = 2
T = NSEQ * S
NH = 16
NCH = 8
TG = 512
NTG = S // TG
NE = 4
DFF = 4096
NIC = DFF // 128
SCALE = 1.0 / 32.0
MASKVAL = -30000.0
LN_EPS_OUT = 1e-5
LN_EPS_IN = 1e-12
CCAP = 640
GR = 320
SLOTS = NE * CCAP
TRASH = NSEQ * SLOTS
NXG = TRASH + 128

CAP = 205312


def _round32(x):
    return (x + 31) // 32 * 32


class Arena:

    _ctr = [0]

    def __init__(self, nc, cap):
        self.nc = nc
        self.cap = cap
        self.free = [(0, cap)]
        self.live = {}
        Arena._ctr[0] += 1
        self.backing = nc.alloc_sbuf_tensor(f"arena{Arena._ctr[0]}", [128, cap],
                                            mybir.dt.uint8)
        self.base_addr = nc.lookup_mloc(self.backing).addr

    def alloc(self, name, shape, dtype):
        per_part = int(np.prod(shape[1:])) * mybir.dt.size(dtype)
        need = _round32(per_part)
        for idx, (off, sz) in enumerate(self.free):
            if sz >= need:
                self.free[idx] = (off + need, sz - need)
                if self.free[idx][1] == 0:
                    self.free.pop(idx)
                h = self.nc.alloc_sbuf_tensor_at(name, list(shape), dtype,
                                                 offset=self.base_addr + off)
                self.live[name] = (off, need)
                return h
        raise MemoryError(f"sbuf arena: no room for {name} ({need}B); free={self.free}")

    def freebuf(self, *names):
        for name in names:
            off, sz = self.live.pop(name)
            self.free.append((off, sz))
        self.free.sort()
        out = []
        for off, sz in self.free:
            if out and out[-1][0] + out[-1][1] == off:
                out[-1] = (out[-1][0], out[-1][1] + sz)
            else:
                out.append((off, sz))
        self.free = [tuple(x) for x in out]


class Walk:

    def __init__(self, me, handle, sems):
        self.me = me
        self.E = handle
        self.sems = sems
        self.cnt = {k: 0 for k in sems}
        self.last_wait = {}
        self.marks = {}

    def wait(self, eng, *pairs):
        if self.me == eng:
            for s, v in pairs:
                if v <= 0 or self.last_wait.get(s, -1) >= v:
                    continue
                self.E.wait_ge(self.sems[s], v)
                self.last_wait[s] = v

    def do(self, eng, fn, inc=None, wait=None):
        if self.me == eng:
            if wait:
                self.wait(eng, *wait)
            inst = fn(self.E)
            if inc:
                inst.then_inc(self.sems[inc[0]], inc[1])
        if inc:
            self.cnt[inc[0]] += inc[1]

    def drain(self, eng):
        if self.me == eng:
            self.E.drain()

    def dma(self, eng, out, in_, sem, wait=None):
        self.do(eng, lambda E: E.dma_start(out=out, in_=in_), inc=(sem, 16), wait=wait)

    def dmal(self, eng, out, in_, wait=None):
        pool = self.pools[eng]
        lane = pool[self.rr[eng] % len(pool)]
        self.rr[eng] += 1
        ws = [(lane, self.cnt[lane])]
        if wait:
            ws += list(wait)
        self.do(eng, lambda E: E.dma_start(out=out, in_=in_), inc=(lane, 16), wait=ws)
        return (lane, self.cnt[lane])

    def waitpool(self, eng, pool_of):
        for lane in self.pools[pool_of]:
            self.wait(eng, (lane, self.cnt[lane]))

    def mm(self, out, lhsT, rhs, start, stop, tp=None, wait=None, inc=None):
        self.do("t",
                lambda E: E.matmul(out, lhsT, rhs, start=start, stop=stop,
                                   tile_position=tp),
                inc=inc, wait=wait)

    def mark(self, name, sem):
        self.marks[name] = [(sem, self.cnt[sem])]

    def wmark(self, eng, name):
        for s, v in self.marks.get(name, ()):
            self.wait(eng, (s, v))

    def barrier(self):
        snap = [(s, v) for s, v in self.cnt.items() if v > 0]
        for eng in ("q", "g", "t", "s", "v"):
            self.wait(eng, *snap)


def build(stage="full", debug=False):
    nc = bass.Bass("TRN2", target_bir_lowering=False)

    xT_d = nc.dram_tensor("xT", [D, T], F16, kind="ExternalInput").ap()
    xrows_d = nc.dram_tensor("xrows", [T, D], F16, kind="ExternalInput").ap()
    qwp_d = nc.dram_tensor("qwp", [D, D], F16, kind="ExternalInput").ap()
    kwp_d = nc.dram_tensor("kwp", [D, D], F16, kind="ExternalInput").ap()
    vwp_d = nc.dram_tensor("vwp", [D, 1040], F16, kind="ExternalInput").ap()
    ow_d = nc.dram_tensor("oww", [D, D], F16, kind="ExternalInput").ap()
    wr_d = nc.dram_tensor("wrc", [128, 32], F32, kind="ExternalInput").ap()
    w1_d = nc.dram_tensor("w1r", [NE, NIC, 128, 1024], F32, kind="ExternalInput").ap()
    b1_d = nc.dram_tensor("b1c", [128, 128], F32, kind="ExternalInput").ap()
    wo_d = nc.dram_tensor("wor2", [NCH, 128, NIC, 128], F32, kind="ExternalInput").ap()
    bo_d = nc.dram_tensor("bor", [1, D], F32, kind="ExternalInput").ap()
    gam_d = nc.dram_tensor("gamt", [128, D], F32, kind="ExternalInput").ap()
    bet_d = nc.dram_tensor("bett", [128, D], F32, kind="ExternalInput").ap()
    out_d = nc.dram_tensor("out", [T, D], F32, kind="ExternalOutput").ap()

    x1f_d = nc.dram_tensor("x1f", [T, D], F32, kind="Internal").ap()
    ghi_d = nc.dram_tensor("ghi", [NIC, 128, NSEQ * SLOTS], F16, kind="Internal").ap()
    glo_d = nc.dram_tensor("glo", [NIC, 128, NSEQ * SLOTS], F16, kind="Internal").ap()
    xg_d = nc.dram_tensor("xg", [NXG, D], F32, kind="Internal").ap()
    yg_d = nc.dram_tensor("yg", [NXG, D], F32, kind="Internal").ap()
    slot_d = nc.dram_tensor("slotd", [NSEQ * NTG, TG], F32, kind="Internal").ap()

    dbg = {}
    if debug and stage == "qkv":
        dbg["qT"] = nc.dram_tensor("dbg_qT", [D, S], F16, kind="ExternalOutput").ap()
        dbg["kT"] = nc.dram_tensor("dbg_kT", [D, S], F16, kind="ExternalOutput").ap()
        dbg["vp"] = nc.dram_tensor("dbg_vp", [128, 16 * 1040], F16, kind="ExternalOutput").ap()
    if debug and stage == "attn":
        dbg["at_hi"] = nc.dram_tensor("dbg_athi", [D, TG], F16, kind="ExternalOutput").ap()
        dbg["at_lo"] = nc.dram_tensor("dbg_atlo", [D, TG], F16, kind="ExternalOutput").ap()
    if debug and stage == "ln1":
        dbg["x1"] = nc.dram_tensor("dbg_x1", [T, D], F32, kind="ExternalOutput").ap()
    if debug and stage == "slots":
        dbg["idx"] = nc.dram_tensor("dbg_idx", [128, 32], F32, kind="ExternalOutput").ap()

    pos = np.arange(S, dtype=np.float64)
    inv = 10000.0 ** (-np.arange(32, dtype=np.float64) / 32.0)
    freq = inv[:, None] * pos[None, :]
    cc_d = nc.inline_tensor(np.tile(np.cos(freq), (4, 1)).astype(np.float32), "cc_c")
    ss_d = nc.inline_tensor(np.tile(np.sin(freq), (4, 1)).astype(np.float32), "ss_c")
    masks_np = np.zeros((4, 128, TG), np.float16)
    for di in range(4):
        k_idx = np.arange(128)[:, None] + di * 128
        q_idx = np.arange(TG)[None, :]
        masks_np[di][k_idx > q_idx] = np.float16(MASKVAL)
    mask_d = [nc.inline_tensor(masks_np[i], f"mask{i}_c") for i in range(4)]
    id16_d = nc.inline_tensor(np.eye(128, dtype=np.float16), "id16_c")
    id32_d = nc.inline_tensor(np.eye(128, dtype=np.float32), "id32_c")
    sel64_np = np.zeros((1, 128), np.float32)
    sel64_np[0, :64] = 1.0
    sel64_d = nc.inline_tensor(sel64_np, "sel64_c")
    sel64b_np = np.zeros((1, 128), np.float32)
    sel64b_np[0, 64:] = 1.0
    sel64b_d = nc.inline_tensor(sel64b_np, "sel64b_c")
    ones128_d = nc.inline_tensor(np.ones((1, 128), np.float16), "ones128_c")
    ones512_d = nc.inline_tensor(np.ones((1, 512), np.float32), "ones512_c")
    trrow_np = (TRASH + (np.arange(512) % 128)).astype(np.float32)[None, :]
    trrow_d = nc.inline_tensor(trrow_np, "trrow_c")

    AP_ = Arena(nc, 40448)
    AR1 = Arena(nc, 63488)
    AR2 = Arena(nc, 98816)

    class Multi:

        def alloc(self, name, shape, dtype):
            try:
                return AR1.alloc(name, shape, dtype)
            except MemoryError:
                return AR2.alloc(name, shape, dtype)

    AC = Multi()

    id16_s = AP_.alloc("id16", [128, 128], F16)
    id32_s = AP_.alloc("id32", [128, 128], F32)
    sel64_s = AP_.alloc("sel64", [1, 128], F32)
    sel64b_s = AP_.alloc("sel64b", [1, 128], F32)
    ones128_s = AP_.alloc("ones128", [1, 128], F16)
    wr32_s = AP_.alloc("wr32", [128, 32], F32)
    wrh_s = AP_.alloc("wrh", [128, 32], F16)
    wrl_s = AP_.alloc("wrl", [128, 32], F16)
    b1_s = AP_.alloc("b1", [128, 128], F32)
    slab32_s = [AP_.alloc(f"slab32_{i}", [128, 1024], F32) for i in range(2)]
    xrow_s = [AP_.alloc(f"xrow{i}", [128, D], F16) for i in range(4)]
    z_s = [AP_.alloc(f"z{i}", [128, D], F32) for i in range(2)]
    sqd_s = AP_.alloc("sqd", [128, D], F32)
    vec_s = AP_.alloc("vecs", [128, 32], F32)
    x1t_s = [AP_.alloc(f"x1t{i}", [128, D], F32) for i in range(2)]
    eps5_s = AP_.alloc("eps5", [128, 1], F32)
    eps12_s = AP_.alloc("eps12", [128, 1], F32)
    idxf_s = AP_.alloc("idxf", [128, 32], F32)
    idxi_s = AP_.alloc("idxi", [128, 32], mybir.dt.int32)

    xts_s = [AR1.alloc(f"xts{c}", [128, S], F16) for c in range(NCH)]
    wch_s = [AR1.alloc(f"wch{c}", [128, 1040], F16) for c in range(NCH)]
    ccs_s = AR1.alloc("ccs", [128, TG], F32)
    sss_s = AR1.alloc("sss", [128, TG], F32)
    rp_s = [AR1.alloc(f"rp{i}", [128, TG], F32) for i in range(4)]
    AR1.freebuf(*([f"xts{c}" for c in range(NCH)] + [f"wch{c}" for c in range(NCH)]
                  + ["ccs", "sss"] + [f"rp{i}" for i in range(4)]))
    e32_s = [AR1.alloc(f"e32_{i}", [128, 1024], F32) for i in range(2)]
    ehi_s = [AR1.alloc(f"ehi_{i}", [128, 1024], F16) for i in range(2)]
    elo_s = [AR1.alloc(f"elo_{i}", [128, 1024], F16) for i in range(2)]
    rec_s = AR1.alloc("rec", [1, 2048], F32)
    bsb_s = AR1.alloc("bsb", [64, 2048], F32)
    t32_s = AR1.alloc("t32", [128, TG], F32)
    athi_s = [AR1.alloc(f"athi{c}", [128, TG], F16) for c in range(NCH)]
    atlo_s = [AR1.alloc(f"atlo{c}", [128, TG], F16) for c in range(NCH)]
    owh_s = [AR1.alloc(f"owh{i}", [128, D], F16) for i in range(4)]
    msk_s = [AR1.alloc(f"msk{i}", [128, TG], F16) for i in range(4)]
    AR1.freebuf(*(["e32_0", "e32_1", "ehi_0", "ehi_1", "elo_0", "elo_1",
                   "rec", "bsb", "t32"]
                  + [f"athi{c}" for c in range(NCH)] + [f"atlo{c}" for c in range(NCH)]
                  + [f"owh{i}" for i in range(4)] + [f"msk{i}" for i in range(4)]))
    lgs_s = AR1.alloc("lgs", [4, TG], F32)
    lrow_s = AR1.alloc("lrow", [1, NE * TG], F32)
    rtmp_s = AR1.alloc("rtmp", [1, 7 * TG], F32)
    onesr_s = AR1.alloc("onesr", [1, TG], F32)
    trrow_s = AR1.alloc("trrow", [1, TG], F32)
    carr_s = AR1.alloc("carr", [1, 32], F32)
    ovi_s = AR1.alloc("ovi", [1, TG], mybir.dt.int32)
    slotr_s = AR1.alloc("slotr", [1, TG], F32)
    AR1.freebuf("lgs", "lrow", "rtmp", "onesr", "trrow", "carr", "ovi", "slotr")
    slabh_s = [[AR1.alloc(f"slabh{i}_{e}", [128, 1024], F16) for e in range(NE)]
               for i in range(2)]
    slabl_s = [[AR1.alloc(f"slabl{i}_{e}", [128, 1024], F16) for e in range(NE)]
               for i in range(2)]
    gf_s = [AR1.alloc(f"gfull{i}", [128, GR], F32) for i in range(2)]
    ghi_sb = [AR1.alloc(f"ghisb{i}", [128, GR], F16) for i in range(2)]
    glo_sb = [AR1.alloc(f"glosb{i}", [128, GR], F16) for i in range(2)]
    AR1.freebuf(*[f"slabh{i}_{e}" for i in range(2) for e in range(NE)],
                *[f"slabl{i}_{e}" for i in range(2) for e in range(NE)],
                "gfull0", "gfull1", "ghisb0", "ghisb1", "glosb0", "glosb1")

    qT_s = [AR2.alloc(f"qT{c}", [128, S], F16) for c in range(NCH)]
    kT_s = [AR2.alloc(f"kT{c}", [128, S], F16) for c in range(NCH)]
    vp_s = AR2.alloc("vp", [128, 16 * 1040], F16)
    AR2.freebuf(*([f"qT{c}" for c in range(NCH)] + [f"kT{c}" for c in range(NCH)]
                  + ["vp"]))
    xgTh_s = [AR2.alloc(f"xgTh{c}", [128, SLOTS], F16) for c in range(NCH)]
    xgTl_s = [AR2.alloc(f"xgTl{c}", [128, SLOTS], F16) for c in range(NCH)]
    x1Tt_s = [AR2.alloc(f"x1Tt{i}", [128, 1024], F16) for i in range(4)]
    AR2.freebuf(*([f"xgTh{c}" for c in range(NCH)] + [f"xgTl{c}" for c in range(NCH)]
                  + [f"x1Tt{i}" for i in range(4)]))
    WB = 512
    gslh_s = [AC.alloc(f"gslh{ic}", [128, WB], F16) for ic in range(NIC)]
    gsll_s = [AC.alloc(f"gsll{ic}", [128, WB], F16) for ic in range(NIC)]
    wjh_s = [AC.alloc(f"wjh{i}", [128, NIC * 128], F16) for i in range(2)]
    wjl_s = [AC.alloc(f"wjl{i}", [128, NIC * 128], F16) for i in range(2)]
    wsp_h = AC.alloc("wsph", [128, 1024], F16)
    wsp_l = AC.alloc("wspl", [128, 1024], F16)
    ysb_s = AC.alloc("ysb", [128, WB], F32)
    yrow_s = [AC.alloc(f"yrow{i}", [128, D], F32) for i in range(4)]
    x1r_s = [AC.alloc(f"x1r{i}", [128, D], F32) for i in range(2)]
    outt_s = [AC.alloc(f"outt{i}", [128, D], F32) for i in range(2)]
    gam_s = AC.alloc("gams", [128, D], F32)
    bet_s = AC.alloc("bets", [128, D], F32)
    bo_s = AC.alloc("bo", [1, D], F32)
    ones512_s = AC.alloc("ones512", [1, 512], F32)

    PA = nc.alloc_psum_tensor("PA", [128, 2048], F32)
    PB = nc.alloc_psum_tensor("PB", [128, 1024], F32)
    PC = nc.alloc_psum_tensor("PC", [128, 1024], F32)

    QL = [f"qa{i}" for i in range(6)]
    GL = [f"gb{i}" for i in range(4)]
    sem_handles = {n: nc.alloc_semaphore(name=f"sem_{n}")
                   for n in tuple(QL) + tuple(GL) + ("t", "s", "v")}

    def walk(me, E):
        w = Walk(me, E, sem_handles)
        w.pools = {"q": QL, "g": GL}
        w.rr = {"q": 0, "g": 0}
        C = w.cnt
        mark, wmark = w.mark, w.wmark

        def qsn():
            return [(l, C[l]) for l in QL]

        def gsn():
            return [(l, C[l]) for l in GL]

        for dst, srcp in ((id16_s, id16_d.ap()), (id32_s, id32_d.ap()),
                          (sel64_s, sel64_d.ap()), (sel64b_s, sel64b_d.ap()),
                          (ones128_s, ones128_d.ap()),
                          (wr32_s, wr_d), (b1_s, b1_d)):
            w.dmal("q", dst[:], srcp)
        const_tok = qsn()
        w.do("v", lambda E: E.memset(eps5_s[:], LN_EPS_OUT))
        w.do("v", lambda E: E.memset(eps12_s[:], LN_EPS_IN))
        w.drain("v")
        w.do("s", lambda E: E.activation(wrh_s[:], wr32_s[:], AF.Copy),
             inc=("s", 1), wait=const_tok)
        sC = C["s"]
        w.do("v", lambda E: E.tensor_tensor(wrl_s[:], wr32_s[:], wrh_s[:], ALU.subtract),
             inc=("v", 1), wait=[("s", sC)])
        w.do("v", lambda E: E.memset(z_s[0][:], 0.0), inc=("v", 1))
        w.drain("v")
        zf_v = C["v"]
        for r0 in range(0, NXG, 128):
            w.dmal("g", xg_d[r0:r0 + 128, :], z_s[0][:], wait=[("v", zf_v)])
            w.dmal("g", yg_d[r0:r0 + 128, :], z_s[0][:])
        zf_tok = gsn()
        w.marks["z0"] = list(zf_tok)

        def qkv_phase(s_idx):
            tok0 = s_idx * S
            for c in range(NCH):
                wmark("q", f"xts{c}")
                w.dmal("q", xts_s[c][:], xT_d[c * 128:(c + 1) * 128, tok0:tok0 + S])
            x_tok = qsn()

            for wsrc, dstT in ((qwp_d, qT_s), (kwp_d, kT_s)):
                wmark("q", "wch")
                for dc in range(NCH):
                    w.dmal("q", wch_s[dc][:, 0:1024], wsrc[dc * 128:(dc + 1) * 128, :])
                w_tok = qsn()
                for tg in range(NTG):
                    wmark("q", "ccs")
                    ct1 = w.dmal("q", ccs_s[:], cc_d.ap()[:, tg * TG:(tg + 1) * TG])
                    ct2 = w.dmal("q", sss_s[:], ss_d.ap()[:, tg * TG:(tg + 1) * TG])
                    for qd in range(4):
                        b = (qd % 2) * 1024
                        pe = PA[:, b:b + 512]
                        po = PA[:, b + 512:b + 1024]
                        wmark("t", f"pab{qd % 2}")
                        for dc in range(NCH):
                            rhs = xts_s[dc][:, tg * TG:(tg + 1) * TG]
                            w.mm(pe, wch_s[dc][:, (2 * qd) * 128:(2 * qd + 1) * 128], rhs,
                                 start=(dc == 0), stop=(dc == NCH - 1),
                                 wait=w_tok + x_tok)
                            w.mm(po, wch_s[dc][:, (2 * qd + 1) * 128:(2 * qd + 2) * 128],
                                 rhs, start=(dc == 0), stop=(dc == NCH - 1),
                                 inc=("t", 1) if dc == NCH - 1 else None)
                        tC = C["t"]
                        wmark("v", "rp")
                        w.do("v", lambda E, pe=pe: E.tensor_tensor(
                            rp_s[0][:], pe, ccs_s[:], ALU.mult),
                             wait=[("t", tC), ct1, ct2])
                        w.do("v", lambda E, po=po: E.tensor_tensor(
                            rp_s[1][:], po, sss_s[:], ALU.mult))
                        w.do("v", lambda E, pe=pe: E.tensor_tensor(
                            rp_s[2][:], pe, sss_s[:], ALU.mult))
                        w.do("v", lambda E, po=po: E.tensor_tensor(
                            rp_s[3][:], po, ccs_s[:], ALU.mult), inc=("v", 1))
                        mark(f"pab{qd % 2}", "v")
                        if qd == 3:
                            mark("ccs", "v")
                        w.drain("v")
                        de = dstT[2 * qd][:, tg * TG:(tg + 1) * TG]
                        do_ = dstT[2 * qd + 1][:, tg * TG:(tg + 1) * TG]
                        w.do("v", lambda E, de=de: E.tensor_tensor(
                            de, rp_s[0][:], rp_s[1][:], ALU.subtract))
                        w.do("v", lambda E, do_=do_: E.tensor_tensor(
                            do_, rp_s[2][:], rp_s[3][:], ALU.add), inc=("v", 1))
                        mark("rp", "v")
                        w.drain("v")
                mark("wch", "t")

            wmark("q", "wch")
            for dc in range(NCH):
                w.dmal("q", wch_s[dc][:, 0:1040], vwp_d[dc * 128:(dc + 1) * 128, :])
            w_tok = qsn()
            pv = 0
            for tt in range(16):
                for j0, jn in ((0, 512), (512, 512), (1024, 16)):
                    ps = PB[:, (pv % 2) * 512:(pv % 2) * 512 + 512]
                    wmark("t", f"pbb{pv % 2}")
                    for dc in range(NCH):
                        w.mm(ps[:, 0:jn], xts_s[dc][:, tt * 128:(tt + 1) * 128],
                             wch_s[dc][:, j0:j0 + jn],
                             start=(dc == 0), stop=(dc == NCH - 1),
                             wait=w_tok + x_tok,
                             inc=("t", 1) if dc == NCH - 1 else None)
                    tC = C["t"]
                    dst = vp_s[:, tt * 1040 + j0: tt * 1040 + j0 + jn]
                    w.do("s", lambda E, dst=dst, ps=ps, jn=jn: E.activation(
                        dst, ps[:, 0:jn], AF.Copy), inc=("s", 1), wait=[("t", tC)])
                    mark(f"pbb{pv % 2}", "s")
                    pv += 1
            sv = C["s"]
            w.do("v", lambda E: E.memset(
                vp_s[:, :].rearrange("p (t c) -> p t c", c=1040)[:, :, 64::65], 1.0),
                 inc=("v", 1), wait=[("s", sv)])
            w.drain("v")
            mark("wch", "t")
            w.barrier()

        def attn_phase(s_idx):
            for i in range(4):
                w.dmal("q", msk_s[i][:], mask_d[i].ap())
            msk_tok = qsn()

            for qg in range(NTG):
                for q in range(4):
                    nkt = 4 * (qg + 1)
                    svtok = {}

                    def emit_scores(kt, q=q, qg=qg, nkt=nkt, svtok=svtok):
                        wmark("t", "bq")
                        wmark("t", "pafV")
                        for m in range(4):
                            if m == 0:
                                wmark("t", "scq0")
                            elif m == 2:
                                wmark("t", "scq1")
                            dst = PA[:, m * 512:(m + 1) * 512]
                            for half in range(2):
                                ch = 2 * q + half
                                last = (half == 1) and (kt < qg * 4)
                                w.mm(dst,
                                     kT_s[ch][32 * m:32 * m + 32,
                                              kt * 128:(kt + 1) * 128],
                                     qT_s[ch][32 * m:32 * m + 32,
                                              qg * TG:(qg + 1) * TG],
                                     start=(half == 0), stop=last,
                                     tp=(32 * m, 0), wait=msk_tok,
                                     inc=("t", 1) if (last and m == 3) else None)
                        if kt >= qg * 4:
                            di = kt - qg * 4
                            for m in range(4):
                                w.mm(PA[:, m * 512:(m + 1) * 512], id16_s[:],
                                     msk_s[di][:], start=False, stop=True,
                                     inc=("t", 1) if m == 3 else None)
                        svtok[kt] = C["t"]

                    def emit_expchain(kt, svtok=svtok):
                        tC = svtok[kt]
                        toks = {}
                        for h2 in range(2):
                            eb = (kt + h2) % 2
                            wmark("s", f"e32_{eb}")
                            w.do("s", lambda E, h2=h2, eb=eb: E.activation(
                                e32_s[eb][:], PA[:, h2 * 1024:(h2 + 1) * 1024],
                                AF.Exp, scale=SCALE),
                                 inc=("s", 1), wait=[("t", tC)])
                            wmark("s", f"ehi_{eb}")
                            w.do("s", lambda E, h2=h2, eb=eb: E.activation(
                                ehi_s[eb][:], PA[:, h2 * 1024:(h2 + 1) * 1024],
                                AF.Exp, scale=SCALE), inc=("s", 1))
                            mark(f"scq{h2}", "s")
                            sC = C["s"]
                            wmark("v", f"elo_{eb}")
                            w.do("v", lambda E, eb=eb: E.tensor_tensor(
                                elo_s[eb][:], e32_s[eb][:], ehi_s[eb][:], ALU.subtract),
                                 inc=("v", 1), wait=[("s", sC)])
                            mark(f"e32_{eb}", "v")
                            toks[h2] = (sC, C["v"])
                        svtok[kt] = toks

                    def emit_vmm(kt, q=q, nkt=nkt, svtok=svtok):
                        toks = svtok[kt]
                        if kt == 0:
                            wmark("t", "atb")
                        for hh in range(4):
                            h2 = hh // 2
                            eb = (kt + h2) % 2
                            sC, vC = toks[h2]
                            h = 4 * q + hh
                            PX = PB if hh < 2 else PC
                            pat = PX[0:65, (hh % 2) * 512:(hh % 2) * 512 + 512]
                            lhs = vp_s[:, kt * 1040 + 65 * h: kt * 1040 + 65 * h + 65]
                            w.mm(pat, lhs, ehi_s[eb][:, (hh % 2) * 512:(hh % 2) * 512 + 512],
                                 start=(kt == 0), stop=False,
                                 wait=[("v", vC), ("s", sC)])
                            w.mm(pat, lhs, elo_s[eb][:, (hh % 2) * 512:(hh % 2) * 512 + 512],
                                 start=False, stop=(kt == nkt - 1),
                                 inc=("t", 1) if hh == 3 else None)
                            if hh % 2 == 1:
                                mark(f"ehi_{eb}", "t")
                                mark(f"elo_{eb}", "t")

                    for kt in range(nkt):
                        emit_scores(kt)
                        if kt > 0:
                            emit_vmm(kt - 1)
                        emit_expchain(kt)
                    emit_vmm(nkt - 1)
                    tC = C["t"]

                    wmark("s", "rec")
                    for hh in range(4):
                        PX = PB if hh < 2 else PC
                        w.do("s", lambda E, hh=hh, PX=PX: E.activation(
                            rec_s[:, hh * 512:(hh + 1) * 512],
                            PX[64:65, (hh % 2) * 512:(hh % 2) * 512 + 512], AF.Copy),
                             inc=("s", 1) if hh == 3 else None,
                             wait=[("t", tC)])
                    sC = C["s"]
                    w.do("v", lambda E: E.reciprocal(rec_s[:], rec_s[:]),
                         inc=("v", 1), wait=[("s", sC)])
                    vC = C["v"]
                    mark("rec", "v")
                    wmark("t", "scq0")
                    wmark("t", "scq1")
                    for hh in range(4):
                        w.mm(PA[:, hh * 512:hh * 512 + 512], sel64_s[:],
                             rec_s[:, hh * 512:(hh + 1) * 512], start=True, stop=True,
                             wait=[("v", vC)], inc=("t", 1) if hh == 3 else None)
                    tC2 = C["t"]
                    wmark("s", "bsb")
                    for hh in range(4):
                        w.do("s", lambda E, hh=hh: E.activation(
                            bsb_s[0:64, hh * 512:(hh + 1) * 512],
                            PA[0:64, hh * 512:hh * 512 + 512], AF.Copy),
                             inc=("s", 1) if hh == 3 else None, wait=[("t", tC2)])
                    sC = C["s"]
                    mark("bq", "s")
                    for half in range(2):
                        p = 2 * q + half
                        hh0, hh1 = 2 * half, 2 * half + 1
                        PX = PB if half == 0 else PC
                        wmark("v", "t32")
                        w.do("v", lambda E, PX=PX, hh0=hh0: E.tensor_tensor(
                            t32_s[0:64, :], PX[0:64, 0:512],
                            bsb_s[0:64, hh0 * 512:(hh0 + 1) * 512], ALU.mult),
                             wait=[("s", sC)])
                        w.do("v", lambda E, PX=PX, hh1=hh1: E.tensor_tensor(
                            t32_s[64:128, :], PX[0:64, 512:1024],
                            bsb_s[0:64, hh1 * 512:(hh1 + 1) * 512], ALU.mult),
                             inc=("v", 1))
                        vC = C["v"]
                        if half == 1:
                            mark("bsb", "v")
                            mark("atb", "v")
                        wmark("s", f"athi{p}")
                        w.do("s", lambda E, p=p: E.activation(
                            athi_s[p][:], t32_s[:], AF.Copy), inc=("s", 1),
                             wait=[("v", vC)])
                        sC2 = C["s"]
                        wmark("v", f"atlo{p}")
                        w.do("v", lambda E, p=p: E.tensor_tensor(
                            atlo_s[p][:], t32_s[:], athi_s[p][:], ALU.subtract),
                             inc=("v", 1), wait=[("s", sC2)])
                        mark("t32", "v")

                if stage == "attn" and qg == 0:
                    vv, ss_ = C["v"], C["s"]
                    for c in range(NCH):
                        w.dmal("g", dbg["at_hi"][c * 128:(c + 1) * 128, :], athi_s[c][:],
                               wait=[("v", vv), ("s", ss_)])
                        w.dmal("g", dbg["at_lo"][c * 128:(c + 1) * 128, :], atlo_s[c][:])
                    w.waitpool("g", "g")
                    return True

                athi_v = C["v"]
                athi_sc = C["s"]
                pbank = {0: PA[:, 0:1024], 1: PA[:, 1024:2048],
                         2: PB[:, 0:1024], 3: PC[:, 0:1024]}
                xr_toks = {}
                for tt in range(4):
                    tok0 = s_idx * S + qg * TG + tt * 128
                    wmark("q", f"xrow{tt}")
                    xr_toks[tt] = w.dmal("q", xrow_s[tt][:], xrows_d[tok0:tok0 + 128, :])
                for halfw in range(2):
                    wmark("q", "owh")
                    for cc in range(4):
                        c = 4 * halfw + cc
                        w.dmal("q", owh_s[cc][:], ow_d[c * 128:(c + 1) * 128, :])
                    ow_tok = qsn()
                    for tt in range(4):
                        ps = pbank[tt]
                        if halfw == 0:
                            if tt < 2:
                                wmark("t", "pafV")
                            else:
                                wmark("t", "atb")
                        for cc in range(4):
                            c = 4 * halfw + cc
                            for jg in range(2):
                                w.mm(ps[:, jg * 512:(jg + 1) * 512],
                                     athi_s[c][:, tt * 128:(tt + 1) * 128],
                                     owh_s[cc][:, jg * 512:(jg + 1) * 512],
                                     start=(halfw == 0 and cc == 0), stop=False,
                                     wait=[("v", athi_v), ("s", athi_sc)] + ow_tok)
                                w.mm(ps[:, jg * 512:(jg + 1) * 512],
                                     atlo_s[c][:, tt * 128:(tt + 1) * 128],
                                     owh_s[cc][:, jg * 512:(jg + 1) * 512],
                                     start=False, stop=(halfw == 1 and cc == 3),
                                     inc=("t", 1) if (tt == 3 and cc == 3
                                                      and jg == 1) else None)
                    mark("owh", "t")
                opt_t = C["t"]
                for tt in range(4):
                    ti = tt % 2
                    tok0 = s_idx * S + qg * TG + tt * 128
                    ps = pbank[tt]
                    wmark("v", f"z{ti}")
                    w.do("v", lambda E, ps=ps, ti=ti, tt=tt:
                         E.scalar_tensor_tensor(
                             z_s[ti][:], ps[:, :], 1.0, xrow_s[tt][:],
                             ALU.mult, ALU.add, accum_out=vec_s[:, 8:9]),
                         inc=("v", 1), wait=[("t", opt_t), xr_toks[tt]])
                    if tt == 1:
                        mark("pafV", "v")
                    if tt == 3:
                        mark("atb", "v")
                    mark(f"xrow{tt}", "v")
                    vC = C["v"]
                    wmark("s", "sqd")
                    w.do("s", lambda E, ti=ti: E.activation(
                        sqd_s[:], z_s[ti][:], AF.Square, accum_out=vec_s[:, 10:11]),
                         wait=[("v", vC)])
                    w.do("s", lambda E: E.mul(vec_s[:, 1:2], vec_s[:, 8:9], 1.0 / D))
                    w.drain("s")
                    w.do("s", lambda E: E.square(vec_s[:, 2:3], vec_s[:, 1:2]),
                         inc=("s", 1))
                    sC = C["s"]
                    mark("sqd", "s")
                    w.do("v", lambda E: E.scalar_tensor_tensor(
                        vec_s[:, 3:4], vec_s[:, 10:11], 1.0 / D, vec_s[:, 2:3],
                        ALU.mult, ALU.subtract), inc=("v", 1), wait=[("s", sC)])
                    vC = C["v"]
                    w.do("s", lambda E: E.activation(
                        vec_s[:, 4:5], vec_s[:, 3:4], AF.Sqrt, bias=eps5_s[:]),
                         inc=("s", 1), wait=[("v", vC)])
                    sC = C["s"]
                    w.drain("v")
                    w.do("v", lambda E: E.reciprocal(vec_s[:, 5:6], vec_s[:, 4:5]),
                         wait=[("s", sC)])
                    w.drain("v")
                    wmark("v", f"x1t{ti}")
                    w.do("v", lambda E, ti=ti: E.tensor_scalar(
                        x1t_s[ti][:], z_s[ti][:], vec_s[:, 1:2], vec_s[:, 5:6],
                        ALU.subtract, ALU.mult), inc=("v", 1))
                    vC = C["v"]
                    mark(f"z{ti}", "v")
                    x1tok = [w.dmal("g", x1f_d[tok0:tok0 + 128, :], x1t_s[ti][:],
                                    wait=[("v", vC)])]
                    if stage == "ln1":
                        x1tok.append(w.dmal("g", dbg["x1"][tok0:tok0 + 128, :],
                                            x1t_s[ti][:]))
                    w.marks[f"x1t{ti}"] = x1tok
            w.barrier()
            return False

        def slots_phase(s_idx):
            tok0 = s_idx * S
            w.do("v", lambda E: (E.memset(carr_s[:], 0.0), E.memset(onesr_s[:], 1.0),
                                 E.drain())[-1], inc=("v", 1))
            tr_tok = w.dmal("q", trrow_s[:], trrow_d.ap())
            for tg in range(NTG):
                for tl4 in range(4):
                    tile = tg * 4 + tl4
                    ti = tile % 2
                    wmark("q", f"x1t{ti}")
                    xt_tok = w.dmal("q", x1t_s[ti][:],
                                    x1f_d[tok0 + tile * 128: tok0 + (tile + 1) * 128, :])
                    tb = 2 * (tile % 2)
                    for c in range(NCH):
                        sl = PA[:, (c % 4) * 512:(c % 4) * 512 + 128]
                        wmark("t", f"pas{c % 4}")
                        w.do("t", lambda E, sl=sl, ti=ti, c=c: E.transpose(
                            sl, x1t_s[ti][:, c * 128:(c + 1) * 128], id32_s[:]),
                             inc=("t", 1), wait=[xt_tok])
                        tC = C["t"]
                        wmark("s", f"x1Tt{tb}")
                        w.do("s", lambda E, c=c, sl=sl, tb=tb: E.activation(
                            x1Tt_s[tb][:, c * 128:(c + 1) * 128], sl, AF.Copy),
                             inc=("s", 1), wait=[("t", tC)])
                        sC = C["s"]
                        wmark("v", f"x1Tt{tb + 1}")
                        w.do("v", lambda E, c=c, sl=sl, tb=tb: E.tensor_tensor(
                            x1Tt_s[tb + 1][:, c * 128:(c + 1) * 128], sl,
                            x1Tt_s[tb][:, c * 128:(c + 1) * 128], ALU.subtract),
                             inc=("v", 1), wait=[("s", sC)])
                        mark(f"pas{c % 4}", "v")
                    tile_s, tile_v = C["s"], C["v"]
                    if tl4 == 0:
                        wmark("t", "lgbank")
                    first = True
                    for dc in range(NCH):
                        lh = wrh_s[:, dc * 4:(dc + 1) * 4]
                        ll = wrl_s[:, dc * 4:(dc + 1) * 4]
                        xh = x1Tt_s[tb][:, dc * 128:(dc + 1) * 128]
                        xl = x1Tt_s[tb + 1][:, dc * 128:(dc + 1) * 128]
                        dstp = PB[0:4, tl4 * 128:(tl4 + 1) * 128]
                        w.mm(dstp, lh, xh, start=first, stop=False,
                             wait=[("s", tile_s), ("v", tile_v)])
                        first = False
                        w.mm(dstp, lh, xl, start=False, stop=False)
                        w.mm(dstp, ll, xh, start=False, stop=(dc == NCH - 1),
                             inc=("t", 1) if dc == NCH - 1 else None)
                    mark(f"x1Tt{tb}", "t")
                    mark(f"x1Tt{tb + 1}", "t")
                    mark(f"x1t{ti}", "t")
                tC = C["t"]
                wmark("v", "lgs")
                w.do("v", lambda E: E.tensor_copy(lgs_s[:], PB[0:4, 0:512]),
                     inc=("v", 1), wait=[("t", tC)])
                vC = C["v"]
                mark("lgbank", "v")
                wmark("q", "lrow")
                lr_tok = w.dmal("q", lrow_s[0:1, :].rearrange("a (p f) -> a p f", p=4),
                                lgs_s[:], wait=[("v", vC)])
                w.marks["lgs"] = [lr_tok]

                def rowops(E, tg=tg, s_idx=s_idx):
                    L, R = lrow_s, rtmp_s
                    CSUM = R[:, 4 * TG:5 * TG]
                    SLV = R[:, 5 * TG:6 * TG]
                    OV = R[:, 6 * TG:7 * TG]
                    E.tensor_tensor(R[:, 0:TG], L[:, 0:TG], L[:, TG:2 * TG], ALU.max)
                    E.tensor_tensor(R[:, TG:2 * TG], L[:, 2 * TG:3 * TG],
                                    L[:, 3 * TG:4 * TG], ALU.max)
                    E.drain()
                    E.tensor_tensor(R[:, 2 * TG:3 * TG], R[:, 0:TG], R[:, TG:2 * TG],
                                    ALU.max)
                    E.drain()
                    E.memset(R[:, 0:TG], 1.0)
                    E.memset(slotr_s[:], 0.0)
                    E.drain()
                    for e in range(NE):
                        ge = L[:, e * TG:(e + 1) * TG]
                        E.tensor_tensor(ge, ge, R[:, 2 * TG:3 * TG], ALU.is_ge)
                    E.drain()
                    for e in range(NE):
                        ge = L[:, e * TG:(e + 1) * TG]
                        me = R[:, 3 * TG:4 * TG]
                        E.tensor_tensor(me, ge, R[:, 0:TG], ALU.mult)
                        E.drain()
                        E.tensor_tensor_scan(CSUM, onesr_s[:], me,
                                             carr_s[:, e:e + 1], ALU.mult, ALU.add)
                        E.drain()
                        E.tensor_copy(carr_s[:, e:e + 1], CSUM[:, TG - 1:TG])
                        E.tensor_single_scalar(
                            SLV, CSUM, float(s_idx * SLOTS + e * CCAP - 1), ALU.add)
                        E.tensor_single_scalar(ovi_s[:], CSUM, float(CCAP), ALU.is_gt)
                        E.drain()
                        E.copy_predicated(SLV, ovi_s[:], trrow_s[:])
                        E.drain()
                        E.tensor_tensor(SLV, SLV, me, ALU.mult)
                        E.drain()
                        E.tensor_tensor(slotr_s[:], slotr_s[:], SLV, ALU.add)
                        E.drain()
                        if e < NE - 1:
                            E.tensor_tensor(R[:, 0:TG], R[:, 0:TG], me, ALU.subtract)
                            E.drain()
                    return E.drain()
                wmark("v", "slotr")
                w.do("v", rowops, inc=("v", 1), wait=[lr_tok, tr_tok])
                mark("lrow", "v")
                sr_v = C["v"]
                cb = s_idx * 16 + tg * 4
                row = s_idx * NTG + tg
                st_tok = w.dmal("g", slot_d[row:row + 1, :], slotr_s[:],
                                wait=[("v", sr_v)])
                w.marks["slotr"] = [st_tok]
                w.do("q", lambda E, cb=cb, row=row: E.dma_start(
                    out=idxf_s[:, cb:cb + 4],
                    in_=slot_d[row:row + 1, :].rearrange("a (c p) -> (a p) c", p=128),
                    allow_slow_non_contiguous=True),
                     inc=(QL[0], 16), wait=[(QL[0], C[QL[0]]), st_tok])
            idx_tok = (QL[0], C[QL[0]])
            w.do("v", lambda E, s_idx=s_idx: E.tensor_copy(
                idxi_s[:, s_idx * 16:(s_idx + 1) * 16],
                idxf_s[:, s_idx * 16:(s_idx + 1) * 16]),
                 inc=("v", 1), wait=[idx_tok])
            w.drain("v")
            idx_v = C["v"]

            if stage == "slots" and s_idx == 0:
                w.dmal("g", dbg["idx"][:, 0:16], idxf_s[:, 0:16], wait=[("v", idx_v)])
                w.waitpool("g", "g")
                return True

            from concourse.bass_core_rust import IndirectOffsetOnAxis
            for tile in range(16):
                ti = tile % 2
                wmark("q", f"x1t{ti}")
                xt_tok = w.dmal("q", x1t_s[ti][:],
                                x1f_d[tok0 + tile * 128: tok0 + (tile + 1) * 128, :])
                col = s_idx * 16 + tile
                w.do("g", lambda E, ti=ti, col=col: E.indirect_dma_start(
                    xg_d[:, :],
                    IndirectOffsetOnAxis(idxi_s[:, col:col + 1], 0),
                    x1t_s[ti][:], None),
                     inc=(GL[tile % 4], 16),
                     wait=[(GL[tile % 4], C[GL[tile % 4]]), xt_tok, ("v", idx_v)]
                     + zf_tok)
                w.marks[f"x1t{ti}"] = [(GL[tile % 4], C[GL[tile % 4]])]
            sc_tok = gsn()

            for st in range(SLOTS // 128):
                ti = st % 2
                wmark("q", f"x1t{ti}")
                xt_tok = w.dmal("q", x1t_s[ti][:],
                                xg_d[s_idx * SLOTS + st * 128:
                                     s_idx * SLOTS + (st + 1) * 128, :],
                                wait=sc_tok)
                col0 = st * 128
                for c in range(NCH):
                    sl = PA[:, (c % 4) * 512:(c % 4) * 512 + 128]
                    wmark("t", f"pas{c % 4}")
                    w.do("t", lambda E, sl=sl, ti=ti, c=c: E.transpose(
                        sl, x1t_s[ti][:, c * 128:(c + 1) * 128], id32_s[:]),
                         inc=("t", 1), wait=[xt_tok])
                    tC = C["t"]
                    wmark("s", f"xgTh{c}")
                    w.do("s", lambda E, c=c, sl=sl, col0=col0: E.activation(
                        xgTh_s[c][:, col0:col0 + 128], sl, AF.Copy),
                         inc=("s", 1), wait=[("t", tC)])
                    sC = C["s"]
                    wmark("v", f"xgTl{c}")
                    w.do("v", lambda E, c=c, sl=sl, col0=col0: E.tensor_tensor(
                        xgTl_s[c][:, col0:col0 + 128], sl,
                        xgTh_s[c][:, col0:col0 + 128], ALU.subtract),
                         inc=("v", 1), wait=[("s", sC)])
                    mark(f"pas{c % 4}", "v")
                mark(f"x1t{ti}", "t")
            w.barrier()
            return False

        def ffn_phase(s_idx):
            slot0 = s_idx * SLOTS
            sl_tok = {}

            def emit_w1split(ic):
                ip = ic % 2
                for e in range(NE):
                    sb32 = slab32_s[e % 2]
                    wmark("q", f"slab32_{e % 2}")
                    dtok = w.dmal("q", sb32[:], w1_d[e, ic])
                    wmark("s", f"slabh{ip}_{e}")
                    w.do("s", lambda E, e=e, ip=ip, sb32=sb32: E.activation(
                        slabh_s[ip][e][:], sb32[:], AF.Copy), inc=("s", 1),
                         wait=[dtok])
                    sC = C["s"]
                    wmark("v", f"slabl{ip}_{e}")
                    w.do("v", lambda E, e=e, ip=ip, sb32=sb32: E.tensor_tensor(
                        slabl_s[ip][e][:], sb32[:], slabh_s[ip][e][:], ALU.subtract),
                         inc=("v", 1), wait=[("s", sC)])
                    mark(f"slab32_{e % 2}", "v")
                sl_tok[ic] = (C["v"], C["s"])

            emit_w1split(0)
            for ic in range(NIC):
                ip = ic % 2
                if ic + 1 < NIC:
                    emit_w1split(ic + 1)
                slab_v, slab_s_cnt = sl_tok[ic]
                for e in range(NE):
                    for gr in range(CCAP // GR):
                        c0 = e * CCAP + gr * GR
                        ps = PA[:, (2 * e + gr) % 4 * 512:((2 * e + gr) % 4) * 512 + GR]
                        wmark("t", f"fps{(2 * e + gr) % 4}")
                        for dc in range(NCH):
                            lh = slabh_s[ip][e][:, dc * 128:(dc + 1) * 128]
                            ll = slabl_s[ip][e][:, dc * 128:(dc + 1) * 128]
                            xh = xgTh_s[dc][:, c0:c0 + GR]
                            xl = xgTl_s[dc][:, c0:c0 + GR]
                            w.mm(ps, lh, xh, start=(dc == 0), stop=False,
                                 wait=[("v", slab_v), ("s", slab_s_cnt)])
                            w.mm(ps, lh, xl, start=False, stop=False)
                            w.mm(ps, ll, xh, start=False, stop=(dc == NCH - 1),
                                 inc=("t", 1) if dc == NCH - 1 else None)
                        tC = C["t"]
                        gp = (2 * e + gr) % 2
                        wmark("s", f"gfull{gp}")
                        w.do("s", lambda E, ps=ps, e=e, ic=ic, gp=gp: E.activation(
                            gf_s[gp][:], ps, AF.Gelu,
                            bias=b1_s[:, e * 32 + ic:e * 32 + ic + 1]),
                             inc=("s", 1), wait=[("t", tC)])
                        mark(f"fps{(2 * e + gr) % 4}", "s")
                        w.drain("s")
                        wmark("s", f"ghisb{gp}")
                        w.do("s", lambda E, gp=gp: E.activation(
                            ghi_sb[gp][:], gf_s[gp][:], AF.Copy), inc=("s", 1))
                        sC = C["s"]
                        wmark("v", f"glosb{gp}")
                        w.do("v", lambda E, gp=gp: E.tensor_tensor(
                            glo_sb[gp][:], gf_s[gp][:], ghi_sb[gp][:], ALU.subtract),
                             inc=("v", 1), wait=[("s", sC)])
                        vC = C["v"]
                        mark(f"gfull{gp}", "v")
                        gh_tok = w.dmal("g", ghi_d[ic, :, slot0 + c0: slot0 + c0 + GR],
                                        ghi_sb[gp][:], wait=[("s", sC), ("v", vC)])
                        gl_tok = w.dmal("g", glo_d[ic, :, slot0 + c0: slot0 + c0 + GR],
                                        glo_sb[gp][:])
                        w.marks[f"ghisb{gp}"] = [gh_tok]
                        w.marks[f"glosb{gp}"] = [gl_tok]
                for e in range(NE):
                    w.marks[f"slabh{ip}_{e}"] = [("t", C["t"])]
                    w.marks[f"slabl{ip}_{e}"] = [("t", C["t"])]
            w.barrier()

        def wo_phase():
            for dst, srcp in ((gam_s, gam_d), (bet_s, bet_d), (bo_s, bo_d),
                              (ones512_s, ones512_d.ap())):
                w.dmal("q", dst[:], srcp)
            wo_tok = qsn()
            sp_tok = {}
            from concourse.bass_core_rust import IndirectOffsetOnAxis

            NWB = SLOTS // WB
            for s_idx in range(NSEQ):
              slot0 = s_idx * SLOTS
              for wb in range(NWB):
                g_ic_tok = {}
                for ic in range(NIC):
                    wmark("q", f"gsl{ic}")
                    t1 = w.dmal("q", gslh_s[ic][:],
                                ghi_d[ic, :, slot0 + wb * WB:slot0 + (wb + 1) * WB])
                    t2 = w.dmal("q", gsll_s[ic][:],
                                glo_d[ic, :, slot0 + wb * WB:slot0 + (wb + 1) * WB])
                    g_ic_tok[ic] = [t1, t2]

                def emit_wosplit(jc):
                    jp = jc % 2
                    wmark("s", f"wjh{jp}")
                    wmark("v", f"wjl{jp}")
                    for q4 in range(4):
                        sb32 = slab32_s[q4 % 2]
                        wmark("q", f"slab32_{q4 % 2}")
                        wsl_tok = w.dmal("q", sb32[:],
                                         wo_d[jc, :, q4 * 8:(q4 + 1) * 8, :])
                        w.do("s", lambda E, sb32=sb32, jp=jp, q4=q4: E.activation(
                            wjh_s[jp][:, q4 * 1024:(q4 + 1) * 1024], sb32[:], AF.Copy),
                             inc=("s", 1), wait=[wsl_tok])
                        sC = C["s"]
                        w.do("v", lambda E, sb32=sb32, jp=jp, q4=q4: E.tensor_tensor(
                            wjl_s[jp][:, q4 * 1024:(q4 + 1) * 1024], sb32[:],
                            wjh_s[jp][:, q4 * 1024:(q4 + 1) * 1024], ALU.subtract),
                             inc=("v", 1), wait=[("s", sC)])
                        mark(f"slab32_{q4 % 2}", "v")
                    sp_tok[jc] = (C["v"], C["s"])

                if wb == 0:
                    emit_wosplit(0)
                for jc in range(NCH):
                    jp = jc % 2
                    if not (jc == NCH - 1 and wb == NWB - 1):
                        emit_wosplit((jc + 1) % NCH)
                    wo_v, wo_s_cnt = sp_tok[jc]
                    ps = PB[:, jp * 512:jp * 512 + WB]
                    wmark("t", f"pbb{jp}")
                    for ic in range(NIC):
                        w.mm(ps, wjh_s[jp][:, ic * 128:(ic + 1) * 128], gslh_s[ic][:],
                             start=(ic == 0), stop=False,
                             wait=[("v", wo_v), ("s", wo_s_cnt)] + g_ic_tok[ic]
                                  + wo_tok)
                        w.mm(ps, wjh_s[jp][:, ic * 128:(ic + 1) * 128], gsll_s[ic][:],
                             start=False, stop=False)
                        w.mm(ps, wjl_s[jp][:, ic * 128:(ic + 1) * 128], gslh_s[ic][:],
                             start=False, stop=False)
                    w.mm(ps, bo_s[:, jc * 128:(jc + 1) * 128], ones512_s[:, 0:WB],
                         start=False, stop=True, inc=("t", 1))
                    tC = C["t"]
                    mark(f"wjh{jp}", "t")
                    mark(f"wjl{jp}", "t")
                    wmark("s", "ysb")
                    w.do("s", lambda E, ps=ps: E.activation(ysb_s[:], ps, AF.Copy),
                         inc=("s", 1), wait=[("t", tC)])
                    sC = C["s"]
                    mark(f"pbb{jp}", "s")
                    for quarter in range(4):
                        sl = PC[:, (quarter % 2) * 512:(quarter % 2) * 512 + 128]
                        wmark("t", f"pc{quarter % 2}")
                        w.do("t", lambda E, sl=sl, quarter=quarter: E.transpose(
                            sl, ysb_s[:, quarter * 128:quarter * 128 + 128], id32_s[:]),
                             inc=("t", 1), wait=[("s", sC)])
                        tC = C["t"]
                        wmark("v", f"yrow{quarter}")
                        w.do("v", lambda E, quarter=quarter, jc=jc, sl=sl: E.tensor_copy(
                            yrow_s[quarter][:, jc * 128:(jc + 1) * 128], sl),
                             inc=("v", 1), wait=[("t", tC)])
                        mark(f"pc{quarter % 2}", "v")
                    mark("ysb", "t")
                for ic in range(NIC):
                    w.marks[f"gsl{ic}"] = [("t", C["t"])]
                y_v = C["v"]
                for quarter in range(4):
                    r0 = slot0 + wb * WB + quarter * 128
                    yg_tok = w.dmal("g", yg_d[r0:r0 + 128, :], yrow_s[quarter][:],
                                    wait=[("v", y_v)])
                    w.marks[f"yrow{quarter}"] = [yg_tok]
              ygs_tok = gsn()
              for tile in range(16):
                    tok0 = s_idx * S + tile * 128
                    ti = tile % 2
                    col = s_idx * 16 + tile
                    wmark("v", f"yrow{ti}")
                    wmark("s", f"yrow{ti}")
                    gl = GL[2 + tile % 2]
                    w.do("g", lambda E, ti=ti, col=col: E.indirect_dma_start(
                        yrow_s[ti][:], None,
                        yg_d[:, :],
                        IndirectOffsetOnAxis(idxi_s[:, col:col + 1], 0)),
                         inc=(gl, 16),
                         wait=[(gl, C[gl])] + ygs_tok)
                    yr_tok = (gl, C[gl])
                    wmark("q", f"x1r{ti}")
                    x1r_tok = w.dmal("q", x1r_s[ti][:], x1f_d[tok0:tok0 + 128, :])
                    wmark("v", f"z{ti}")
                    wmark("v", f"outt{ti}")
                    w.do("v", lambda E, ti=ti: E.scalar_tensor_tensor(
                        z_s[ti][:], yrow_s[ti][:], 1.0, x1r_s[ti][:],
                        ALU.mult, ALU.add, accum_out=vec_s[:, 12:13]),
                         inc=("v", 1), wait=[yr_tok, x1r_tok])
                    vC = C["v"]
                    mark(f"yrow{ti}", "v")
                    w.do("v", lambda E, ti=ti: E.tensor_tensor(
                        outt_s[ti][:], bet_s[:], x1r_s[ti][:], ALU.add), inc=("v", 1))
                    wmark("s", "sqd")
                    w.do("s", lambda E, ti=ti: E.activation(
                        sqd_s[:], z_s[ti][:], AF.Square, accum_out=vec_s[:, 13:14]),
                         wait=[("v", vC)])
                    w.do("s", lambda E: E.mul(vec_s[:, 1:2], vec_s[:, 12:13], 1.0 / D))
                    w.drain("s")
                    w.do("s", lambda E: E.square(vec_s[:, 2:3], vec_s[:, 1:2]),
                         inc=("s", 1))
                    sC = C["s"]
                    mark("sqd", "s")
                    w.do("v", lambda E: E.scalar_tensor_tensor(
                        vec_s[:, 3:4], vec_s[:, 13:14], 1.0 / D, vec_s[:, 2:3],
                        ALU.mult, ALU.subtract), inc=("v", 1), wait=[("s", sC)])
                    vC = C["v"]
                    w.do("s", lambda E: E.activation(
                        vec_s[:, 4:5], vec_s[:, 3:4], AF.Sqrt, bias=eps12_s[:]),
                         inc=("s", 1), wait=[("v", vC)])
                    sC = C["s"]
                    w.drain("v")
                    w.do("v", lambda E: E.reciprocal(vec_s[:, 5:6], vec_s[:, 4:5]),
                         wait=[("s", sC)])
                    w.drain("v")
                    wmark("v", "sqd")
                    w.do("v", lambda E, ti=ti: E.tensor_scalar(
                        sqd_s[:], z_s[ti][:], vec_s[:, 1:2], vec_s[:, 5:6],
                        ALU.subtract, ALU.mult))
                    w.drain("v")
                    w.do("v", lambda E: E.tensor_tensor(
                        sqd_s[:], sqd_s[:], gam_s[:], ALU.mult))
                    w.drain("v")
                    w.do("v", lambda E, ti=ti: E.scalar_tensor_tensor(
                        z_s[ti][:], sqd_s[:], 1.0, outt_s[ti][:], ALU.mult, ALU.add,
                        accum_out=vec_s[:, 14:15]), inc=("v", 1))
                    vC = C["v"]
                    mark(f"x1r{ti}", "v")
                    mark("sqd", "v")
                    w.do("s", lambda E, ti=ti: E.activation(
                        sqd_s[:], z_s[ti][:], AF.Square, accum_out=vec_s[:, 15:16]),
                         wait=[("v", vC)])
                    w.do("s", lambda E: E.mul(vec_s[:, 1:2], vec_s[:, 14:15], 1.0 / D))
                    w.drain("s")
                    w.do("s", lambda E: E.square(vec_s[:, 2:3], vec_s[:, 1:2]),
                         inc=("s", 1))
                    sC = C["s"]
                    mark("sqd", "s")
                    w.do("v", lambda E: E.scalar_tensor_tensor(
                        vec_s[:, 3:4], vec_s[:, 15:16], 1.0 / D, vec_s[:, 2:3],
                        ALU.mult, ALU.subtract), inc=("v", 1), wait=[("s", sC)])
                    vC = C["v"]
                    w.do("s", lambda E: E.activation(
                        vec_s[:, 4:5], vec_s[:, 3:4], AF.Sqrt, bias=eps5_s[:]),
                         inc=("s", 1), wait=[("v", vC)])
                    sC = C["s"]
                    w.drain("v")
                    w.do("v", lambda E: E.reciprocal(vec_s[:, 5:6], vec_s[:, 4:5]),
                         wait=[("s", sC)])
                    w.drain("v")
                    w.do("v", lambda E, ti=ti: E.tensor_scalar(
                        outt_s[ti][:], z_s[ti][:], vec_s[:, 1:2], vec_s[:, 5:6],
                        ALU.subtract, ALU.mult), inc=("v", 1))
                    vC = C["v"]
                    mark(f"z{ti}", "v")
                    o_tok = w.dmal("g", out_d[tok0:tok0 + 128, :], outt_s[ti][:],
                                   wait=[("v", vC)])
                    w.marks[f"outt{ti}"] = [o_tok]
            w.waitpool("g", "g")

        for s_idx in range(NSEQ):
            qkv_phase(s_idx)
            if stage == "qkv" and s_idx == 0:
                vv, ss_ = C["v"], C["s"]
                for c in range(NCH):
                    w.dmal("g", dbg["qT"][c * 128:(c + 1) * 128, :], qT_s[c][:],
                           wait=[("v", vv), ("s", ss_)])
                    w.dmal("g", dbg["kT"][c * 128:(c + 1) * 128, :], kT_s[c][:])
                w.dmal("g", dbg["vp"][:], vp_s[:])
                w.waitpool("g", "g")
                return
            if attn_phase(s_idx):
                return
            if slots_phase(s_idx):
                return
            ffn_phase(s_idx)
        if stage == "ffn":
            w.waitpool("g", "g")
            return
        wo_phase()

    with nc.Block() as block:
        @block.sync
        def _(eng):
            walk("q", eng)

        @block.gpsimd
        def _(eng):
            walk("g", eng)

        @block.tensor
        def _(eng):
            walk("t", eng)

        @block.scalar
        def _(eng):
            walk("s", eng)

        @block.vector
        def _(eng):
            walk("v", eng)

    return nc


def _stage_inputs(inputs):
    x = np.asarray(inputs["L0_attn_x"])
    qw = np.asarray(inputs["L0_attn_qw"])
    kw = np.asarray(inputs["L0_attn_kw"])
    vw = np.asarray(inputs["L0_attn_vw"])
    ow = np.asarray(inputs["L0_attn_ow"])
    wr = np.asarray(inputs["L0_ffn_W_router"])
    w1 = np.asarray(inputs["L0_ffn_W1"])
    b1 = np.asarray(inputs["L0_ffn_b1"])
    wo = np.asarray(inputs["L0_ffn_Wo"])
    bo = np.asarray(inputs["L0_ffn_bo"])
    gam = np.asarray(inputs["L0_ffn_ln_gamma"])
    bet = np.asarray(inputs["L0_ffn_ln_beta"])

    perm = []
    for qd in range(4):
        for par in range(2):
            for m in range(4):
                h = 4 * qd + m
                for i in range(32):
                    perm.append(64 * h + 2 * i + par)
    perm = np.array(perm)
    qwp = np.ascontiguousarray(qw[:, perm])
    kwp = np.ascontiguousarray(kw[:, perm])
    vwp = np.zeros((D, 1040), np.float16)
    for h in range(NH):
        vwp[:, 65 * h:65 * h + 64] = vw[:, 64 * h:64 * h + 64]
    wrc = np.ascontiguousarray(
        wr.reshape(8, 128, 4).transpose(1, 0, 2).reshape(128, 32)).astype(np.float32)
    w1r = np.ascontiguousarray(
        w1.reshape(NE, 8, 128, NIC, 128).transpose(0, 3, 2, 1, 4)
        .reshape(NE, NIC, 128, 1024)).astype(np.float32)
    b1c = np.ascontiguousarray(
        b1.reshape(NE, NIC, 128).transpose(2, 0, 1).reshape(128, NE * NIC)
    ).astype(np.float32)
    gamt = np.ascontiguousarray(np.broadcast_to(gam[None, :], (128, D))).astype(np.float32)
    bett = np.ascontiguousarray(np.broadcast_to(bet[None, :], (128, D))).astype(np.float32)
    bor = np.ascontiguousarray(bo[None, :]).astype(np.float32)
    wor2 = np.ascontiguousarray(
        wo.reshape(NIC, 128, NCH, 128).transpose(2, 1, 0, 3)).astype(np.float32)

    in_maps = []
    for c in range(8):
        xc = x[2 * c:2 * c + 2]
        xrows = np.ascontiguousarray(xc.reshape(T, D))
        xT = np.ascontiguousarray(xc.reshape(T, D).T)
        in_maps.append({
            "xT": xT, "xrows": xrows,
            "qwp": qwp, "kwp": kwp, "vwp": vwp, "oww": np.ascontiguousarray(ow),
            "wrc": wrc, "w1r": w1r, "b1c": b1c,
            "wor2": wor2, "bor": bor, "gamt": gamt, "bett": bett,
        })
    return in_maps


def kernel(**inputs):
    in_maps = _stage_inputs(inputs)
    nc = build(stage="full", debug=False)
    res = run_bass_kernel_spmd(nc, in_maps, core_ids=list(range(8)))
    out = np.concatenate([np.asarray(r["out"]) for r in res.results], axis=0)
    return out.reshape(16, 2048, 1024).astype(np.float32)


# version 14
# speedup vs baseline: 1.0384x; 1.0384x over previous
import numpy as np

import concourse.bass as bass
import concourse.mybir as mybir
from concourse.bass_utils import run_bass_kernel_spmd

F32 = mybir.dt.float32
F16 = mybir.dt.float16
AF = mybir.ActivationFunctionType
ALU = mybir.AluOpType

D = 1024
S = 2048
NSEQ = 2
T = NSEQ * S
NH = 16
NCH = 8
TG = 512
NTG = S // TG
NE = 4
DFF = 4096
NIC = DFF // 128
SCALE = 1.0 / 32.0
MASKVAL = -30000.0
LN_EPS_OUT = 1e-5
LN_EPS_IN = 1e-12
CCAP = 640
GR = 320
SLOTS = NE * CCAP
TRASH = NSEQ * SLOTS
NXG = TRASH + 128

CAP = 205312


def _round32(x):
    return (x + 31) // 32 * 32


class Arena:

    _ctr = [0]

    def __init__(self, nc, cap):
        self.nc = nc
        self.cap = cap
        self.free = [(0, cap)]
        self.live = {}
        Arena._ctr[0] += 1
        self.backing = nc.alloc_sbuf_tensor(f"arena{Arena._ctr[0]}", [128, cap],
                                            mybir.dt.uint8)
        self.base_addr = nc.lookup_mloc(self.backing).addr

    def alloc(self, name, shape, dtype):
        per_part = int(np.prod(shape[1:])) * mybir.dt.size(dtype)
        need = _round32(per_part)
        for idx, (off, sz) in enumerate(self.free):
            if sz >= need:
                self.free[idx] = (off + need, sz - need)
                if self.free[idx][1] == 0:
                    self.free.pop(idx)
                h = self.nc.alloc_sbuf_tensor_at(name, list(shape), dtype,
                                                 offset=self.base_addr + off)
                self.live[name] = (off, need)
                return h
        raise MemoryError(f"sbuf arena: no room for {name} ({need}B); free={self.free}")

    def freebuf(self, *names):
        for name in names:
            off, sz = self.live.pop(name)
            self.free.append((off, sz))
        self.free.sort()
        out = []
        for off, sz in self.free:
            if out and out[-1][0] + out[-1][1] == off:
                out[-1] = (out[-1][0], out[-1][1] + sz)
            else:
                out.append((off, sz))
        self.free = [tuple(x) for x in out]


class Walk:

    def __init__(self, me, handle, sems):
        self.me = me
        self.E = handle
        self.sems = sems
        self.cnt = {k: 0 for k in sems}
        self.last_wait = {}
        self.marks = {}

    def wait(self, eng, *pairs):
        if self.me == eng:
            for s, v in pairs:
                if v <= 0 or self.last_wait.get(s, -1) >= v:
                    continue
                self.E.wait_ge(self.sems[s], v)
                self.last_wait[s] = v

    def do(self, eng, fn, inc=None, wait=None):
        if self.me == eng:
            if wait:
                self.wait(eng, *wait)
            inst = fn(self.E)
            if inc:
                inst.then_inc(self.sems[inc[0]], inc[1])
        if inc:
            self.cnt[inc[0]] += inc[1]

    def drain(self, eng):
        if self.me == eng:
            self.E.drain()

    def dma(self, eng, out, in_, sem, wait=None):
        self.do(eng, lambda E: E.dma_start(out=out, in_=in_), inc=(sem, 16), wait=wait)

    def dmal(self, eng, out, in_, wait=None):
        pool = self.pools[eng]
        lane = pool[self.rr[eng] % len(pool)]
        self.rr[eng] += 1
        ws = [(lane, self.cnt[lane])]
        if wait:
            ws += list(wait)
        self.do(eng, lambda E: E.dma_start(out=out, in_=in_), inc=(lane, 16), wait=ws)
        return (lane, self.cnt[lane])

    def waitpool(self, eng, pool_of):
        for lane in self.pools[pool_of]:
            self.wait(eng, (lane, self.cnt[lane]))

    def mm(self, out, lhsT, rhs, start, stop, tp=None, wait=None, inc=None):
        self.do("t",
                lambda E: E.matmul(out, lhsT, rhs, start=start, stop=stop,
                                   tile_position=tp),
                inc=inc, wait=wait)

    def mark(self, name, sem):
        self.marks[name] = [(sem, self.cnt[sem])]

    def wmark(self, eng, name):
        for s, v in self.marks.get(name, ()):
            self.wait(eng, (s, v))

    def barrier(self):
        snap = [(s, v) for s, v in self.cnt.items() if v > 0]
        for eng in ("q", "g", "t", "s", "v"):
            self.wait(eng, *snap)


def build(stage="full", debug=False):
    nc = bass.Bass("TRN2", target_bir_lowering=False)

    xT_d = nc.dram_tensor("xT", [D, T], F16, kind="ExternalInput").ap()
    xrows_d = nc.dram_tensor("xrows", [T, D], F16, kind="ExternalInput").ap()
    qwp_d = nc.dram_tensor("qwp", [D, D], F16, kind="ExternalInput").ap()
    kwp_d = nc.dram_tensor("kwp", [D, D], F16, kind="ExternalInput").ap()
    vwp_d = nc.dram_tensor("vwp", [D, 1040], F16, kind="ExternalInput").ap()
    ow_d = nc.dram_tensor("oww", [D, D], F16, kind="ExternalInput").ap()
    wr_d = nc.dram_tensor("wrc", [128, 32], F32, kind="ExternalInput").ap()
    w1_d = nc.dram_tensor("w1r", [NE, NIC, 128, 1024], F32, kind="ExternalInput").ap()
    b1_d = nc.dram_tensor("b1c", [128, 128], F32, kind="ExternalInput").ap()
    wo_d = nc.dram_tensor("wor2", [NCH, 128, NIC, 128], F32, kind="ExternalInput").ap()
    bo_d = nc.dram_tensor("bor", [1, D], F32, kind="ExternalInput").ap()
    gam_d = nc.dram_tensor("gamt", [128, D], F32, kind="ExternalInput").ap()
    bet_d = nc.dram_tensor("bett", [128, D], F32, kind="ExternalInput").ap()
    out_d = nc.dram_tensor("out", [T, D], F32, kind="ExternalOutput").ap()

    x1f_d = nc.dram_tensor("x1f", [T, D], F32, kind="Internal").ap()
    ghi_d = nc.dram_tensor("ghi", [NIC, 128, NSEQ * SLOTS], F16, kind="Internal").ap()
    glo_d = nc.dram_tensor("glo", [NIC, 128, NSEQ * SLOTS], F16, kind="Internal").ap()
    xg_d = nc.dram_tensor("xg", [NXG, D], F32, kind="Internal").ap()
    yg_d = nc.dram_tensor("yg", [NXG, D], F32, kind="Internal").ap()
    slot_d = nc.dram_tensor("slotd", [NSEQ * NTG, TG], F32, kind="Internal").ap()

    dbg = {}
    if debug and stage == "qkv":
        dbg["qT"] = nc.dram_tensor("dbg_qT", [D, S], F16, kind="ExternalOutput").ap()
        dbg["kT"] = nc.dram_tensor("dbg_kT", [D, S], F16, kind="ExternalOutput").ap()
        dbg["vp"] = nc.dram_tensor("dbg_vp", [128, 16 * 1040], F16, kind="ExternalOutput").ap()
    if debug and stage == "attn":
        dbg["at_hi"] = nc.dram_tensor("dbg_athi", [D, TG], F16, kind="ExternalOutput").ap()
        dbg["at_lo"] = nc.dram_tensor("dbg_atlo", [D, TG], F16, kind="ExternalOutput").ap()
    if debug and stage == "ln1":
        dbg["x1"] = nc.dram_tensor("dbg_x1", [T, D], F32, kind="ExternalOutput").ap()
    if debug and stage == "slots":
        dbg["idx"] = nc.dram_tensor("dbg_idx", [128, 32], F32, kind="ExternalOutput").ap()

    pos = np.arange(S, dtype=np.float64)
    inv = 10000.0 ** (-np.arange(32, dtype=np.float64) / 32.0)
    freq = inv[:, None] * pos[None, :]
    cc_d = nc.inline_tensor(np.tile(np.cos(freq), (4, 1)).astype(np.float32), "cc_c")
    ss_d = nc.inline_tensor(np.tile(np.sin(freq), (4, 1)).astype(np.float32), "ss_c")
    masks_np = np.zeros((4, 128, TG), np.float16)
    for di in range(4):
        k_idx = np.arange(128)[:, None] + di * 128
        q_idx = np.arange(TG)[None, :]
        masks_np[di][k_idx > q_idx] = np.float16(MASKVAL)
    mask_d = [nc.inline_tensor(masks_np[i], f"mask{i}_c") for i in range(4)]
    id16_d = nc.inline_tensor(np.eye(128, dtype=np.float16), "id16_c")
    id32_d = nc.inline_tensor(np.eye(128, dtype=np.float32), "id32_c")
    sel64_np = np.zeros((1, 128), np.float32)
    sel64_np[0, :64] = 1.0
    sel64_d = nc.inline_tensor(sel64_np, "sel64_c")
    sel64b_np = np.zeros((1, 128), np.float32)
    sel64b_np[0, 64:] = 1.0
    sel64b_d = nc.inline_tensor(sel64b_np, "sel64b_c")
    ones128_d = nc.inline_tensor(np.ones((1, 128), np.float16), "ones128_c")
    ones512_d = nc.inline_tensor(np.ones((1, 512), np.float32), "ones512_c")
    trrow_np = (TRASH + (np.arange(512) % 128)).astype(np.float32)[None, :]
    trrow_d = nc.inline_tensor(trrow_np, "trrow_c")

    AP_ = Arena(nc, 40448)
    AR1 = Arena(nc, 63488)
    AR2 = Arena(nc, 98816)

    class Multi:

        def alloc(self, name, shape, dtype):
            try:
                return AR1.alloc(name, shape, dtype)
            except MemoryError:
                return AR2.alloc(name, shape, dtype)

    AC = Multi()

    id16_s = AP_.alloc("id16", [128, 128], F16)
    id32_s = AP_.alloc("id32", [128, 128], F32)
    sel64_s = AP_.alloc("sel64", [1, 128], F32)
    sel64b_s = AP_.alloc("sel64b", [1, 128], F32)
    ones128_s = AP_.alloc("ones128", [1, 128], F16)
    wr32_s = AP_.alloc("wr32", [128, 32], F32)
    wrh_s = AP_.alloc("wrh", [128, 32], F16)
    wrl_s = AP_.alloc("wrl", [128, 32], F16)
    b1_s = AP_.alloc("b1", [128, 128], F32)
    slab32_s = [AP_.alloc(f"slab32_{i}", [128, 1024], F32) for i in range(2)]
    xrow_s = [AP_.alloc(f"xrow{i}", [128, D], F16) for i in range(4)]
    z_s = [AP_.alloc(f"z{i}", [128, D], F32) for i in range(2)]
    sqd_s = AP_.alloc("sqd", [128, D], F32)
    vec_s = AP_.alloc("vecs", [128, 32], F32)
    x1t_s = [AP_.alloc(f"x1t{i}", [128, D], F32) for i in range(2)]
    eps5_s = AP_.alloc("eps5", [128, 1], F32)
    eps12_s = AP_.alloc("eps12", [128, 1], F32)
    idxf_s = AP_.alloc("idxf", [128, 32], F32)
    idxi_s = AP_.alloc("idxi", [128, 32], mybir.dt.int32)

    xts_s = [AR1.alloc(f"xts{c}", [128, S], F16) for c in range(NCH)]
    wch_s = [AR1.alloc(f"wch{c}", [128, 1040], F16) for c in range(NCH)]
    ccs_s = AR1.alloc("ccs", [128, TG], F32)
    sss_s = AR1.alloc("sss", [128, TG], F32)
    rp_s = [AR1.alloc(f"rp{i}", [128, TG], F32) for i in range(4)]
    AR1.freebuf(*([f"xts{c}" for c in range(NCH)] + [f"wch{c}" for c in range(NCH)]
                  + ["ccs", "sss"] + [f"rp{i}" for i in range(4)]))
    e32_s = [AR1.alloc(f"e32_{i}", [128, 1024], F32) for i in range(2)]
    _o0 = AP_.base_addr + AP_.live["slab32_0"][0]
    _o1 = AP_.base_addr + AP_.live["slab32_1"][0]
    ehi_s = [AR1.alloc(f"ehi_{i}", [128, 1024], F16) for i in range(2)] + [
        nc.alloc_sbuf_tensor_at("ehi_2", [128, 1024], F16, offset=_o0),
        nc.alloc_sbuf_tensor_at("ehi_3", [128, 1024], F16, offset=_o0 + 2048)]
    elo_s = [AR1.alloc(f"elo_{i}", [128, 1024], F16) for i in range(2)] + [
        nc.alloc_sbuf_tensor_at("elo_2", [128, 1024], F16, offset=_o1),
        nc.alloc_sbuf_tensor_at("elo_3", [128, 1024], F16, offset=_o1 + 2048)]
    rec_s = AR1.alloc("rec", [1, 2048], F32)
    bsb_s = AR1.alloc("bsb", [64, 2048], F32)
    t32_s = AR1.alloc("t32", [128, TG], F32)
    athi_s = [AR1.alloc(f"athi{c}", [128, TG], F16) for c in range(NCH)]
    atlo_s = [AR1.alloc(f"atlo{c}", [128, TG], F16) for c in range(NCH)]
    owh_s = [AR1.alloc(f"owh{i}", [128, D], F16) for i in range(4)]
    msk_s = [AR1.alloc(f"msk{i}", [128, TG], F16) for i in range(4)]
    AR1.freebuf(*(["e32_0", "e32_1", "ehi_0", "ehi_1", "elo_0", "elo_1",
                   "rec", "bsb", "t32"]
                  + [f"athi{c}" for c in range(NCH)] + [f"atlo{c}" for c in range(NCH)]
                  + [f"owh{i}" for i in range(4)] + [f"msk{i}" for i in range(4)]))
    lgs_s = AR1.alloc("lgs", [4, TG], F32)
    lrow_s = AR1.alloc("lrow", [1, NE * TG], F32)
    rtmp_s = AR1.alloc("rtmp", [1, 7 * TG], F32)
    onesr_s = AR1.alloc("onesr", [1, TG], F32)
    trrow_s = AR1.alloc("trrow", [1, TG], F32)
    carr_s = AR1.alloc("carr", [1, 32], F32)
    ovi_s = AR1.alloc("ovi", [1, TG], mybir.dt.int32)
    slotr_s = AR1.alloc("slotr", [1, TG], F32)
    AR1.freebuf("lgs", "lrow", "rtmp", "onesr", "trrow", "carr", "ovi", "slotr")
    slabh_s = [[AR1.alloc(f"slabh{i}_{e}", [128, 1024], F16) for e in range(NE)]
               for i in range(2)]
    slabl_s = [[AR1.alloc(f"slabl{i}_{e}", [128, 1024], F16) for e in range(NE)]
               for i in range(2)]
    gf_s = [AR1.alloc(f"gfull{i}", [128, GR], F32) for i in range(2)]
    ghi_sb = [AR1.alloc(f"ghisb{i}", [128, GR], F16) for i in range(2)]
    glo_sb = [AR1.alloc(f"glosb{i}", [128, GR], F16) for i in range(2)]
    AR1.freebuf(*[f"slabh{i}_{e}" for i in range(2) for e in range(NE)],
                *[f"slabl{i}_{e}" for i in range(2) for e in range(NE)],
                "gfull0", "gfull1", "ghisb0", "ghisb1", "glosb0", "glosb1")

    qT_s = [AR2.alloc(f"qT{c}", [128, S], F16) for c in range(NCH)]
    kT_s = [AR2.alloc(f"kT{c}", [128, S], F16) for c in range(NCH)]
    vp_s = AR2.alloc("vp", [128, 16 * 1040], F16)
    AR2.freebuf(*([f"qT{c}" for c in range(NCH)] + [f"kT{c}" for c in range(NCH)]
                  + ["vp"]))
    xgTh_s = [AR2.alloc(f"xgTh{c}", [128, SLOTS], F16) for c in range(NCH)]
    xgTl_s = [AR2.alloc(f"xgTl{c}", [128, SLOTS], F16) for c in range(NCH)]
    x1Tt_s = [AR2.alloc(f"x1Tt{i}", [128, 1024], F16) for i in range(4)]
    AR2.freebuf(*([f"xgTh{c}" for c in range(NCH)] + [f"xgTl{c}" for c in range(NCH)]
                  + [f"x1Tt{i}" for i in range(4)]))
    WB = 512
    gslh_s = [AC.alloc(f"gslh{ic}", [128, WB], F16) for ic in range(NIC)]
    gsll_s = [AC.alloc(f"gsll{ic}", [128, WB], F16) for ic in range(NIC)]
    wjh_s = [AC.alloc(f"wjh{i}", [128, NIC * 128], F16) for i in range(2)]
    wjl_s = [AC.alloc(f"wjl{i}", [128, NIC * 128], F16) for i in range(2)]
    wsp_h = AC.alloc("wsph", [128, 1024], F16)
    wsp_l = AC.alloc("wspl", [128, 1024], F16)
    ysb_s = AC.alloc("ysb", [128, WB], F32)
    yrow_s = [AC.alloc(f"yrow{i}", [128, D], F32) for i in range(4)]
    x1r_s = [AC.alloc(f"x1r{i}", [128, D], F32) for i in range(2)]
    outt_s = [AC.alloc(f"outt{i}", [128, D], F32) for i in range(2)]
    gam_s = AC.alloc("gams", [128, D], F32)
    bet_s = AC.alloc("bets", [128, D], F32)
    bo_s = AC.alloc("bo", [1, D], F32)
    ones512_s = AC.alloc("ones512", [1, 512], F32)

    PA = nc.alloc_psum_tensor("PA", [128, 2048], F32)
    PB = nc.alloc_psum_tensor("PB", [128, 1024], F32)
    PC = nc.alloc_psum_tensor("PC", [128, 1024], F32)

    QL = [f"qa{i}" for i in range(6)]
    GL = [f"gb{i}" for i in range(4)]
    sem_handles = {n: nc.alloc_semaphore(name=f"sem_{n}")
                   for n in tuple(QL) + tuple(GL) + ("t", "s", "v")}

    def walk(me, E):
        w = Walk(me, E, sem_handles)
        w.pools = {"q": QL, "g": GL}
        w.rr = {"q": 0, "g": 0}
        C = w.cnt
        mark, wmark = w.mark, w.wmark

        def qsn():
            return [(l, C[l]) for l in QL]

        def gsn():
            return [(l, C[l]) for l in GL]

        for dst, srcp in ((id16_s, id16_d.ap()), (id32_s, id32_d.ap()),
                          (sel64_s, sel64_d.ap()), (sel64b_s, sel64b_d.ap()),
                          (ones128_s, ones128_d.ap()),
                          (wr32_s, wr_d), (b1_s, b1_d)):
            w.dmal("q", dst[:], srcp)
        const_tok = qsn()
        w.do("v", lambda E: E.memset(eps5_s[:], LN_EPS_OUT))
        w.do("v", lambda E: E.memset(eps12_s[:], LN_EPS_IN))
        w.drain("v")
        w.do("s", lambda E: E.activation(wrh_s[:], wr32_s[:], AF.Copy),
             inc=("s", 1), wait=const_tok)
        sC = C["s"]
        w.do("v", lambda E: E.tensor_tensor(wrl_s[:], wr32_s[:], wrh_s[:], ALU.subtract),
             inc=("v", 1), wait=[("s", sC)])
        w.do("v", lambda E: E.memset(z_s[0][:], 0.0), inc=("v", 1))
        w.drain("v")
        zf_v = C["v"]
        for r0 in range(0, NXG, 128):
            w.dmal("g", xg_d[r0:r0 + 128, :], z_s[0][:], wait=[("v", zf_v)])
            w.dmal("g", yg_d[r0:r0 + 128, :], z_s[0][:])
        zf_tok = gsn()
        w.marks["z0"] = list(zf_tok)

        def qkv_phase(s_idx):
            tok0 = s_idx * S
            for c in range(NCH):
                wmark("q", f"xts{c}")
                w.dmal("q", xts_s[c][:], xT_d[c * 128:(c + 1) * 128, tok0:tok0 + S])
            x_tok = qsn()

            for wsrc, dstT in ((qwp_d, qT_s), (kwp_d, kT_s)):
                wmark("q", "wch")
                for dc in range(NCH):
                    w.dmal("q", wch_s[dc][:, 0:1024], wsrc[dc * 128:(dc + 1) * 128, :])
                w_tok = qsn()
                for tg in range(NTG):
                    wmark("q", "ccs")
                    ct1 = w.dmal("q", ccs_s[:], cc_d.ap()[:, tg * TG:(tg + 1) * TG])
                    ct2 = w.dmal("q", sss_s[:], ss_d.ap()[:, tg * TG:(tg + 1) * TG])
                    for qd in range(4):
                        b = (qd % 2) * 1024
                        pe = PA[:, b:b + 512]
                        po = PA[:, b + 512:b + 1024]
                        wmark("t", f"pab{qd % 2}")
                        for dc in range(NCH):
                            rhs = xts_s[dc][:, tg * TG:(tg + 1) * TG]
                            w.mm(pe, wch_s[dc][:, (2 * qd) * 128:(2 * qd + 1) * 128], rhs,
                                 start=(dc == 0), stop=(dc == NCH - 1),
                                 wait=w_tok + x_tok)
                            w.mm(po, wch_s[dc][:, (2 * qd + 1) * 128:(2 * qd + 2) * 128],
                                 rhs, start=(dc == 0), stop=(dc == NCH - 1),
                                 inc=("t", 1) if dc == NCH - 1 else None)
                        tC = C["t"]
                        wmark("v", "rp")
                        w.do("v", lambda E, pe=pe: E.tensor_tensor(
                            rp_s[0][:], pe, ccs_s[:], ALU.mult),
                             wait=[("t", tC), ct1, ct2])
                        w.do("v", lambda E, po=po: E.tensor_tensor(
                            rp_s[1][:], po, sss_s[:], ALU.mult))
                        w.do("v", lambda E, pe=pe: E.tensor_tensor(
                            rp_s[2][:], pe, sss_s[:], ALU.mult))
                        w.do("v", lambda E, po=po: E.tensor_tensor(
                            rp_s[3][:], po, ccs_s[:], ALU.mult), inc=("v", 1))
                        mark(f"pab{qd % 2}", "v")
                        if qd == 3:
                            mark("ccs", "v")
                        w.drain("v")
                        de = dstT[2 * qd][:, tg * TG:(tg + 1) * TG]
                        do_ = dstT[2 * qd + 1][:, tg * TG:(tg + 1) * TG]
                        w.do("v", lambda E, de=de: E.tensor_tensor(
                            de, rp_s[0][:], rp_s[1][:], ALU.subtract))
                        w.do("v", lambda E, do_=do_: E.tensor_tensor(
                            do_, rp_s[2][:], rp_s[3][:], ALU.add), inc=("v", 1))
                        mark("rp", "v")
                        w.drain("v")
                mark("wch", "t")

            wmark("q", "wch")
            for dc in range(NCH):
                w.dmal("q", wch_s[dc][:, 0:1040], vwp_d[dc * 128:(dc + 1) * 128, :])
            w_tok = qsn()
            pv = 0
            for tt in range(16):
                for j0, jn in ((0, 512), (512, 512), (1024, 16)):
                    ps = PB[:, (pv % 2) * 512:(pv % 2) * 512 + 512]
                    wmark("t", f"pbb{pv % 2}")
                    for dc in range(NCH):
                        w.mm(ps[:, 0:jn], xts_s[dc][:, tt * 128:(tt + 1) * 128],
                             wch_s[dc][:, j0:j0 + jn],
                             start=(dc == 0), stop=(dc == NCH - 1),
                             wait=w_tok + x_tok,
                             inc=("t", 1) if dc == NCH - 1 else None)
                    tC = C["t"]
                    dst = vp_s[:, tt * 1040 + j0: tt * 1040 + j0 + jn]
                    w.do("s", lambda E, dst=dst, ps=ps, jn=jn: E.activation(
                        dst, ps[:, 0:jn], AF.Copy), inc=("s", 1), wait=[("t", tC)])
                    mark(f"pbb{pv % 2}", "s")
                    pv += 1
            sv = C["s"]
            w.do("v", lambda E: E.memset(
                vp_s[:, :].rearrange("p (t c) -> p t c", c=1040)[:, :, 64::65], 1.0),
                 inc=("v", 1), wait=[("s", sv)])
            w.drain("v")
            mark("wch", "t")
            w.barrier()

        def attn_phase(s_idx):
            for i in range(4):
                w.dmal("q", msk_s[i][:], mask_d[i].ap())
            msk_tok = qsn()

            for qg in range(NTG):
                for q in range(4):
                    nkt = 4 * (qg + 1)
                    svtok = {}

                    def emit_scores(kt, q=q, qg=qg, nkt=nkt, svtok=svtok):
                        wmark("t", "bq")
                        wmark("t", "pafV")
                        masked = kt >= qg * 4
                        di = kt - qg * 4
                        toks2 = []
                        for h2 in range(2):
                            wmark("t", f"scq{h2}")
                            for m in (2 * h2, 2 * h2 + 1):
                                dst = PA[:, m * 512:(m + 1) * 512]
                                for half in range(2):
                                    ch = 2 * q + half
                                    last = (half == 1) and not masked
                                    w.mm(dst,
                                         kT_s[ch][32 * m:32 * m + 32,
                                                  kt * 128:(kt + 1) * 128],
                                         qT_s[ch][32 * m:32 * m + 32,
                                                  qg * TG:(qg + 1) * TG],
                                         start=(half == 0), stop=last,
                                         tp=(32 * m, 0), wait=msk_tok,
                                         inc=("t", 1) if (last and m == 2 * h2 + 1)
                                         else None)
                            if masked:
                                for m in (2 * h2, 2 * h2 + 1):
                                    w.mm(PA[:, m * 512:(m + 1) * 512], id16_s[:],
                                         msk_s[di][:], start=False, stop=True,
                                         inc=("t", 1) if m == 2 * h2 + 1 else None)
                            toks2.append(C["t"])
                        svtok[kt] = toks2

                    def emit_expchain(kt, svtok=svtok):
                        tC0, tC1 = svtok[kt]
                        stoks = {}
                        for h2 in range(2):
                            eb = (kt + h2) % 2
                            wmark("s", f"e32_{eb}")
                            w.do("s", lambda E, h2=h2, eb=eb: E.activation(
                                e32_s[eb][:], PA[:, h2 * 1024:(h2 + 1) * 1024],
                                AF.Exp, scale=SCALE),
                                 inc=("s", 1), wait=[("t", tC0 if h2 == 0 else tC1)])
                            mark(f"scq{h2}", "s")
                            stoks[h2] = C["s"]
                        toks = {}
                        for h2 in (1, 0):
                            eb = (kt + h2) % 2
                            e4 = (2 * kt + h2) % 4
                            wmark("v", f"ehi_{e4}")
                            w.do("v", lambda E, eb=eb, e4=e4: E.tensor_copy(
                                ehi_s[e4][:], e32_s[eb][:]), inc=("v", 1),
                                 wait=[("s", stoks[h2])])
                            vCc = C["v"]
                            wmark("v", f"elo_{e4}")
                            w.do("v", lambda E, eb=eb, e4=e4: E.tensor_tensor(
                                elo_s[e4][:], e32_s[eb][:], ehi_s[e4][:], ALU.subtract),
                                 inc=("v", 1), wait=[("v", vCc)])
                            mark(f"e32_{eb}", "v")
                            toks[h2] = (vCc, C["v"])
                        svtok[kt] = toks

                    def emit_vmm(kt, q=q, nkt=nkt, svtok=svtok):
                        toks = svtok[kt]
                        if kt == 0:
                            wmark("t", "atb")
                        for lo in range(2):
                            for hh in range(4):
                                h2 = hh // 2
                                e4 = (2 * kt + h2) % 4
                                h = 4 * q + hh
                                PX = PB if hh < 2 else PC
                                pat = PX[0:65, (hh % 2) * 512:(hh % 2) * 512 + 512]
                                lhs = vp_s[:, kt * 1040 + 65 * h: kt * 1040 + 65 * h + 65]
                                src = (elo_s if lo else ehi_s)[e4]
                                w.mm(pat, lhs,
                                     src[:, (hh % 2) * 512:(hh % 2) * 512 + 512],
                                     start=(kt == 0 and lo == 0),
                                     stop=(kt == nkt - 1 and lo == 1),
                                     wait=[("v", toks[h2][lo])] if hh % 2 == 0 else None,
                                     inc=("t", 1) if hh == 3 else None)
                            nm = "elo" if lo else "ehi"
                            mark(f"{nm}_{(2 * kt) % 4}", "t")
                            mark(f"{nm}_{(2 * kt + 1) % 4}", "t")

                    for kt in range(nkt):
                        emit_scores(kt)
                        if kt > 0:
                            emit_vmm(kt - 1)
                        emit_expchain(kt)
                    emit_vmm(nkt - 1)
                    tC = C["t"]

                    wmark("s", "rec")
                    for hh in range(4):
                        PX = PB if hh < 2 else PC
                        w.do("s", lambda E, hh=hh, PX=PX: E.activation(
                            rec_s[:, hh * 512:(hh + 1) * 512],
                            PX[64:65, (hh % 2) * 512:(hh % 2) * 512 + 512], AF.Copy),
                             inc=("s", 1) if hh == 3 else None,
                             wait=[("t", tC)])
                    sC = C["s"]
                    w.do("v", lambda E: E.reciprocal(rec_s[:], rec_s[:]),
                         inc=("v", 1), wait=[("s", sC)])
                    vC = C["v"]
                    mark("rec", "v")
                    wmark("t", "scq0")
                    wmark("t", "scq1")
                    for hh in range(4):
                        w.mm(PA[:, hh * 512:hh * 512 + 512], sel64_s[:],
                             rec_s[:, hh * 512:(hh + 1) * 512], start=True, stop=True,
                             wait=[("v", vC)], inc=("t", 1) if hh == 3 else None)
                    tC2 = C["t"]
                    wmark("s", "bsb")
                    for hh in range(4):
                        w.do("s", lambda E, hh=hh: E.activation(
                            bsb_s[0:64, hh * 512:(hh + 1) * 512],
                            PA[0:64, hh * 512:hh * 512 + 512], AF.Copy),
                             inc=("s", 1) if hh == 3 else None, wait=[("t", tC2)])
                    sC = C["s"]
                    mark("bq", "s")
                    for half in range(2):
                        p = 2 * q + half
                        hh0, hh1 = 2 * half, 2 * half + 1
                        PX = PB if half == 0 else PC
                        wmark("v", "t32")
                        w.do("v", lambda E, PX=PX, hh0=hh0: E.tensor_tensor(
                            t32_s[0:64, :], PX[0:64, 0:512],
                            bsb_s[0:64, hh0 * 512:(hh0 + 1) * 512], ALU.mult),
                             wait=[("s", sC)])
                        w.do("v", lambda E, PX=PX, hh1=hh1: E.tensor_tensor(
                            t32_s[64:128, :], PX[0:64, 512:1024],
                            bsb_s[0:64, hh1 * 512:(hh1 + 1) * 512], ALU.mult),
                             inc=("v", 1))
                        vC = C["v"]
                        if half == 1:
                            mark("bsb", "v")
                            mark("atb", "v")
                        wmark("s", f"athi{p}")
                        w.do("s", lambda E, p=p: E.activation(
                            athi_s[p][:], t32_s[:], AF.Copy), inc=("s", 1),
                             wait=[("v", vC)])
                        sC2 = C["s"]
                        wmark("v", f"atlo{p}")
                        w.do("v", lambda E, p=p: E.tensor_tensor(
                            atlo_s[p][:], t32_s[:], athi_s[p][:], ALU.subtract),
                             inc=("v", 1), wait=[("s", sC2)])
                        mark("t32", "v")

                if stage == "attn" and qg == 0:
                    vv, ss_ = C["v"], C["s"]
                    for c in range(NCH):
                        w.dmal("g", dbg["at_hi"][c * 128:(c + 1) * 128, :], athi_s[c][:],
                               wait=[("v", vv), ("s", ss_)])
                        w.dmal("g", dbg["at_lo"][c * 128:(c + 1) * 128, :], atlo_s[c][:])
                    w.waitpool("g", "g")
                    return True

                athi_v = C["v"]
                athi_sc = C["s"]
                pbank = {0: PA[:, 0:1024], 1: PA[:, 1024:2048],
                         2: PB[:, 0:1024], 3: PC[:, 0:1024]}
                xr_toks = {}
                for tt in range(4):
                    tok0 = s_idx * S + qg * TG + tt * 128
                    wmark("q", f"xrow{tt}")
                    xr_toks[tt] = w.dmal("q", xrow_s[tt][:], xrows_d[tok0:tok0 + 128, :])
                for halfw in range(2):
                    wmark("q", "owh")
                    for cc in range(4):
                        c = 4 * halfw + cc
                        w.dmal("q", owh_s[cc][:], ow_d[c * 128:(c + 1) * 128, :])
                    ow_tok = qsn()
                    for tt in range(4):
                        ps = pbank[tt]
                        if halfw == 0:
                            if tt < 2:
                                wmark("t", "pafV")
                            else:
                                wmark("t", "atb")
                        for cc in range(4):
                            c = 4 * halfw + cc
                            for jg in range(2):
                                w.mm(ps[:, jg * 512:(jg + 1) * 512],
                                     athi_s[c][:, tt * 128:(tt + 1) * 128],
                                     owh_s[cc][:, jg * 512:(jg + 1) * 512],
                                     start=(halfw == 0 and cc == 0), stop=False,
                                     wait=[("v", athi_v), ("s", athi_sc)] + ow_tok)
                                w.mm(ps[:, jg * 512:(jg + 1) * 512],
                                     atlo_s[c][:, tt * 128:(tt + 1) * 128],
                                     owh_s[cc][:, jg * 512:(jg + 1) * 512],
                                     start=False, stop=(halfw == 1 and cc == 3),
                                     inc=("t", 1) if (tt == 3 and cc == 3
                                                      and jg == 1) else None)
                    mark("owh", "t")
                opt_t = C["t"]
                for tt in range(4):
                    ti = tt % 2
                    tok0 = s_idx * S + qg * TG + tt * 128
                    ps = pbank[tt]
                    wmark("v", f"z{ti}")
                    w.do("v", lambda E, ps=ps, ti=ti, tt=tt:
                         E.scalar_tensor_tensor(
                             z_s[ti][:], ps[:, :], 1.0, xrow_s[tt][:],
                             ALU.mult, ALU.add, accum_out=vec_s[:, 8:9]),
                         inc=("v", 1), wait=[("t", opt_t), xr_toks[tt]])
                    if tt == 1:
                        mark("pafV", "v")
                    if tt == 3:
                        mark("atb", "v")
                    mark(f"xrow{tt}", "v")
                    vC = C["v"]
                    wmark("s", "sqd")
                    w.do("s", lambda E, ti=ti: E.activation(
                        sqd_s[:], z_s[ti][:], AF.Square, accum_out=vec_s[:, 10:11]),
                         wait=[("v", vC)])
                    w.do("s", lambda E: E.mul(vec_s[:, 1:2], vec_s[:, 8:9], 1.0 / D))
                    w.drain("s")
                    w.do("s", lambda E: E.square(vec_s[:, 2:3], vec_s[:, 1:2]),
                         inc=("s", 1))
                    sC = C["s"]
                    mark("sqd", "s")
                    w.do("v", lambda E: E.scalar_tensor_tensor(
                        vec_s[:, 3:4], vec_s[:, 10:11], 1.0 / D, vec_s[:, 2:3],
                        ALU.mult, ALU.subtract), inc=("v", 1), wait=[("s", sC)])
                    vC = C["v"]
                    w.do("s", lambda E: E.activation(
                        vec_s[:, 4:5], vec_s[:, 3:4], AF.Sqrt, bias=eps5_s[:]),
                         inc=("s", 1), wait=[("v", vC)])
                    sC = C["s"]
                    w.drain("v")
                    w.do("v", lambda E: E.reciprocal(vec_s[:, 5:6], vec_s[:, 4:5]),
                         wait=[("s", sC)])
                    w.drain("v")
                    wmark("v", f"x1t{ti}")
                    w.do("v", lambda E, ti=ti: E.tensor_scalar(
                        x1t_s[ti][:], z_s[ti][:], vec_s[:, 1:2], vec_s[:, 5:6],
                        ALU.subtract, ALU.mult), inc=("v", 1))
                    vC = C["v"]
                    mark(f"z{ti}", "v")
                    x1tok = [w.dmal("g", x1f_d[tok0:tok0 + 128, :], x1t_s[ti][:],
                                    wait=[("v", vC)])]
                    if stage == "ln1":
                        x1tok.append(w.dmal("g", dbg["x1"][tok0:tok0 + 128, :],
                                            x1t_s[ti][:]))
                    w.marks[f"x1t{ti}"] = x1tok
            w.barrier()
            return False

        def slots_phase(s_idx):
            tok0 = s_idx * S
            w.do("v", lambda E: (E.memset(carr_s[:], 0.0), E.memset(onesr_s[:], 1.0),
                                 E.drain())[-1], inc=("v", 1))
            tr_tok = w.dmal("q", trrow_s[:], trrow_d.ap())
            for tg in range(NTG):
                for tl4 in range(4):
                    tile = tg * 4 + tl4
                    ti = tile % 2
                    wmark("q", f"x1t{ti}")
                    xt_tok = w.dmal("q", x1t_s[ti][:],
                                    x1f_d[tok0 + tile * 128: tok0 + (tile + 1) * 128, :])
                    tb = 2 * (tile % 2)
                    for c in range(NCH):
                        sl = PA[:, (c % 4) * 512:(c % 4) * 512 + 128]
                        wmark("t", f"pas{c % 4}")
                        w.do("t", lambda E, sl=sl, ti=ti, c=c: E.transpose(
                            sl, x1t_s[ti][:, c * 128:(c + 1) * 128], id32_s[:]),
                             inc=("t", 1), wait=[xt_tok])
                        tC = C["t"]
                        wmark("s", f"x1Tt{tb}")
                        w.do("s", lambda E, c=c, sl=sl, tb=tb: E.activation(
                            x1Tt_s[tb][:, c * 128:(c + 1) * 128], sl, AF.Copy),
                             inc=("s", 1), wait=[("t", tC)])
                        sC = C["s"]
                        wmark("v", f"x1Tt{tb + 1}")
                        w.do("v", lambda E, c=c, sl=sl, tb=tb: E.tensor_tensor(
                            x1Tt_s[tb + 1][:, c * 128:(c + 1) * 128], sl,
                            x1Tt_s[tb][:, c * 128:(c + 1) * 128], ALU.subtract),
                             inc=("v", 1), wait=[("s", sC)])
                        mark(f"pas{c % 4}", "v")
                    tile_s, tile_v = C["s"], C["v"]
                    if tl4 == 0:
                        wmark("t", "lgbank")
                    first = True
                    for dc in range(NCH):
                        lh = wrh_s[:, dc * 4:(dc + 1) * 4]
                        ll = wrl_s[:, dc * 4:(dc + 1) * 4]
                        xh = x1Tt_s[tb][:, dc * 128:(dc + 1) * 128]
                        xl = x1Tt_s[tb + 1][:, dc * 128:(dc + 1) * 128]
                        dstp = PB[0:4, tl4 * 128:(tl4 + 1) * 128]
                        w.mm(dstp, lh, xh, start=first, stop=False,
                             wait=[("s", tile_s), ("v", tile_v)])
                        first = False
                        w.mm(dstp, lh, xl, start=False, stop=False)
                        w.mm(dstp, ll, xh, start=False, stop=(dc == NCH - 1),
                             inc=("t", 1) if dc == NCH - 1 else None)
                    mark(f"x1Tt{tb}", "t")
                    mark(f"x1Tt{tb + 1}", "t")
                    mark(f"x1t{ti}", "t")
                tC = C["t"]
                wmark("v", "lgs")
                w.do("v", lambda E: E.tensor_copy(lgs_s[:], PB[0:4, 0:512]),
                     inc=("v", 1), wait=[("t", tC)])
                vC = C["v"]
                mark("lgbank", "v")
                wmark("q", "lrow")
                lr_tok = w.dmal("q", lrow_s[0:1, :].rearrange("a (p f) -> a p f", p=4),
                                lgs_s[:], wait=[("v", vC)])
                w.marks["lgs"] = [lr_tok]

                def rowops(E, tg=tg, s_idx=s_idx):
                    L, R = lrow_s, rtmp_s
                    CSUM = R[:, 4 * TG:5 * TG]
                    SLV = R[:, 5 * TG:6 * TG]
                    OV = R[:, 6 * TG:7 * TG]
                    E.tensor_tensor(R[:, 0:TG], L[:, 0:TG], L[:, TG:2 * TG], ALU.max)
                    E.tensor_tensor(R[:, TG:2 * TG], L[:, 2 * TG:3 * TG],
                                    L[:, 3 * TG:4 * TG], ALU.max)
                    E.drain()
                    E.tensor_tensor(R[:, 2 * TG:3 * TG], R[:, 0:TG], R[:, TG:2 * TG],
                                    ALU.max)
                    E.drain()
                    E.memset(R[:, 0:TG], 1.0)
                    E.memset(slotr_s[:], 0.0)
                    E.drain()
                    for e in range(NE):
                        ge = L[:, e * TG:(e + 1) * TG]
                        E.tensor_tensor(ge, ge, R[:, 2 * TG:3 * TG], ALU.is_ge)
                    E.drain()
                    for e in range(NE):
                        ge = L[:, e * TG:(e + 1) * TG]
                        me = R[:, 3 * TG:4 * TG]
                        E.tensor_tensor(me, ge, R[:, 0:TG], ALU.mult)
                        E.drain()
                        E.tensor_tensor_scan(CSUM, onesr_s[:], me,
                                             carr_s[:, e:e + 1], ALU.mult, ALU.add)
                        E.drain()
                        E.tensor_copy(carr_s[:, e:e + 1], CSUM[:, TG - 1:TG])
                        E.tensor_single_scalar(
                            SLV, CSUM, float(s_idx * SLOTS + e * CCAP - 1), ALU.add)
                        E.tensor_single_scalar(ovi_s[:], CSUM, float(CCAP), ALU.is_gt)
                        E.drain()
                        E.copy_predicated(SLV, ovi_s[:], trrow_s[:])
                        E.drain()
                        E.tensor_tensor(SLV, SLV, me, ALU.mult)
                        E.drain()
                        E.tensor_tensor(slotr_s[:], slotr_s[:], SLV, ALU.add)
                        E.drain()
                        if e < NE - 1:
                            E.tensor_tensor(R[:, 0:TG], R[:, 0:TG], me, ALU.subtract)
                            E.drain()
                    return E.drain()
                wmark("v", "slotr")
                w.do("v", rowops, inc=("v", 1), wait=[lr_tok, tr_tok])
                mark("lrow", "v")
                sr_v = C["v"]
                cb = s_idx * 16 + tg * 4
                row = s_idx * NTG + tg
                st_tok = w.dmal("g", slot_d[row:row + 1, :], slotr_s[:],
                                wait=[("v", sr_v)])
                w.marks["slotr"] = [st_tok]
                w.do("q", lambda E, cb=cb, row=row: E.dma_start(
                    out=idxf_s[:, cb:cb + 4],
                    in_=slot_d[row:row + 1, :].rearrange("a (c p) -> (a p) c", p=128),
                    allow_slow_non_contiguous=True),
                     inc=(QL[0], 16), wait=[(QL[0], C[QL[0]]), st_tok])
            idx_tok = (QL[0], C[QL[0]])
            w.do("v", lambda E, s_idx=s_idx: E.tensor_copy(
                idxi_s[:, s_idx * 16:(s_idx + 1) * 16],
                idxf_s[:, s_idx * 16:(s_idx + 1) * 16]),
                 inc=("v", 1), wait=[idx_tok])
            w.drain("v")
            idx_v = C["v"]

            if stage == "slots" and s_idx == 0:
                w.dmal("g", dbg["idx"][:, 0:16], idxf_s[:, 0:16], wait=[("v", idx_v)])
                w.waitpool("g", "g")
                return True

            from concourse.bass_core_rust import IndirectOffsetOnAxis
            for tile in range(16):
                ti = tile % 2
                wmark("q", f"x1t{ti}")
                xt_tok = w.dmal("q", x1t_s[ti][:],
                                x1f_d[tok0 + tile * 128: tok0 + (tile + 1) * 128, :])
                col = s_idx * 16 + tile
                w.do("g", lambda E, ti=ti, col=col: E.indirect_dma_start(
                    xg_d[:, :],
                    IndirectOffsetOnAxis(idxi_s[:, col:col + 1], 0),
                    x1t_s[ti][:], None),
                     inc=(GL[tile % 4], 16),
                     wait=[(GL[tile % 4], C[GL[tile % 4]]), xt_tok, ("v", idx_v)]
                     + zf_tok)
                w.marks[f"x1t{ti}"] = [(GL[tile % 4], C[GL[tile % 4]])]
            sc_tok = gsn()

            for st in range(SLOTS // 128):
                ti = st % 2
                wmark("q", f"x1t{ti}")
                xt_tok = w.dmal("q", x1t_s[ti][:],
                                xg_d[s_idx * SLOTS + st * 128:
                                     s_idx * SLOTS + (st + 1) * 128, :],
                                wait=sc_tok)
                col0 = st * 128
                for c in range(NCH):
                    sl = PA[:, (c % 4) * 512:(c % 4) * 512 + 128]
                    wmark("t", f"pas{c % 4}")
                    w.do("t", lambda E, sl=sl, ti=ti, c=c: E.transpose(
                        sl, x1t_s[ti][:, c * 128:(c + 1) * 128], id32_s[:]),
                         inc=("t", 1), wait=[xt_tok])
                    tC = C["t"]
                    wmark("s", f"xgTh{c}")
                    w.do("s", lambda E, c=c, sl=sl, col0=col0: E.activation(
                        xgTh_s[c][:, col0:col0 + 128], sl, AF.Copy),
                         inc=("s", 1), wait=[("t", tC)])
                    sC = C["s"]
                    wmark("v", f"xgTl{c}")
                    w.do("v", lambda E, c=c, sl=sl, col0=col0: E.tensor_tensor(
                        xgTl_s[c][:, col0:col0 + 128], sl,
                        xgTh_s[c][:, col0:col0 + 128], ALU.subtract),
                         inc=("v", 1), wait=[("s", sC)])
                    mark(f"pas{c % 4}", "v")
                mark(f"x1t{ti}", "t")
            w.barrier()
            return False

        def ffn_phase(s_idx):
            slot0 = s_idx * SLOTS
            sl_tok = {}

            def emit_w1split(ic):
                ip = ic % 2
                for e in range(NE):
                    sb32 = slab32_s[e % 2]
                    wmark("q", f"slab32_{e % 2}")
                    dtok = w.dmal("q", sb32[:], w1_d[e, ic])
                    wmark("s", f"slabh{ip}_{e}")
                    w.do("s", lambda E, e=e, ip=ip, sb32=sb32: E.activation(
                        slabh_s[ip][e][:], sb32[:], AF.Copy), inc=("s", 1),
                         wait=[dtok])
                    sC = C["s"]
                    wmark("v", f"slabl{ip}_{e}")
                    w.do("v", lambda E, e=e, ip=ip, sb32=sb32: E.tensor_tensor(
                        slabl_s[ip][e][:], sb32[:], slabh_s[ip][e][:], ALU.subtract),
                         inc=("v", 1), wait=[("s", sC)])
                    mark(f"slab32_{e % 2}", "v")
                sl_tok[ic] = (C["v"], C["s"])

            emit_w1split(0)
            for ic in range(NIC):
                ip = ic % 2
                if ic + 1 < NIC:
                    emit_w1split(ic + 1)
                slab_v, slab_s_cnt = sl_tok[ic]
                for e in range(NE):
                    for gr in range(CCAP // GR):
                        c0 = e * CCAP + gr * GR
                        ps = PA[:, (2 * e + gr) % 4 * 512:((2 * e + gr) % 4) * 512 + GR]
                        wmark("t", f"fps{(2 * e + gr) % 4}")
                        for dc in range(NCH):
                            lh = slabh_s[ip][e][:, dc * 128:(dc + 1) * 128]
                            ll = slabl_s[ip][e][:, dc * 128:(dc + 1) * 128]
                            xh = xgTh_s[dc][:, c0:c0 + GR]
                            xl = xgTl_s[dc][:, c0:c0 + GR]
                            w.mm(ps, lh, xh, start=(dc == 0), stop=False,
                                 wait=[("v", slab_v), ("s", slab_s_cnt)])
                            w.mm(ps, lh, xl, start=False, stop=False)
                            w.mm(ps, ll, xh, start=False, stop=(dc == NCH - 1),
                                 inc=("t", 1) if dc == NCH - 1 else None)
                        tC = C["t"]
                        gp = (2 * e + gr) % 2
                        wmark("s", f"gfull{gp}")
                        w.do("s", lambda E, ps=ps, e=e, ic=ic, gp=gp: E.activation(
                            gf_s[gp][:], ps, AF.Gelu,
                            bias=b1_s[:, e * 32 + ic:e * 32 + ic + 1]),
                             inc=("s", 1), wait=[("t", tC)])
                        mark(f"fps{(2 * e + gr) % 4}", "s")
                        w.drain("s")
                        wmark("s", f"ghisb{gp}")
                        w.do("s", lambda E, gp=gp: E.activation(
                            ghi_sb[gp][:], gf_s[gp][:], AF.Copy), inc=("s", 1))
                        sC = C["s"]
                        wmark("v", f"glosb{gp}")
                        w.do("v", lambda E, gp=gp: E.tensor_tensor(
                            glo_sb[gp][:], gf_s[gp][:], ghi_sb[gp][:], ALU.subtract),
                             inc=("v", 1), wait=[("s", sC)])
                        vC = C["v"]
                        mark(f"gfull{gp}", "v")
                        gh_tok = w.dmal("g", ghi_d[ic, :, slot0 + c0: slot0 + c0 + GR],
                                        ghi_sb[gp][:], wait=[("s", sC), ("v", vC)])
                        gl_tok = w.dmal("g", glo_d[ic, :, slot0 + c0: slot0 + c0 + GR],
                                        glo_sb[gp][:])
                        w.marks[f"ghisb{gp}"] = [gh_tok]
                        w.marks[f"glosb{gp}"] = [gl_tok]
                for e in range(NE):
                    w.marks[f"slabh{ip}_{e}"] = [("t", C["t"])]
                    w.marks[f"slabl{ip}_{e}"] = [("t", C["t"])]
            w.barrier()

        def wo_phase():
            for dst, srcp in ((gam_s, gam_d), (bet_s, bet_d), (bo_s, bo_d),
                              (ones512_s, ones512_d.ap())):
                w.dmal("q", dst[:], srcp)
            wo_tok = qsn()
            sp_tok = {}
            from concourse.bass_core_rust import IndirectOffsetOnAxis

            NWB = SLOTS // WB
            for s_idx in range(NSEQ):
              slot0 = s_idx * SLOTS
              for wb in range(NWB):
                g_ic_tok = {}
                for ic in range(NIC):
                    wmark("q", f"gsl{ic}")
                    t1 = w.dmal("q", gslh_s[ic][:],
                                ghi_d[ic, :, slot0 + wb * WB:slot0 + (wb + 1) * WB])
                    t2 = w.dmal("q", gsll_s[ic][:],
                                glo_d[ic, :, slot0 + wb * WB:slot0 + (wb + 1) * WB])
                    g_ic_tok[ic] = [t1, t2]

                def emit_wosplit(jc):
                    jp = jc % 2
                    wmark("s", f"wjh{jp}")
                    wmark("v", f"wjl{jp}")
                    for q4 in range(4):
                        sb32 = slab32_s[q4 % 2]
                        wmark("q", f"slab32_{q4 % 2}")
                        wsl_tok = w.dmal("q", sb32[:],
                                         wo_d[jc, :, q4 * 8:(q4 + 1) * 8, :])
                        w.do("s", lambda E, sb32=sb32, jp=jp, q4=q4: E.activation(
                            wjh_s[jp][:, q4 * 1024:(q4 + 1) * 1024], sb32[:], AF.Copy),
                             inc=("s", 1), wait=[wsl_tok])
                        sC = C["s"]
                        w.do("v", lambda E, sb32=sb32, jp=jp, q4=q4: E.tensor_tensor(
                            wjl_s[jp][:, q4 * 1024:(q4 + 1) * 1024], sb32[:],
                            wjh_s[jp][:, q4 * 1024:(q4 + 1) * 1024], ALU.subtract),
                             inc=("v", 1), wait=[("s", sC)])
                        mark(f"slab32_{q4 % 2}", "v")
                    sp_tok[jc] = (C["v"], C["s"])

                if wb == 0:
                    emit_wosplit(0)
                for jc in range(NCH):
                    jp = jc % 2
                    if not (jc == NCH - 1 and wb == NWB - 1):
                        emit_wosplit((jc + 1) % NCH)
                    wo_v, wo_s_cnt = sp_tok[jc]
                    ps = PB[:, jp * 512:jp * 512 + WB]
                    wmark("t", f"pbb{jp}")
                    for ic in range(NIC):
                        w.mm(ps, wjh_s[jp][:, ic * 128:(ic + 1) * 128], gslh_s[ic][:],
                             start=(ic == 0), stop=False,
                             wait=[("v", wo_v), ("s", wo_s_cnt)] + g_ic_tok[ic]
                                  + wo_tok)
                        w.mm(ps, wjh_s[jp][:, ic * 128:(ic + 1) * 128], gsll_s[ic][:],
                             start=False, stop=False)
                        w.mm(ps, wjl_s[jp][:, ic * 128:(ic + 1) * 128], gslh_s[ic][:],
                             start=False, stop=False)
                    w.mm(ps, bo_s[:, jc * 128:(jc + 1) * 128], ones512_s[:, 0:WB],
                         start=False, stop=True, inc=("t", 1))
                    tC = C["t"]
                    mark(f"wjh{jp}", "t")
                    mark(f"wjl{jp}", "t")
                    wmark("s", "ysb")
                    w.do("s", lambda E, ps=ps: E.activation(ysb_s[:], ps, AF.Copy),
                         inc=("s", 1), wait=[("t", tC)])
                    sC = C["s"]
                    mark(f"pbb{jp}", "s")
                    for quarter in range(4):
                        sl = PC[:, (quarter % 2) * 512:(quarter % 2) * 512 + 128]
                        wmark("t", f"pc{quarter % 2}")
                        w.do("t", lambda E, sl=sl, quarter=quarter: E.transpose(
                            sl, ysb_s[:, quarter * 128:quarter * 128 + 128], id32_s[:]),
                             inc=("t", 1), wait=[("s", sC)])
                        tC = C["t"]
                        wmark("v", f"yrow{quarter}")
                        w.do("v", lambda E, quarter=quarter, jc=jc, sl=sl: E.tensor_copy(
                            yrow_s[quarter][:, jc * 128:(jc + 1) * 128], sl),
                             inc=("v", 1), wait=[("t", tC)])
                        mark(f"pc{quarter % 2}", "v")
                    mark("ysb", "t")
                for ic in range(NIC):
                    w.marks[f"gsl{ic}"] = [("t", C["t"])]
                y_v = C["v"]
                for quarter in range(4):
                    r0 = slot0 + wb * WB + quarter * 128
                    yg_tok = w.dmal("g", yg_d[r0:r0 + 128, :], yrow_s[quarter][:],
                                    wait=[("v", y_v)])
                    w.marks[f"yrow{quarter}"] = [yg_tok]
              ygs_tok = gsn()
              for tile in range(16):
                    tok0 = s_idx * S + tile * 128
                    ti = tile % 2
                    col = s_idx * 16 + tile
                    wmark("v", f"yrow{ti}")
                    wmark("s", f"yrow{ti}")
                    gl = GL[2 + tile % 2]
                    w.do("g", lambda E, ti=ti, col=col: E.indirect_dma_start(
                        yrow_s[ti][:], None,
                        yg_d[:, :],
                        IndirectOffsetOnAxis(idxi_s[:, col:col + 1], 0)),
                         inc=(gl, 16),
                         wait=[(gl, C[gl])] + ygs_tok)
                    yr_tok = (gl, C[gl])
                    wmark("q", f"x1r{ti}")
                    x1r_tok = w.dmal("q", x1r_s[ti][:], x1f_d[tok0:tok0 + 128, :])
                    wmark("v", f"z{ti}")
                    wmark("v", f"outt{ti}")
                    w.do("v", lambda E, ti=ti: E.scalar_tensor_tensor(
                        z_s[ti][:], yrow_s[ti][:], 1.0, x1r_s[ti][:],
                        ALU.mult, ALU.add, accum_out=vec_s[:, 12:13]),
                         inc=("v", 1), wait=[yr_tok, x1r_tok])
                    vC = C["v"]
                    mark(f"yrow{ti}", "v")
                    w.do("v", lambda E, ti=ti: E.tensor_tensor(
                        outt_s[ti][:], bet_s[:], x1r_s[ti][:], ALU.add), inc=("v", 1))
                    wmark("s", "sqd")
                    w.do("s", lambda E, ti=ti: E.activation(
                        sqd_s[:], z_s[ti][:], AF.Square, accum_out=vec_s[:, 13:14]),
                         wait=[("v", vC)])
                    w.do("s", lambda E: E.mul(vec_s[:, 1:2], vec_s[:, 12:13], 1.0 / D))
                    w.drain("s")
                    w.do("s", lambda E: E.square(vec_s[:, 2:3], vec_s[:, 1:2]),
                         inc=("s", 1))
                    sC = C["s"]
                    mark("sqd", "s")
                    w.do("v", lambda E: E.scalar_tensor_tensor(
                        vec_s[:, 3:4], vec_s[:, 13:14], 1.0 / D, vec_s[:, 2:3],
                        ALU.mult, ALU.subtract), inc=("v", 1), wait=[("s", sC)])
                    vC = C["v"]
                    w.do("s", lambda E: E.activation(
                        vec_s[:, 4:5], vec_s[:, 3:4], AF.Sqrt, bias=eps12_s[:]),
                         inc=("s", 1), wait=[("v", vC)])
                    sC = C["s"]
                    w.drain("v")
                    w.do("v", lambda E: E.reciprocal(vec_s[:, 5:6], vec_s[:, 4:5]),
                         wait=[("s", sC)])
                    w.drain("v")
                    wmark("v", "sqd")
                    w.do("v", lambda E, ti=ti: E.tensor_scalar(
                        sqd_s[:], z_s[ti][:], vec_s[:, 1:2], vec_s[:, 5:6],
                        ALU.subtract, ALU.mult))
                    w.drain("v")
                    w.do("v", lambda E: E.tensor_tensor(
                        sqd_s[:], sqd_s[:], gam_s[:], ALU.mult))
                    w.drain("v")
                    w.do("v", lambda E, ti=ti: E.scalar_tensor_tensor(
                        z_s[ti][:], sqd_s[:], 1.0, outt_s[ti][:], ALU.mult, ALU.add,
                        accum_out=vec_s[:, 14:15]), inc=("v", 1))
                    vC = C["v"]
                    mark(f"x1r{ti}", "v")
                    mark("sqd", "v")
                    w.do("s", lambda E, ti=ti: E.activation(
                        sqd_s[:], z_s[ti][:], AF.Square, accum_out=vec_s[:, 15:16]),
                         wait=[("v", vC)])
                    w.do("s", lambda E: E.mul(vec_s[:, 1:2], vec_s[:, 14:15], 1.0 / D))
                    w.drain("s")
                    w.do("s", lambda E: E.square(vec_s[:, 2:3], vec_s[:, 1:2]),
                         inc=("s", 1))
                    sC = C["s"]
                    mark("sqd", "s")
                    w.do("v", lambda E: E.scalar_tensor_tensor(
                        vec_s[:, 3:4], vec_s[:, 15:16], 1.0 / D, vec_s[:, 2:3],
                        ALU.mult, ALU.subtract), inc=("v", 1), wait=[("s", sC)])
                    vC = C["v"]
                    w.do("s", lambda E: E.activation(
                        vec_s[:, 4:5], vec_s[:, 3:4], AF.Sqrt, bias=eps5_s[:]),
                         inc=("s", 1), wait=[("v", vC)])
                    sC = C["s"]
                    w.drain("v")
                    w.do("v", lambda E: E.reciprocal(vec_s[:, 5:6], vec_s[:, 4:5]),
                         wait=[("s", sC)])
                    w.drain("v")
                    w.do("v", lambda E, ti=ti: E.tensor_scalar(
                        outt_s[ti][:], z_s[ti][:], vec_s[:, 1:2], vec_s[:, 5:6],
                        ALU.subtract, ALU.mult), inc=("v", 1))
                    vC = C["v"]
                    mark(f"z{ti}", "v")
                    o_tok = w.dmal("g", out_d[tok0:tok0 + 128, :], outt_s[ti][:],
                                   wait=[("v", vC)])
                    w.marks[f"outt{ti}"] = [o_tok]
            w.waitpool("g", "g")

        for s_idx in range(NSEQ):
            qkv_phase(s_idx)
            if stage == "qkv" and s_idx == 0:
                vv, ss_ = C["v"], C["s"]
                for c in range(NCH):
                    w.dmal("g", dbg["qT"][c * 128:(c + 1) * 128, :], qT_s[c][:],
                           wait=[("v", vv), ("s", ss_)])
                    w.dmal("g", dbg["kT"][c * 128:(c + 1) * 128, :], kT_s[c][:])
                w.dmal("g", dbg["vp"][:], vp_s[:])
                w.waitpool("g", "g")
                return
            if attn_phase(s_idx):
                return
            if slots_phase(s_idx):
                return
            ffn_phase(s_idx)
        if stage == "ffn":
            w.waitpool("g", "g")
            return
        wo_phase()

    with nc.Block() as block:
        @block.sync
        def _(eng):
            walk("q", eng)

        @block.gpsimd
        def _(eng):
            walk("g", eng)

        @block.tensor
        def _(eng):
            walk("t", eng)

        @block.scalar
        def _(eng):
            walk("s", eng)

        @block.vector
        def _(eng):
            walk("v", eng)

    return nc


def _stage_inputs(inputs):
    x = np.asarray(inputs["L0_attn_x"])
    qw = np.asarray(inputs["L0_attn_qw"])
    kw = np.asarray(inputs["L0_attn_kw"])
    vw = np.asarray(inputs["L0_attn_vw"])
    ow = np.asarray(inputs["L0_attn_ow"])
    wr = np.asarray(inputs["L0_ffn_W_router"])
    w1 = np.asarray(inputs["L0_ffn_W1"])
    b1 = np.asarray(inputs["L0_ffn_b1"])
    wo = np.asarray(inputs["L0_ffn_Wo"])
    bo = np.asarray(inputs["L0_ffn_bo"])
    gam = np.asarray(inputs["L0_ffn_ln_gamma"])
    bet = np.asarray(inputs["L0_ffn_ln_beta"])

    perm = []
    for qd in range(4):
        for par in range(2):
            for m in range(4):
                h = 4 * qd + m
                for i in range(32):
                    perm.append(64 * h + 2 * i + par)
    perm = np.array(perm)
    qwp = np.ascontiguousarray(qw[:, perm])
    kwp = np.ascontiguousarray(kw[:, perm])
    vwp = np.zeros((D, 1040), np.float16)
    for h in range(NH):
        vwp[:, 65 * h:65 * h + 64] = vw[:, 64 * h:64 * h + 64]
    wrc = np.ascontiguousarray(
        wr.reshape(8, 128, 4).transpose(1, 0, 2).reshape(128, 32)).astype(np.float32)
    w1r = np.ascontiguousarray(
        w1.reshape(NE, 8, 128, NIC, 128).transpose(0, 3, 2, 1, 4)
        .reshape(NE, NIC, 128, 1024)).astype(np.float32)
    b1c = np.ascontiguousarray(
        b1.reshape(NE, NIC, 128).transpose(2, 0, 1).reshape(128, NE * NIC)
    ).astype(np.float32)
    gamt = np.ascontiguousarray(np.broadcast_to(gam[None, :], (128, D))).astype(np.float32)
    bett = np.ascontiguousarray(np.broadcast_to(bet[None, :], (128, D))).astype(np.float32)
    bor = np.ascontiguousarray(bo[None, :]).astype(np.float32)
    wor2 = np.ascontiguousarray(
        wo.reshape(NIC, 128, NCH, 128).transpose(2, 1, 0, 3)).astype(np.float32)

    in_maps = []
    for c in range(8):
        xc = x[2 * c:2 * c + 2]
        xrows = np.ascontiguousarray(xc.reshape(T, D))
        xT = np.ascontiguousarray(xc.reshape(T, D).T)
        in_maps.append({
            "xT": xT, "xrows": xrows,
            "qwp": qwp, "kwp": kwp, "vwp": vwp, "oww": np.ascontiguousarray(ow),
            "wrc": wrc, "w1r": w1r, "b1c": b1c,
            "wor2": wor2, "bor": bor, "gamt": gamt, "bett": bett,
        })
    return in_maps


def kernel(**inputs):
    in_maps = _stage_inputs(inputs)
    nc = build(stage="full", debug=False)
    res = run_bass_kernel_spmd(nc, in_maps, core_ids=list(range(8)))
    out = np.concatenate([np.asarray(r["out"]) for r in res.results], axis=0)
    return out.reshape(16, 2048, 1024).astype(np.float32)


# version 15
# speedup vs baseline: 1.1020x; 1.0612x over previous
import numpy as np

import concourse.bass as bass
import concourse.mybir as mybir
from concourse.bass_utils import run_bass_kernel_spmd

F32 = mybir.dt.float32
F16 = mybir.dt.float16
AF = mybir.ActivationFunctionType
ALU = mybir.AluOpType

D = 1024
S = 2048
NSEQ = 2
T = NSEQ * S
NH = 16
NCH = 8
TG = 512
NTG = S // TG
NE = 4
DFF = 4096
NIC = DFF // 128
SCALE = 1.0 / 32.0
MASKVAL = -30000.0
LN_EPS_OUT = 1e-5
LN_EPS_IN = 1e-12
CCAP = 640
GR = 320
SLOTS = NE * CCAP
TRASH = NSEQ * SLOTS
NXG = TRASH + 128

CAP = 205312


def _round32(x):
    return (x + 31) // 32 * 32


class Arena:

    _ctr = [0]

    def __init__(self, nc, cap):
        self.nc = nc
        self.cap = cap
        self.free = [(0, cap)]
        self.live = {}
        Arena._ctr[0] += 1
        self.backing = nc.alloc_sbuf_tensor(f"arena{Arena._ctr[0]}", [128, cap],
                                            mybir.dt.uint8)
        self.base_addr = nc.lookup_mloc(self.backing).addr

    def alloc(self, name, shape, dtype):
        per_part = int(np.prod(shape[1:])) * mybir.dt.size(dtype)
        need = _round32(per_part)
        for idx, (off, sz) in enumerate(self.free):
            if sz >= need:
                self.free[idx] = (off + need, sz - need)
                if self.free[idx][1] == 0:
                    self.free.pop(idx)
                h = self.nc.alloc_sbuf_tensor_at(name, list(shape), dtype,
                                                 offset=self.base_addr + off)
                self.live[name] = (off, need)
                return h
        raise MemoryError(f"sbuf arena: no room for {name} ({need}B); free={self.free}")

    def freebuf(self, *names):
        for name in names:
            off, sz = self.live.pop(name)
            self.free.append((off, sz))
        self.free.sort()
        out = []
        for off, sz in self.free:
            if out and out[-1][0] + out[-1][1] == off:
                out[-1] = (out[-1][0], out[-1][1] + sz)
            else:
                out.append((off, sz))
        self.free = [tuple(x) for x in out]


class Walk:

    def __init__(self, me, handle, sems):
        self.me = me
        self.E = handle
        self.sems = sems
        self.cnt = {k: 0 for k in sems}
        self.last_wait = {}
        self.marks = {}

    def wait(self, eng, *pairs):
        if self.me == eng:
            for s, v in pairs:
                if v <= 0 or self.last_wait.get(s, -1) >= v:
                    continue
                self.E.wait_ge(self.sems[s], v)
                self.last_wait[s] = v

    def do(self, eng, fn, inc=None, wait=None):
        if self.me == eng:
            if wait:
                self.wait(eng, *wait)
            inst = fn(self.E)
            if inc:
                inst.then_inc(self.sems[inc[0]], inc[1])
        if inc:
            self.cnt[inc[0]] += inc[1]

    def drain(self, eng):
        if self.me == eng:
            self.E.drain()

    def dma(self, eng, out, in_, sem, wait=None):
        self.do(eng, lambda E: E.dma_start(out=out, in_=in_), inc=(sem, 16), wait=wait)

    def dmal(self, eng, out, in_, wait=None):
        pool = self.pools[eng]
        lane = pool[self.rr[eng] % len(pool)]
        self.rr[eng] += 1
        ws = [(lane, self.cnt[lane])]
        if wait:
            ws += list(wait)
        self.do(eng, lambda E: E.dma_start(out=out, in_=in_), inc=(lane, 16), wait=ws)
        return (lane, self.cnt[lane])

    def waitpool(self, eng, pool_of):
        for lane in self.pools[pool_of]:
            self.wait(eng, (lane, self.cnt[lane]))

    def mm(self, out, lhsT, rhs, start, stop, tp=None, wait=None, inc=None):
        self.do("t",
                lambda E: E.matmul(out, lhsT, rhs, start=start, stop=stop,
                                   tile_position=tp),
                inc=inc, wait=wait)

    def mark(self, name, sem):
        self.marks[name] = [(sem, self.cnt[sem])]

    def wmark(self, eng, name):
        for s, v in self.marks.get(name, ()):
            self.wait(eng, (s, v))

    def barrier(self):
        snap = [(s, v) for s, v in self.cnt.items() if v > 0]
        for eng in ("q", "g", "t", "s", "v"):
            self.wait(eng, *snap)


def build(stage="full", debug=False):
    nc = bass.Bass("TRN2", target_bir_lowering=False)

    xT_d = nc.dram_tensor("xT", [D, T], F16, kind="ExternalInput").ap()
    xrows_d = nc.dram_tensor("xrows", [T, D], F16, kind="ExternalInput").ap()
    qwp_d = nc.dram_tensor("qwp", [D, D], F16, kind="ExternalInput").ap()
    kwp_d = nc.dram_tensor("kwp", [D, D], F16, kind="ExternalInput").ap()
    vwp_d = nc.dram_tensor("vwp", [D, 1040], F16, kind="ExternalInput").ap()
    ow_d = nc.dram_tensor("oww", [D, D], F16, kind="ExternalInput").ap()
    wr_d = nc.dram_tensor("wrc", [128, 32], F32, kind="ExternalInput").ap()
    w1_d = nc.dram_tensor("w1r", [NE, NIC, 128, 1024], F32, kind="ExternalInput").ap()
    b1_d = nc.dram_tensor("b1c", [128, 128], F32, kind="ExternalInput").ap()
    wo_d = nc.dram_tensor("wor2", [NCH, 128, NIC, 128], F32, kind="ExternalInput").ap()
    bo_d = nc.dram_tensor("bor", [1, D], F32, kind="ExternalInput").ap()
    gam_d = nc.dram_tensor("gamt", [128, D], F32, kind="ExternalInput").ap()
    bet_d = nc.dram_tensor("bett", [128, D], F32, kind="ExternalInput").ap()
    out_d = nc.dram_tensor("out", [T, D], F32, kind="ExternalOutput").ap()

    x1f_d = nc.dram_tensor("x1f", [T, D], F32, kind="Internal").ap()
    ghi_d = nc.dram_tensor("ghi", [NIC, 128, NSEQ * SLOTS], F16, kind="Internal").ap()
    glo_d = nc.dram_tensor("glo", [NIC, 128, NSEQ * SLOTS], F16, kind="Internal").ap()
    xg_d = nc.dram_tensor("xg", [NXG, D], F32, kind="Internal").ap()
    yg_d = nc.dram_tensor("yg", [NXG, D], F32, kind="Internal").ap()
    slot_d = nc.dram_tensor("slotd", [NSEQ * NTG, TG], F32, kind="Internal").ap()

    dbg = {}
    if debug and stage == "qkv":
        dbg["qT"] = nc.dram_tensor("dbg_qT", [D, S], F16, kind="ExternalOutput").ap()
        dbg["kT"] = nc.dram_tensor("dbg_kT", [D, S], F16, kind="ExternalOutput").ap()
        dbg["vp"] = nc.dram_tensor("dbg_vp", [128, 16 * 1040], F16, kind="ExternalOutput").ap()
    if debug and stage == "attn":
        dbg["at_hi"] = nc.dram_tensor("dbg_athi", [D, TG], F16, kind="ExternalOutput").ap()
        dbg["at_lo"] = nc.dram_tensor("dbg_atlo", [D, TG], F16, kind="ExternalOutput").ap()
    if debug and stage == "ln1":
        dbg["x1"] = nc.dram_tensor("dbg_x1", [T, D], F32, kind="ExternalOutput").ap()
    if debug and stage == "slots":
        dbg["idx"] = nc.dram_tensor("dbg_idx", [128, 32], F32, kind="ExternalOutput").ap()

    pos = np.arange(S, dtype=np.float64)
    inv = 10000.0 ** (-np.arange(32, dtype=np.float64) / 32.0)
    freq = inv[:, None] * pos[None, :]
    cc_d = nc.inline_tensor(np.tile(np.cos(freq), (4, 1)).astype(np.float32), "cc_c")
    ss_d = nc.inline_tensor(np.tile(np.sin(freq), (4, 1)).astype(np.float32), "ss_c")
    masks_np = np.zeros((4, 128, TG), np.float16)
    for di in range(4):
        k_idx = np.arange(128)[:, None] + di * 128
        q_idx = np.arange(TG)[None, :]
        masks_np[di][k_idx > q_idx] = np.float16(MASKVAL)
    mask_d = [nc.inline_tensor(masks_np[i], f"mask{i}_c") for i in range(4)]
    id16_d = nc.inline_tensor(np.eye(128, dtype=np.float16), "id16_c")
    id32_d = nc.inline_tensor(np.eye(128, dtype=np.float32), "id32_c")
    sel64_np = np.zeros((128, 128), np.float32)
    for _r in (0, 32, 64, 96):
        sel64_np[_r, :64] = 1.0
    sel64_d = nc.inline_tensor(sel64_np, "sel64_c")
    sel64b_np = np.zeros((1, 128), np.float32)
    sel64b_np[0, 64:] = 1.0
    sel64b_d = nc.inline_tensor(sel64b_np, "sel64b_c")
    ones128_d = nc.inline_tensor(np.ones((1, 128), np.float16), "ones128_c")
    ones512_d = nc.inline_tensor(np.ones((1, 512), np.float32), "ones512_c")
    trrow_np = (TRASH + (np.arange(512) % 128)).astype(np.float32)[None, :]
    trrow_d = nc.inline_tensor(trrow_np, "trrow_c")

    AP_ = Arena(nc, 40448)
    AR1 = Arena(nc, 63488)
    AR2 = Arena(nc, 98816)

    class Multi:

        def alloc(self, name, shape, dtype):
            try:
                return AR1.alloc(name, shape, dtype)
            except MemoryError:
                return AR2.alloc(name, shape, dtype)

    AC = Multi()

    id16_s = AP_.alloc("id16", [128, 128], F16)
    id32_s = AP_.alloc("id32", [128, 128], F32)
    sel64_s = AP_.alloc("sel64", [128, 128], F32)
    sel64b_s = AP_.alloc("sel64b", [1, 128], F32)
    ones128_s = AP_.alloc("ones128", [1, 128], F16)
    wr32_s = AP_.alloc("wr32", [128, 32], F32)
    wrh_s = AP_.alloc("wrh", [128, 32], F16)
    wrl_s = AP_.alloc("wrl", [128, 32], F16)
    b1_s = AP_.alloc("b1", [128, 128], F32)
    slab32_s = [AP_.alloc(f"slab32_{i}", [128, 1024], F32) for i in range(2)]
    xrow_s = [AP_.alloc(f"xrow{i}", [128, D], F16) for i in range(4)]
    z_s = [AP_.alloc(f"z{i}", [128, D], F32) for i in range(2)]
    sqd_s = AP_.alloc("sqd", [128, D], F32)
    vec_s = AP_.alloc("vecs", [128, 32], F32)
    x1t_s = [AP_.alloc(f"x1t{i}", [128, D], F32) for i in range(2)]
    eps5_s = AP_.alloc("eps5", [128, 1], F32)
    eps12_s = AP_.alloc("eps12", [128, 1], F32)
    idxf_s = AP_.alloc("idxf", [128, 32], F32)
    idxi_s = AP_.alloc("idxi", [128, 32], mybir.dt.int32)

    xts_s = [AR1.alloc(f"xts{c}", [128, S], F16) for c in range(NCH)]
    wch_s = [AR1.alloc(f"wch{c}", [128, 1040], F16) for c in range(NCH)]
    ccs_s = AR1.alloc("ccs", [128, TG], F32)
    sss_s = AR1.alloc("sss", [128, TG], F32)
    rp_s = [AR1.alloc(f"rp{i}", [128, TG], F32) for i in range(4)]
    AR1.freebuf(*([f"xts{c}" for c in range(NCH)] + [f"wch{c}" for c in range(NCH)]
                  + ["ccs", "sss"] + [f"rp{i}" for i in range(4)]))
    e32_s = [AR1.alloc(f"e32_{i}", [128, 1024], F32) for i in range(3)]
    _o0 = AP_.base_addr + AP_.live["slab32_0"][0]
    _o1 = AP_.base_addr + AP_.live["slab32_1"][0]
    ehi_s = [AR1.alloc(f"ehi_{i}", [128, 1024], F16) for i in range(2)] + [
        nc.alloc_sbuf_tensor_at("ehi_2", [128, 1024], F16, offset=_o0),
        nc.alloc_sbuf_tensor_at("ehi_3", [128, 1024], F16, offset=_o0 + 2048)]
    elo_s = [AR1.alloc(f"elo_{i}", [128, 1024], F16) for i in range(2)] + [
        nc.alloc_sbuf_tensor_at("elo_2", [128, 1024], F16, offset=_o1),
        nc.alloc_sbuf_tensor_at("elo_3", [128, 1024], F16, offset=_o1 + 2048)]
    rec_s = AR1.alloc("rec", [128, 1024], F32)

    def rec_ap(hh):
        if hh < 3:
            return rec_s[32 * hh:32 * hh + 1, 0:512]
        return rec_s[0:1, 512:1024]
    bsb_s = AR1.alloc("bsb", [64, 2048], F32)
    t32_s = AR1.alloc("t32", [128, TG], F32)
    athi_s = [AR1.alloc(f"athi{c}", [128, TG], F16) for c in range(NCH)]
    atlo_s = [AR1.alloc(f"atlo{c}", [128, TG], F16) for c in range(NCH)]
    owh_s = [AR1.alloc(f"owh{i}", [128, D], F16) for i in range(4)]
    msk_s = [AR1.alloc(f"msk{i}", [128, TG], F16) for i in range(4)]
    AR1.freebuf(*(["e32_0", "e32_1", "e32_2",
                   "ehi_0", "ehi_1", "elo_0", "elo_1",
                   "rec", "bsb", "t32"]
                  + [f"athi{c}" for c in range(NCH)] + [f"atlo{c}" for c in range(NCH)]
                  + [f"owh{i}" for i in range(4)] + [f"msk{i}" for i in range(4)]))
    lgs_s = AR1.alloc("lgs", [4, TG], F32)
    lrow_s = AR1.alloc("lrow", [1, NE * TG], F32)
    rtmp_s = AR1.alloc("rtmp", [1, 7 * TG], F32)
    onesr_s = AR1.alloc("onesr", [1, TG], F32)
    trrow_s = AR1.alloc("trrow", [1, TG], F32)
    carr_s = AR1.alloc("carr", [1, 32], F32)
    ovi_s = AR1.alloc("ovi", [1, TG], mybir.dt.int32)
    slotr_s = AR1.alloc("slotr", [1, TG], F32)
    AR1.freebuf("lgs", "lrow", "rtmp", "onesr", "trrow", "carr", "ovi", "slotr")
    slabh_s = [[AR1.alloc(f"slabh{i}_{e}", [128, 1024], F16) for e in range(NE)]
               for i in range(2)]
    slabl_s = [[AR1.alloc(f"slabl{i}_{e}", [128, 1024], F16) for e in range(NE)]
               for i in range(2)]
    gf_s = [AR1.alloc(f"gfull{i}", [128, GR], F32) for i in range(2)]
    ghi_sb = [AR1.alloc(f"ghisb{i}", [128, GR], F16) for i in range(2)]
    glo_sb = [AR1.alloc(f"glosb{i}", [128, GR], F16) for i in range(2)]
    AR1.freebuf(*[f"slabh{i}_{e}" for i in range(2) for e in range(NE)],
                *[f"slabl{i}_{e}" for i in range(2) for e in range(NE)],
                "gfull0", "gfull1", "ghisb0", "ghisb1", "glosb0", "glosb1")

    qT_s = [AR2.alloc(f"qT{c}", [128, S], F16) for c in range(NCH)]
    kT_s = [AR2.alloc(f"kT{c}", [128, S], F16) for c in range(NCH)]
    vp_s = AR2.alloc("vp", [128, 16 * 1040], F16)
    AR2.freebuf(*([f"qT{c}" for c in range(NCH)] + [f"kT{c}" for c in range(NCH)]
                  + ["vp"]))
    xgTh_s = [AR2.alloc(f"xgTh{c}", [128, SLOTS], F16) for c in range(NCH)]
    xgTl_s = [AR2.alloc(f"xgTl{c}", [128, SLOTS], F16) for c in range(NCH)]
    x1Tt_s = [AR2.alloc(f"x1Tt{i}", [128, 1024], F16) for i in range(4)]
    AR2.freebuf(*([f"xgTh{c}" for c in range(NCH)] + [f"xgTl{c}" for c in range(NCH)]
                  + [f"x1Tt{i}" for i in range(4)]))
    WB = 512
    gslh_s = [AC.alloc(f"gslh{ic}", [128, WB], F16) for ic in range(NIC)]
    gsll_s = [AC.alloc(f"gsll{ic}", [128, WB], F16) for ic in range(NIC)]
    wjh_s = [AC.alloc(f"wjh{i}", [128, NIC * 128], F16) for i in range(2)]
    wjl_s = [AC.alloc(f"wjl{i}", [128, NIC * 128], F16) for i in range(2)]
    wsp_h = AC.alloc("wsph", [128, 1024], F16)
    wsp_l = AC.alloc("wspl", [128, 1024], F16)
    ysb_s = AC.alloc("ysb", [128, WB], F32)
    yrow_s = [AC.alloc(f"yrow{i}", [128, D], F32) for i in range(4)]
    x1r_s = [AC.alloc(f"x1r{i}", [128, D], F32) for i in range(2)]
    outt_s = [AC.alloc(f"outt{i}", [128, D], F32) for i in range(2)]
    gam_s = AC.alloc("gams", [128, D], F32)
    bet_s = AC.alloc("bets", [128, D], F32)
    bo_s = AC.alloc("bo", [1, D], F32)
    ones512_s = AC.alloc("ones512", [1, 512], F32)

    PA = nc.alloc_psum_tensor("PA", [128, 2048], F32)
    PB = nc.alloc_psum_tensor("PB", [128, 1024], F32)
    PC = nc.alloc_psum_tensor("PC", [128, 1024], F32)

    QL = [f"qa{i}" for i in range(6)]
    GL = [f"gb{i}" for i in range(4)]
    sem_handles = {n: nc.alloc_semaphore(name=f"sem_{n}")
                   for n in tuple(QL) + tuple(GL) + ("t", "s", "v")}

    def walk(me, E):
        w = Walk(me, E, sem_handles)
        w.pools = {"q": QL, "g": GL}
        w.rr = {"q": 0, "g": 0}
        C = w.cnt
        mark, wmark = w.mark, w.wmark

        def qsn():
            return [(l, C[l]) for l in QL]

        def gsn():
            return [(l, C[l]) for l in GL]

        for dst, srcp in ((id16_s, id16_d.ap()), (id32_s, id32_d.ap()),
                          (sel64_s, sel64_d.ap()), (sel64b_s, sel64b_d.ap()),
                          (ones128_s, ones128_d.ap()),
                          (wr32_s, wr_d), (b1_s, b1_d)):
            w.dmal("q", dst[:], srcp)
        const_tok = qsn()
        w.do("v", lambda E: E.memset(eps5_s[:], LN_EPS_OUT))
        w.do("v", lambda E: E.memset(eps12_s[:], LN_EPS_IN))
        w.drain("v")
        w.do("s", lambda E: E.activation(wrh_s[:], wr32_s[:], AF.Copy),
             inc=("s", 1), wait=const_tok)
        sC = C["s"]
        w.do("v", lambda E: E.tensor_tensor(wrl_s[:], wr32_s[:], wrh_s[:], ALU.subtract),
             inc=("v", 1), wait=[("s", sC)])
        w.do("v", lambda E: E.memset(z_s[0][:], 0.0), inc=("v", 1))
        w.drain("v")
        zf_v = C["v"]
        for r0 in range(0, NXG, 128):
            w.dmal("g", xg_d[r0:r0 + 128, :], z_s[0][:], wait=[("v", zf_v)])
            w.dmal("g", yg_d[r0:r0 + 128, :], z_s[0][:])
        zf_tok = gsn()
        w.marks["z0"] = list(zf_tok)

        def qkv_phase(s_idx):
            tok0 = s_idx * S
            for c in range(NCH):
                wmark("q", f"xts{c}")
                w.dmal("q", xts_s[c][:], xT_d[c * 128:(c + 1) * 128, tok0:tok0 + S])
            x_tok = qsn()

            for wsrc, dstT in ((qwp_d, qT_s), (kwp_d, kT_s)):
                wmark("q", "wch")
                for dc in range(NCH):
                    w.dmal("q", wch_s[dc][:, 0:1024], wsrc[dc * 128:(dc + 1) * 128, :])
                w_tok = qsn()
                for tg in range(NTG):
                    wmark("q", "ccs")
                    ct1 = w.dmal("q", ccs_s[:], cc_d.ap()[:, tg * TG:(tg + 1) * TG])
                    ct2 = w.dmal("q", sss_s[:], ss_d.ap()[:, tg * TG:(tg + 1) * TG])
                    for qd in range(4):
                        b = (qd % 2) * 1024
                        pe = PA[:, b:b + 512]
                        po = PA[:, b + 512:b + 1024]
                        wmark("t", f"pab{qd % 2}")
                        for dc in range(NCH):
                            rhs = xts_s[dc][:, tg * TG:(tg + 1) * TG]
                            w.mm(pe, wch_s[dc][:, (2 * qd) * 128:(2 * qd + 1) * 128], rhs,
                                 start=(dc == 0), stop=(dc == NCH - 1),
                                 wait=w_tok + x_tok)
                            w.mm(po, wch_s[dc][:, (2 * qd + 1) * 128:(2 * qd + 2) * 128],
                                 rhs, start=(dc == 0), stop=(dc == NCH - 1),
                                 inc=("t", 1) if dc == NCH - 1 else None)
                        tC = C["t"]
                        wmark("v", "rp")
                        w.do("v", lambda E, pe=pe: E.tensor_tensor(
                            rp_s[0][:], pe, ccs_s[:], ALU.mult),
                             wait=[("t", tC), ct1, ct2])
                        w.do("v", lambda E, po=po: E.tensor_tensor(
                            rp_s[1][:], po, sss_s[:], ALU.mult))
                        w.do("v", lambda E, pe=pe: E.tensor_tensor(
                            rp_s[2][:], pe, sss_s[:], ALU.mult))
                        w.do("v", lambda E, po=po: E.tensor_tensor(
                            rp_s[3][:], po, ccs_s[:], ALU.mult), inc=("v", 1))
                        mark(f"pab{qd % 2}", "v")
                        if qd == 3:
                            mark("ccs", "v")
                        w.drain("v")
                        de = dstT[2 * qd][:, tg * TG:(tg + 1) * TG]
                        do_ = dstT[2 * qd + 1][:, tg * TG:(tg + 1) * TG]
                        w.do("v", lambda E, de=de: E.tensor_tensor(
                            de, rp_s[0][:], rp_s[1][:], ALU.subtract))
                        w.do("v", lambda E, do_=do_: E.tensor_tensor(
                            do_, rp_s[2][:], rp_s[3][:], ALU.add), inc=("v", 1))
                        mark("rp", "v")
                        w.drain("v")
                mark("wch", "t")

            wmark("q", "wch")
            for dc in range(NCH):
                w.dmal("q", wch_s[dc][:, 0:1040], vwp_d[dc * 128:(dc + 1) * 128, :])
            w_tok = qsn()
            pv = 0
            for tt in range(16):
                for j0, jn in ((0, 512), (512, 512), (1024, 16)):
                    ps = PB[:, (pv % 2) * 512:(pv % 2) * 512 + 512]
                    wmark("t", f"pbb{pv % 2}")
                    for dc in range(NCH):
                        w.mm(ps[:, 0:jn], xts_s[dc][:, tt * 128:(tt + 1) * 128],
                             wch_s[dc][:, j0:j0 + jn],
                             start=(dc == 0), stop=(dc == NCH - 1),
                             wait=w_tok + x_tok,
                             inc=("t", 1) if dc == NCH - 1 else None)
                    tC = C["t"]
                    dst = vp_s[:, tt * 1040 + j0: tt * 1040 + j0 + jn]
                    w.do("s", lambda E, dst=dst, ps=ps, jn=jn: E.activation(
                        dst, ps[:, 0:jn], AF.Copy), inc=("s", 1), wait=[("t", tC)])
                    mark(f"pbb{pv % 2}", "s")
                    pv += 1
            sv = C["s"]
            w.do("v", lambda E: E.memset(
                vp_s[:, :].rearrange("p (t c) -> p t c", c=1040)[:, :, 64::65], 1.0),
                 inc=("v", 1), wait=[("s", sv)])
            w.drain("v")
            mark("wch", "t")
            w.barrier()

        def attn_phase(s_idx):
            for i in range(4):
                w.dmal("q", msk_s[i][:], mask_d[i].ap())
            msk_tok = qsn()

            for qg in range(NTG):
                for q in range(4):
                    nkt = 4 * (qg + 1)
                    svtok = {}

                    def emit_scores(kt, q=q, qg=qg, nkt=nkt, svtok=svtok):
                        wmark("t", "bq")
                        wmark("t", "pafV")
                        masked = kt >= qg * 4
                        di = kt - qg * 4
                        toks2 = []
                        for h2 in range(2):
                            wmark("t", f"scq{h2}")
                            for m in (2 * h2, 2 * h2 + 1):
                                dst = PA[:, m * 512:(m + 1) * 512]
                                for half in range(2):
                                    ch = 2 * q + half
                                    last = (half == 1) and not masked
                                    w.mm(dst,
                                         kT_s[ch][32 * m:32 * m + 32,
                                                  kt * 128:(kt + 1) * 128],
                                         qT_s[ch][32 * m:32 * m + 32,
                                                  qg * TG:(qg + 1) * TG],
                                         start=(half == 0), stop=last,
                                         tp=(32 * m, 0), wait=msk_tok,
                                         inc=("t", 1) if (last and m == 2 * h2 + 1)
                                         else None)
                            if masked:
                                for m in (2 * h2, 2 * h2 + 1):
                                    w.mm(PA[:, m * 512:(m + 1) * 512], id16_s[:],
                                         msk_s[di][:], start=False, stop=True,
                                         inc=("t", 1) if m == 2 * h2 + 1 else None)
                            toks2.append(C["t"])
                        svtok[kt] = toks2

                    def emit_expchain(kt, svtok=svtok):
                        tC0, tC1 = svtok[kt]
                        stoks = {}
                        for h2 in range(2):
                            e3 = (2 * kt + h2) % 3
                            wmark("s", f"e32_{e3}")
                            w.do("s", lambda E, h2=h2, e3=e3: E.activation(
                                e32_s[e3][:], PA[:, h2 * 1024:(h2 + 1) * 1024],
                                AF.Exp, scale=SCALE),
                                 inc=("s", 1), wait=[("t", tC0 if h2 == 0 else tC1)])
                            mark(f"scq{h2}", "s")
                            stoks[h2] = C["s"]
                        toks = {}
                        for h2 in range(2):
                            e3 = (2 * kt + h2) % 3
                            e4 = (2 * kt + h2) % 4
                            wmark("v", f"ehi_{e4}")
                            w.do("v", lambda E, e3=e3, e4=e4: E.tensor_copy(
                                ehi_s[e4][:], e32_s[e3][:]), inc=("v", 1),
                                 wait=[("s", stoks[h2])])
                            vCc = C["v"]
                            wmark("v", f"elo_{e4}")
                            w.do("v", lambda E, e3=e3, e4=e4: E.tensor_tensor(
                                elo_s[e4][:], e32_s[e3][:], ehi_s[e4][:], ALU.subtract),
                                 inc=("v", 1), wait=[("v", vCc)])
                            mark(f"e32_{e3}", "v")
                            toks[h2] = (vCc, C["v"])
                        svtok[kt] = toks

                    def emit_vmm(kt, q=q, nkt=nkt, svtok=svtok):
                        toks = svtok[kt]
                        if kt == 0:
                            wmark("t", "atb")
                        for lo in range(2):
                            for hh in range(4):
                                h2 = hh // 2
                                e4 = (2 * kt + h2) % 4
                                h = 4 * q + hh
                                PX = PB if hh < 2 else PC
                                pat = PX[0:65, (hh % 2) * 512:(hh % 2) * 512 + 512]
                                lhs = vp_s[:, kt * 1040 + 65 * h: kt * 1040 + 65 * h + 65]
                                src = (elo_s if lo else ehi_s)[e4]
                                w.mm(pat, lhs,
                                     src[:, (hh % 2) * 512:(hh % 2) * 512 + 512],
                                     start=(kt == 0 and lo == 0),
                                     stop=(kt == nkt - 1 and lo == 1),
                                     wait=[("v", toks[h2][lo])] if hh % 2 == 0 else None,
                                     inc=("t", 1) if hh == 3 else None)
                            nm = "elo" if lo else "ehi"
                            mark(f"{nm}_{(2 * kt) % 4}", "t")
                            mark(f"{nm}_{(2 * kt + 1) % 4}", "t")

                    for kt in range(nkt):
                        emit_scores(kt)
                        if kt > 0:
                            emit_vmm(kt - 1)
                        emit_expchain(kt)
                    emit_vmm(nkt - 1)
                    tC = C["t"]

                    wmark("s", "rec")
                    for hh in range(4):
                        PX = PB if hh < 2 else PC
                        w.do("s", lambda E, hh=hh, PX=PX: E.activation(
                            rec_ap(hh),
                            PX[64:65, (hh % 2) * 512:(hh % 2) * 512 + 512], AF.Copy),
                             inc=("s", 1) if hh == 3 else None,
                             wait=[("t", tC)])
                    sC = C["s"]
                    for hh in range(4):
                        w.do("v", lambda E, hh=hh: E.reciprocal(
                            rec_ap(hh), rec_ap(hh)),
                             inc=("v", 1) if hh == 3 else None,
                             wait=[("s", sC)] if hh == 0 else None)
                    vC = C["v"]
                    mark("rec", "v")
                    wmark("t", "scq0")
                    wmark("t", "scq1")
                    for hh in range(4):
                        _r = 0 if hh == 3 else 32 * hh
                        w.mm(PA[:, hh * 512:hh * 512 + 512],
                             sel64_s[_r:_r + 1, :],
                             rec_ap(hh), start=True, stop=True,
                             wait=[("v", vC)], inc=("t", 1) if hh == 3 else None)
                    tC2 = C["t"]
                    wmark("s", "bsb")
                    for hh in range(4):
                        w.do("s", lambda E, hh=hh: E.activation(
                            bsb_s[0:64, hh * 512:(hh + 1) * 512],
                            PA[0:64, hh * 512:hh * 512 + 512], AF.Copy),
                             inc=("s", 1) if hh == 3 else None, wait=[("t", tC2)])
                    sC = C["s"]
                    mark("bq", "s")
                    for half in range(2):
                        p = 2 * q + half
                        hh0, hh1 = 2 * half, 2 * half + 1
                        PX = PB if half == 0 else PC
                        wmark("v", "t32")
                        w.do("v", lambda E, PX=PX, hh0=hh0: E.tensor_tensor(
                            t32_s[0:64, :], PX[0:64, 0:512],
                            bsb_s[0:64, hh0 * 512:(hh0 + 1) * 512], ALU.mult),
                             wait=[("s", sC)])
                        w.do("v", lambda E, PX=PX, hh1=hh1: E.tensor_tensor(
                            t32_s[64:128, :], PX[0:64, 512:1024],
                            bsb_s[0:64, hh1 * 512:(hh1 + 1) * 512], ALU.mult),
                             inc=("v", 1))
                        vC = C["v"]
                        if half == 1:
                            mark("bsb", "v")
                            mark("atb", "v")
                        wmark("s", f"athi{p}")
                        w.do("s", lambda E, p=p: E.activation(
                            athi_s[p][:], t32_s[:], AF.Copy), inc=("s", 1),
                             wait=[("v", vC)])
                        sC2 = C["s"]
                        wmark("v", f"atlo{p}")
                        w.do("v", lambda E, p=p: E.tensor_tensor(
                            atlo_s[p][:], t32_s[:], athi_s[p][:], ALU.subtract),
                             inc=("v", 1), wait=[("s", sC2)])
                        mark("t32", "v")

                if stage == "attn" and qg == 0:
                    vv, ss_ = C["v"], C["s"]
                    for c in range(NCH):
                        w.dmal("g", dbg["at_hi"][c * 128:(c + 1) * 128, :], athi_s[c][:],
                               wait=[("v", vv), ("s", ss_)])
                        w.dmal("g", dbg["at_lo"][c * 128:(c + 1) * 128, :], atlo_s[c][:])
                    w.waitpool("g", "g")
                    return True

                athi_v = C["v"]
                athi_sc = C["s"]
                pbank = {0: PA[:, 0:1024], 1: PA[:, 1024:2048],
                         2: PB[:, 0:1024], 3: PC[:, 0:1024]}
                xr_toks = {}
                for tt in range(4):
                    tok0 = s_idx * S + qg * TG + tt * 128
                    wmark("q", f"xrow{tt}")
                    xr_toks[tt] = w.dmal("q", xrow_s[tt][:], xrows_d[tok0:tok0 + 128, :])
                for halfw in range(2):
                    wmark("q", "owh")
                    for cc in range(4):
                        c = 4 * halfw + cc
                        w.dmal("q", owh_s[cc][:], ow_d[c * 128:(c + 1) * 128, :])
                    ow_tok = qsn()
                    for tt in range(4):
                        ps = pbank[tt]
                        if halfw == 0:
                            if tt < 2:
                                wmark("t", "pafV")
                            else:
                                wmark("t", "atb")
                        for cc in range(4):
                            c = 4 * halfw + cc
                            for jg in range(2):
                                w.mm(ps[:, jg * 512:(jg + 1) * 512],
                                     athi_s[c][:, tt * 128:(tt + 1) * 128],
                                     owh_s[cc][:, jg * 512:(jg + 1) * 512],
                                     start=(halfw == 0 and cc == 0), stop=False,
                                     wait=[("v", athi_v), ("s", athi_sc)] + ow_tok)
                                w.mm(ps[:, jg * 512:(jg + 1) * 512],
                                     atlo_s[c][:, tt * 128:(tt + 1) * 128],
                                     owh_s[cc][:, jg * 512:(jg + 1) * 512],
                                     start=False, stop=(halfw == 1 and cc == 3),
                                     inc=("t", 1) if (tt == 3 and cc == 3
                                                      and jg == 1) else None)
                    mark("owh", "t")
                opt_t = C["t"]
                for tt in range(4):
                    ti = tt % 2
                    tok0 = s_idx * S + qg * TG + tt * 128
                    ps = pbank[tt]
                    wmark("v", f"z{ti}")
                    w.do("v", lambda E, ps=ps, ti=ti, tt=tt:
                         E.scalar_tensor_tensor(
                             z_s[ti][:], ps[:, :], 1.0, xrow_s[tt][:],
                             ALU.mult, ALU.add, accum_out=vec_s[:, 8:9]),
                         inc=("v", 1), wait=[("t", opt_t), xr_toks[tt]])
                    if tt == 1:
                        mark("pafV", "v")
                    if tt == 3:
                        mark("atb", "v")
                    mark(f"xrow{tt}", "v")
                    vC = C["v"]
                    wmark("s", "sqd")
                    w.do("s", lambda E, ti=ti: E.activation(
                        sqd_s[:], z_s[ti][:], AF.Square, accum_out=vec_s[:, 10:11]),
                         wait=[("v", vC)])
                    w.do("s", lambda E: E.mul(vec_s[:, 1:2], vec_s[:, 8:9], 1.0 / D))
                    w.drain("s")
                    w.do("s", lambda E: E.square(vec_s[:, 2:3], vec_s[:, 1:2]),
                         inc=("s", 1))
                    sC = C["s"]
                    mark("sqd", "s")
                    w.do("v", lambda E: E.scalar_tensor_tensor(
                        vec_s[:, 3:4], vec_s[:, 10:11], 1.0 / D, vec_s[:, 2:3],
                        ALU.mult, ALU.subtract), inc=("v", 1), wait=[("s", sC)])
                    vC = C["v"]
                    w.do("s", lambda E: E.activation(
                        vec_s[:, 4:5], vec_s[:, 3:4], AF.Sqrt, bias=eps5_s[:]),
                         inc=("s", 1), wait=[("v", vC)])
                    sC = C["s"]
                    w.drain("v")
                    w.do("v", lambda E: E.reciprocal(vec_s[:, 5:6], vec_s[:, 4:5]),
                         wait=[("s", sC)])
                    w.drain("v")
                    wmark("v", f"x1t{ti}")
                    w.do("v", lambda E, ti=ti: E.tensor_scalar(
                        x1t_s[ti][:], z_s[ti][:], vec_s[:, 1:2], vec_s[:, 5:6],
                        ALU.subtract, ALU.mult), inc=("v", 1))
                    vC = C["v"]
                    mark(f"z{ti}", "v")
                    x1tok = [w.dmal("g", x1f_d[tok0:tok0 + 128, :], x1t_s[ti][:],
                                    wait=[("v", vC)])]
                    if stage == "ln1":
                        x1tok.append(w.dmal("g", dbg["x1"][tok0:tok0 + 128, :],
                                            x1t_s[ti][:]))
                    w.marks[f"x1t{ti}"] = x1tok
            w.barrier()
            return False

        def slots_phase(s_idx):
            tok0 = s_idx * S
            w.do("v", lambda E: (E.memset(carr_s[:], 0.0), E.memset(onesr_s[:], 1.0),
                                 E.drain())[-1], inc=("v", 1))
            tr_tok = w.dmal("q", trrow_s[:], trrow_d.ap())
            for tg in range(NTG):
                for tl4 in range(4):
                    tile = tg * 4 + tl4
                    ti = tile % 2
                    wmark("q", f"x1t{ti}")
                    xt_tok = w.dmal("q", x1t_s[ti][:],
                                    x1f_d[tok0 + tile * 128: tok0 + (tile + 1) * 128, :])
                    tb = 2 * (tile % 2)
                    for c in range(NCH):
                        sl = PA[:, (c % 4) * 512:(c % 4) * 512 + 128]
                        wmark("t", f"pas{c % 4}")
                        w.do("t", lambda E, sl=sl, ti=ti, c=c: E.transpose(
                            sl, x1t_s[ti][:, c * 128:(c + 1) * 128], id32_s[:]),
                             inc=("t", 1), wait=[xt_tok])
                        tC = C["t"]
                        wmark("s", f"x1Tt{tb}")
                        w.do("s", lambda E, c=c, sl=sl, tb=tb: E.activation(
                            x1Tt_s[tb][:, c * 128:(c + 1) * 128], sl, AF.Copy),
                             inc=("s", 1), wait=[("t", tC)])
                        sC = C["s"]
                        wmark("v", f"x1Tt{tb + 1}")
                        w.do("v", lambda E, c=c, sl=sl, tb=tb: E.tensor_tensor(
                            x1Tt_s[tb + 1][:, c * 128:(c + 1) * 128], sl,
                            x1Tt_s[tb][:, c * 128:(c + 1) * 128], ALU.subtract),
                             inc=("v", 1), wait=[("s", sC)])
                        mark(f"pas{c % 4}", "v")
                    tile_s, tile_v = C["s"], C["v"]
                    if tl4 == 0:
                        wmark("t", "lgbank")
                    first = True
                    for dc in range(NCH):
                        lh = wrh_s[:, dc * 4:(dc + 1) * 4]
                        ll = wrl_s[:, dc * 4:(dc + 1) * 4]
                        xh = x1Tt_s[tb][:, dc * 128:(dc + 1) * 128]
                        xl = x1Tt_s[tb + 1][:, dc * 128:(dc + 1) * 128]
                        dstp = PB[0:4, tl4 * 128:(tl4 + 1) * 128]
                        w.mm(dstp, lh, xh, start=first, stop=False,
                             wait=[("s", tile_s), ("v", tile_v)])
                        first = False
                        w.mm(dstp, lh, xl, start=False, stop=False)
                        w.mm(dstp, ll, xh, start=False, stop=(dc == NCH - 1),
                             inc=("t", 1) if dc == NCH - 1 else None)
                    mark(f"x1Tt{tb}", "t")
                    mark(f"x1Tt{tb + 1}", "t")
                    mark(f"x1t{ti}", "t")
                tC = C["t"]
                wmark("v", "lgs")
                w.do("v", lambda E: E.tensor_copy(lgs_s[:], PB[0:4, 0:512]),
                     inc=("v", 1), wait=[("t", tC)])
                vC = C["v"]
                mark("lgbank", "v")
                wmark("q", "lrow")
                lr_tok = w.dmal("q", lrow_s[0:1, :].rearrange("a (p f) -> a p f", p=4),
                                lgs_s[:], wait=[("v", vC)])
                w.marks["lgs"] = [lr_tok]

                def rowops(E, tg=tg, s_idx=s_idx):
                    L, R = lrow_s, rtmp_s
                    CSUM = R[:, 4 * TG:5 * TG]
                    SLV = R[:, 5 * TG:6 * TG]
                    OV = R[:, 6 * TG:7 * TG]
                    E.tensor_tensor(R[:, 0:TG], L[:, 0:TG], L[:, TG:2 * TG], ALU.max)
                    E.tensor_tensor(R[:, TG:2 * TG], L[:, 2 * TG:3 * TG],
                                    L[:, 3 * TG:4 * TG], ALU.max)
                    E.drain()
                    E.tensor_tensor(R[:, 2 * TG:3 * TG], R[:, 0:TG], R[:, TG:2 * TG],
                                    ALU.max)
                    E.drain()
                    E.memset(R[:, 0:TG], 1.0)
                    E.memset(slotr_s[:], 0.0)
                    E.drain()
                    for e in range(NE):
                        ge = L[:, e * TG:(e + 1) * TG]
                        E.tensor_tensor(ge, ge, R[:, 2 * TG:3 * TG], ALU.is_ge)
                    E.drain()
                    for e in range(NE):
                        ge = L[:, e * TG:(e + 1) * TG]
                        me = R[:, 3 * TG:4 * TG]
                        E.tensor_tensor(me, ge, R[:, 0:TG], ALU.mult)
                        E.drain()
                        E.tensor_tensor_scan(CSUM, onesr_s[:], me,
                                             carr_s[:, e:e + 1], ALU.mult, ALU.add)
                        E.drain()
                        E.tensor_copy(carr_s[:, e:e + 1], CSUM[:, TG - 1:TG])
                        E.tensor_single_scalar(
                            SLV, CSUM, float(s_idx * SLOTS + e * CCAP - 1), ALU.add)
                        E.tensor_single_scalar(ovi_s[:], CSUM, float(CCAP), ALU.is_gt)
                        E.drain()
                        E.copy_predicated(SLV, ovi_s[:], trrow_s[:])
                        E.drain()
                        E.tensor_tensor(SLV, SLV, me, ALU.mult)
                        E.drain()
                        E.tensor_tensor(slotr_s[:], slotr_s[:], SLV, ALU.add)
                        E.drain()
                        if e < NE - 1:
                            E.tensor_tensor(R[:, 0:TG], R[:, 0:TG], me, ALU.subtract)
                            E.drain()
                    return E.drain()
                wmark("v", "slotr")
                w.do("v", rowops, inc=("v", 1), wait=[lr_tok, tr_tok])
                mark("lrow", "v")
                sr_v = C["v"]
                cb = s_idx * 16 + tg * 4
                row = s_idx * NTG + tg
                st_tok = w.dmal("g", slot_d[row:row + 1, :], slotr_s[:],
                                wait=[("v", sr_v)])
                w.marks["slotr"] = [st_tok]
                w.do("q", lambda E, cb=cb, row=row: E.dma_start(
                    out=idxf_s[:, cb:cb + 4],
                    in_=slot_d[row:row + 1, :].rearrange("a (c p) -> (a p) c", p=128),
                    allow_slow_non_contiguous=True),
                     inc=(QL[0], 16), wait=[(QL[0], C[QL[0]]), st_tok])
            idx_tok = (QL[0], C[QL[0]])
            w.do("v", lambda E, s_idx=s_idx: E.tensor_copy(
                idxi_s[:, s_idx * 16:(s_idx + 1) * 16],
                idxf_s[:, s_idx * 16:(s_idx + 1) * 16]),
                 inc=("v", 1), wait=[idx_tok])
            w.drain("v")
            idx_v = C["v"]

            if stage == "slots" and s_idx == 0:
                w.dmal("g", dbg["idx"][:, 0:16], idxf_s[:, 0:16], wait=[("v", idx_v)])
                w.waitpool("g", "g")
                return True

            from concourse.bass_core_rust import IndirectOffsetOnAxis
            for tile in range(16):
                ti = tile % 2
                wmark("q", f"x1t{ti}")
                xt_tok = w.dmal("q", x1t_s[ti][:],
                                x1f_d[tok0 + tile * 128: tok0 + (tile + 1) * 128, :])
                col = s_idx * 16 + tile
                w.do("g", lambda E, ti=ti, col=col: E.indirect_dma_start(
                    xg_d[:, :],
                    IndirectOffsetOnAxis(idxi_s[:, col:col + 1], 0),
                    x1t_s[ti][:], None),
                     inc=(GL[tile % 4], 16),
                     wait=[(GL[tile % 4], C[GL[tile % 4]]), xt_tok, ("v", idx_v)]
                     + zf_tok)
                w.marks[f"x1t{ti}"] = [(GL[tile % 4], C[GL[tile % 4]])]
            sc_tok = gsn()

            for st in range(SLOTS // 128):
                ti = st % 2
                wmark("q", f"x1t{ti}")
                xt_tok = w.dmal("q", x1t_s[ti][:],
                                xg_d[s_idx * SLOTS + st * 128:
                                     s_idx * SLOTS + (st + 1) * 128, :],
                                wait=sc_tok)
                col0 = st * 128
                for c in range(NCH):
                    sl = PA[:, (c % 4) * 512:(c % 4) * 512 + 128]
                    wmark("t", f"pas{c % 4}")
                    w.do("t", lambda E, sl=sl, ti=ti, c=c: E.transpose(
                        sl, x1t_s[ti][:, c * 128:(c + 1) * 128], id32_s[:]),
                         inc=("t", 1), wait=[xt_tok])
                    tC = C["t"]
                    wmark("s", f"xgTh{c}")
                    w.do("s", lambda E, c=c, sl=sl, col0=col0: E.activation(
                        xgTh_s[c][:, col0:col0 + 128], sl, AF.Copy),
                         inc=("s", 1), wait=[("t", tC)])
                    sC = C["s"]
                    wmark("v", f"xgTl{c}")
                    w.do("v", lambda E, c=c, sl=sl, col0=col0: E.tensor_tensor(
                        xgTl_s[c][:, col0:col0 + 128], sl,
                        xgTh_s[c][:, col0:col0 + 128], ALU.subtract),
                         inc=("v", 1), wait=[("s", sC)])
                    mark(f"pas{c % 4}", "v")
                mark(f"x1t{ti}", "t")
            w.barrier()
            return False

        def ffn_phase(s_idx):
            slot0 = s_idx * SLOTS
            sl_tok = {}

            def emit_w1split(ic):
                ip = ic % 2
                for e in range(NE):
                    sb32 = slab32_s[e % 2]
                    wmark("q", f"slab32_{e % 2}")
                    dtok = w.dmal("q", sb32[:], w1_d[e, ic])
                    wmark("s", f"slabh{ip}_{e}")
                    w.do("s", lambda E, e=e, ip=ip, sb32=sb32: E.activation(
                        slabh_s[ip][e][:], sb32[:], AF.Copy), inc=("s", 1),
                         wait=[dtok])
                    sC = C["s"]
                    wmark("v", f"slabl{ip}_{e}")
                    w.do("v", lambda E, e=e, ip=ip, sb32=sb32: E.tensor_tensor(
                        slabl_s[ip][e][:], sb32[:], slabh_s[ip][e][:], ALU.subtract),
                         inc=("v", 1), wait=[("s", sC)])
                    mark(f"slab32_{e % 2}", "v")
                sl_tok[ic] = (C["v"], C["s"])

            emit_w1split(0)
            for ic in range(NIC):
                ip = ic % 2
                if ic + 1 < NIC:
                    emit_w1split(ic + 1)
                slab_v, slab_s_cnt = sl_tok[ic]
                for e in range(NE):
                    for gr in range(CCAP // GR):
                        c0 = e * CCAP + gr * GR
                        ps = PA[:, (2 * e + gr) % 4 * 512:((2 * e + gr) % 4) * 512 + GR]
                        wmark("t", f"fps{(2 * e + gr) % 4}")
                        for dc in range(NCH):
                            lh = slabh_s[ip][e][:, dc * 128:(dc + 1) * 128]
                            ll = slabl_s[ip][e][:, dc * 128:(dc + 1) * 128]
                            xh = xgTh_s[dc][:, c0:c0 + GR]
                            xl = xgTl_s[dc][:, c0:c0 + GR]
                            w.mm(ps, lh, xh, start=(dc == 0), stop=False,
                                 wait=[("v", slab_v), ("s", slab_s_cnt)])
                            w.mm(ps, lh, xl, start=False, stop=False)
                            w.mm(ps, ll, xh, start=False, stop=(dc == NCH - 1),
                                 inc=("t", 1) if dc == NCH - 1 else None)
                        tC = C["t"]
                        gp = (2 * e + gr) % 2
                        wmark("s", f"gfull{gp}")
                        w.do("s", lambda E, ps=ps, e=e, ic=ic, gp=gp: E.activation(
                            gf_s[gp][:], ps, AF.Gelu,
                            bias=b1_s[:, e * 32 + ic:e * 32 + ic + 1]),
                             inc=("s", 1), wait=[("t", tC)])
                        mark(f"fps{(2 * e + gr) % 4}", "s")
                        w.drain("s")
                        wmark("s", f"ghisb{gp}")
                        w.do("s", lambda E, gp=gp: E.activation(
                            ghi_sb[gp][:], gf_s[gp][:], AF.Copy), inc=("s", 1))
                        sC = C["s"]
                        wmark("v", f"glosb{gp}")
                        w.do("v", lambda E, gp=gp: E.tensor_tensor(
                            glo_sb[gp][:], gf_s[gp][:], ghi_sb[gp][:], ALU.subtract),
                             inc=("v", 1), wait=[("s", sC)])
                        vC = C["v"]
                        mark(f"gfull{gp}", "v")
                        gh_tok = w.dmal("g", ghi_d[ic, :, slot0 + c0: slot0 + c0 + GR],
                                        ghi_sb[gp][:], wait=[("s", sC), ("v", vC)])
                        gl_tok = w.dmal("g", glo_d[ic, :, slot0 + c0: slot0 + c0 + GR],
                                        glo_sb[gp][:])
                        w.marks[f"ghisb{gp}"] = [gh_tok]
                        w.marks[f"glosb{gp}"] = [gl_tok]
                for e in range(NE):
                    w.marks[f"slabh{ip}_{e}"] = [("t", C["t"])]
                    w.marks[f"slabl{ip}_{e}"] = [("t", C["t"])]
            w.barrier()

        def wo_phase():
            for dst, srcp in ((gam_s, gam_d), (bet_s, bet_d), (bo_s, bo_d),
                              (ones512_s, ones512_d.ap())):
                w.dmal("q", dst[:], srcp)
            wo_tok = qsn()
            sp_tok = {}
            from concourse.bass_core_rust import IndirectOffsetOnAxis

            NWB = SLOTS // WB
            for s_idx in range(NSEQ):
              slot0 = s_idx * SLOTS
              for wb in range(NWB):
                g_ic_tok = {}
                for ic in range(NIC):
                    wmark("q", f"gsl{ic}")
                    t1 = w.dmal("q", gslh_s[ic][:],
                                ghi_d[ic, :, slot0 + wb * WB:slot0 + (wb + 1) * WB])
                    t2 = w.dmal("q", gsll_s[ic][:],
                                glo_d[ic, :, slot0 + wb * WB:slot0 + (wb + 1) * WB])
                    g_ic_tok[ic] = [t1, t2]

                def emit_wosplit(jc):
                    jp = jc % 2
                    wmark("s", f"wjh{jp}")
                    wmark("v", f"wjl{jp}")
                    for q4 in range(4):
                        sb32 = slab32_s[q4 % 2]
                        wmark("q", f"slab32_{q4 % 2}")
                        wsl_tok = w.dmal("q", sb32[:],
                                         wo_d[jc, :, q4 * 8:(q4 + 1) * 8, :])
                        w.do("s", lambda E, sb32=sb32, jp=jp, q4=q4: E.activation(
                            wjh_s[jp][:, q4 * 1024:(q4 + 1) * 1024], sb32[:], AF.Copy),
                             inc=("s", 1), wait=[wsl_tok])
                        sC = C["s"]
                        w.do("v", lambda E, sb32=sb32, jp=jp, q4=q4: E.tensor_tensor(
                            wjl_s[jp][:, q4 * 1024:(q4 + 1) * 1024], sb32[:],
                            wjh_s[jp][:, q4 * 1024:(q4 + 1) * 1024], ALU.subtract),
                             inc=("v", 1), wait=[("s", sC)])
                        mark(f"slab32_{q4 % 2}", "v")
                    sp_tok[jc] = (C["v"], C["s"])

                if wb == 0:
                    emit_wosplit(0)
                for jc in range(NCH):
                    jp = jc % 2
                    if not (jc == NCH - 1 and wb == NWB - 1):
                        emit_wosplit((jc + 1) % NCH)
                    wo_v, wo_s_cnt = sp_tok[jc]
                    ps = PB[:, jp * 512:jp * 512 + WB]
                    wmark("t", f"pbb{jp}")
                    for ic in range(NIC):
                        w.mm(ps, wjh_s[jp][:, ic * 128:(ic + 1) * 128], gslh_s[ic][:],
                             start=(ic == 0), stop=False,
                             wait=[("v", wo_v), ("s", wo_s_cnt)] + g_ic_tok[ic]
                                  + wo_tok)
                        w.mm(ps, wjh_s[jp][:, ic * 128:(ic + 1) * 128], gsll_s[ic][:],
                             start=False, stop=False)
                        w.mm(ps, wjl_s[jp][:, ic * 128:(ic + 1) * 128], gslh_s[ic][:],
                             start=False, stop=False)
                    w.mm(ps, bo_s[:, jc * 128:(jc + 1) * 128], ones512_s[:, 0:WB],
                         start=False, stop=True, inc=("t", 1))
                    tC = C["t"]
                    mark(f"wjh{jp}", "t")
                    mark(f"wjl{jp}", "t")
                    wmark("s", "ysb")
                    w.do("s", lambda E, ps=ps: E.activation(ysb_s[:], ps, AF.Copy),
                         inc=("s", 1), wait=[("t", tC)])
                    sC = C["s"]
                    mark(f"pbb{jp}", "s")
                    for quarter in range(4):
                        sl = PC[:, (quarter % 2) * 512:(quarter % 2) * 512 + 128]
                        wmark("t", f"pc{quarter % 2}")
                        w.do("t", lambda E, sl=sl, quarter=quarter: E.transpose(
                            sl, ysb_s[:, quarter * 128:quarter * 128 + 128], id32_s[:]),
                             inc=("t", 1), wait=[("s", sC)])
                        tC = C["t"]
                        wmark("v", f"yrow{quarter}")
                        w.do("v", lambda E, quarter=quarter, jc=jc, sl=sl: E.tensor_copy(
                            yrow_s[quarter][:, jc * 128:(jc + 1) * 128], sl),
                             inc=("v", 1), wait=[("t", tC)])
                        mark(f"pc{quarter % 2}", "v")
                    mark("ysb", "t")
                for ic in range(NIC):
                    w.marks[f"gsl{ic}"] = [("t", C["t"])]
                y_v = C["v"]
                for quarter in range(4):
                    r0 = slot0 + wb * WB + quarter * 128
                    yg_tok = w.dmal("g", yg_d[r0:r0 + 128, :], yrow_s[quarter][:],
                                    wait=[("v", y_v)])
                    w.marks[f"yrow{quarter}"] = [yg_tok]
              ygs_tok = gsn()
              for tile in range(16):
                    tok0 = s_idx * S + tile * 128
                    ti = tile % 2
                    col = s_idx * 16 + tile
                    wmark("v", f"yrow{ti}")
                    wmark("s", f"yrow{ti}")
                    gl = GL[2 + tile % 2]
                    w.do("g", lambda E, ti=ti, col=col: E.indirect_dma_start(
                        yrow_s[ti][:], None,
                        yg_d[:, :],
                        IndirectOffsetOnAxis(idxi_s[:, col:col + 1], 0)),
                         inc=(gl, 16),
                         wait=[(gl, C[gl])] + ygs_tok)
                    yr_tok = (gl, C[gl])
                    wmark("q", f"x1r{ti}")
                    x1r_tok = w.dmal("q", x1r_s[ti][:], x1f_d[tok0:tok0 + 128, :])
                    wmark("v", f"z{ti}")
                    wmark("v", f"outt{ti}")
                    w.do("v", lambda E, ti=ti: E.scalar_tensor_tensor(
                        z_s[ti][:], yrow_s[ti][:], 1.0, x1r_s[ti][:],
                        ALU.mult, ALU.add, accum_out=vec_s[:, 12:13]),
                         inc=("v", 1), wait=[yr_tok, x1r_tok])
                    vC = C["v"]
                    mark(f"yrow{ti}", "v")
                    w.do("v", lambda E, ti=ti: E.tensor_tensor(
                        outt_s[ti][:], bet_s[:], x1r_s[ti][:], ALU.add), inc=("v", 1))
                    wmark("s", "sqd")
                    w.do("s", lambda E, ti=ti: E.activation(
                        sqd_s[:], z_s[ti][:], AF.Square, accum_out=vec_s[:, 13:14]),
                         wait=[("v", vC)])
                    w.do("s", lambda E: E.mul(vec_s[:, 1:2], vec_s[:, 12:13], 1.0 / D))
                    w.drain("s")
                    w.do("s", lambda E: E.square(vec_s[:, 2:3], vec_s[:, 1:2]),
                         inc=("s", 1))
                    sC = C["s"]
                    mark("sqd", "s")
                    w.do("v", lambda E: E.scalar_tensor_tensor(
                        vec_s[:, 3:4], vec_s[:, 13:14], 1.0 / D, vec_s[:, 2:3],
                        ALU.mult, ALU.subtract), inc=("v", 1), wait=[("s", sC)])
                    vC = C["v"]
                    w.do("s", lambda E: E.activation(
                        vec_s[:, 4:5], vec_s[:, 3:4], AF.Sqrt, bias=eps12_s[:]),
                         inc=("s", 1), wait=[("v", vC)])
                    sC = C["s"]
                    w.drain("v")
                    w.do("v", lambda E: E.reciprocal(vec_s[:, 5:6], vec_s[:, 4:5]),
                         wait=[("s", sC)])
                    w.drain("v")
                    wmark("v", "sqd")
                    w.do("v", lambda E, ti=ti: E.tensor_scalar(
                        sqd_s[:], z_s[ti][:], vec_s[:, 1:2], vec_s[:, 5:6],
                        ALU.subtract, ALU.mult))
                    w.drain("v")
                    w.do("v", lambda E: E.tensor_tensor(
                        sqd_s[:], sqd_s[:], gam_s[:], ALU.mult))
                    w.drain("v")
                    w.do("v", lambda E, ti=ti: E.scalar_tensor_tensor(
                        z_s[ti][:], sqd_s[:], 1.0, outt_s[ti][:], ALU.mult, ALU.add,
                        accum_out=vec_s[:, 14:15]), inc=("v", 1))
                    vC = C["v"]
                    mark(f"x1r{ti}", "v")
                    mark("sqd", "v")
                    w.do("s", lambda E, ti=ti: E.activation(
                        sqd_s[:], z_s[ti][:], AF.Square, accum_out=vec_s[:, 15:16]),
                         wait=[("v", vC)])
                    w.do("s", lambda E: E.mul(vec_s[:, 1:2], vec_s[:, 14:15], 1.0 / D))
                    w.drain("s")
                    w.do("s", lambda E: E.square(vec_s[:, 2:3], vec_s[:, 1:2]),
                         inc=("s", 1))
                    sC = C["s"]
                    mark("sqd", "s")
                    w.do("v", lambda E: E.scalar_tensor_tensor(
                        vec_s[:, 3:4], vec_s[:, 15:16], 1.0 / D, vec_s[:, 2:3],
                        ALU.mult, ALU.subtract), inc=("v", 1), wait=[("s", sC)])
                    vC = C["v"]
                    w.do("s", lambda E: E.activation(
                        vec_s[:, 4:5], vec_s[:, 3:4], AF.Sqrt, bias=eps5_s[:]),
                         inc=("s", 1), wait=[("v", vC)])
                    sC = C["s"]
                    w.drain("v")
                    w.do("v", lambda E: E.reciprocal(vec_s[:, 5:6], vec_s[:, 4:5]),
                         wait=[("s", sC)])
                    w.drain("v")
                    w.do("v", lambda E, ti=ti: E.tensor_scalar(
                        outt_s[ti][:], z_s[ti][:], vec_s[:, 1:2], vec_s[:, 5:6],
                        ALU.subtract, ALU.mult), inc=("v", 1))
                    vC = C["v"]
                    mark(f"z{ti}", "v")
                    o_tok = w.dmal("g", out_d[tok0:tok0 + 128, :], outt_s[ti][:],
                                   wait=[("v", vC)])
                    w.marks[f"outt{ti}"] = [o_tok]
            w.waitpool("g", "g")

        for s_idx in range(NSEQ):
            qkv_phase(s_idx)
            if stage == "qkv" and s_idx == 0:
                vv, ss_ = C["v"], C["s"]
                for c in range(NCH):
                    w.dmal("g", dbg["qT"][c * 128:(c + 1) * 128, :], qT_s[c][:],
                           wait=[("v", vv), ("s", ss_)])
                    w.dmal("g", dbg["kT"][c * 128:(c + 1) * 128, :], kT_s[c][:])
                w.dmal("g", dbg["vp"][:], vp_s[:])
                w.waitpool("g", "g")
                return
            if attn_phase(s_idx):
                return
            if slots_phase(s_idx):
                return
            ffn_phase(s_idx)
        if stage == "ffn":
            w.waitpool("g", "g")
            return
        wo_phase()

    with nc.Block() as block:
        @block.sync
        def _(eng):
            walk("q", eng)

        @block.gpsimd
        def _(eng):
            walk("g", eng)

        @block.tensor
        def _(eng):
            walk("t", eng)

        @block.scalar
        def _(eng):
            walk("s", eng)

        @block.vector
        def _(eng):
            walk("v", eng)

    return nc


def _stage_inputs(inputs):
    x = np.asarray(inputs["L0_attn_x"])
    qw = np.asarray(inputs["L0_attn_qw"])
    kw = np.asarray(inputs["L0_attn_kw"])
    vw = np.asarray(inputs["L0_attn_vw"])
    ow = np.asarray(inputs["L0_attn_ow"])
    wr = np.asarray(inputs["L0_ffn_W_router"])
    w1 = np.asarray(inputs["L0_ffn_W1"])
    b1 = np.asarray(inputs["L0_ffn_b1"])
    wo = np.asarray(inputs["L0_ffn_Wo"])
    bo = np.asarray(inputs["L0_ffn_bo"])
    gam = np.asarray(inputs["L0_ffn_ln_gamma"])
    bet = np.asarray(inputs["L0_ffn_ln_beta"])

    perm = []
    for qd in range(4):
        for par in range(2):
            for m in range(4):
                h = 4 * qd + m
                for i in range(32):
                    perm.append(64 * h + 2 * i + par)
    perm = np.array(perm)
    qwp = np.ascontiguousarray(qw[:, perm])
    kwp = np.ascontiguousarray(kw[:, perm])
    vwp = np.zeros((D, 1040), np.float16)
    for h in range(NH):
        vwp[:, 65 * h:65 * h + 64] = vw[:, 64 * h:64 * h + 64]
    wrc = np.ascontiguousarray(
        wr.reshape(8, 128, 4).transpose(1, 0, 2).reshape(128, 32)).astype(np.float32)
    w1r = np.ascontiguousarray(
        w1.reshape(NE, 8, 128, NIC, 128).transpose(0, 3, 2, 1, 4)
        .reshape(NE, NIC, 128, 1024)).astype(np.float32)
    b1c = np.ascontiguousarray(
        b1.reshape(NE, NIC, 128).transpose(2, 0, 1).reshape(128, NE * NIC)
    ).astype(np.float32)
    gamt = np.ascontiguousarray(np.broadcast_to(gam[None, :], (128, D))).astype(np.float32)
    bett = np.ascontiguousarray(np.broadcast_to(bet[None, :], (128, D))).astype(np.float32)
    bor = np.ascontiguousarray(bo[None, :]).astype(np.float32)
    wor2 = np.ascontiguousarray(
        wo.reshape(NIC, 128, NCH, 128).transpose(2, 1, 0, 3)).astype(np.float32)

    in_maps = []
    for c in range(8):
        xc = x[2 * c:2 * c + 2]
        xrows = np.ascontiguousarray(xc.reshape(T, D))
        xT = np.ascontiguousarray(xc.reshape(T, D).T)
        in_maps.append({
            "xT": xT, "xrows": xrows,
            "qwp": qwp, "kwp": kwp, "vwp": vwp, "oww": np.ascontiguousarray(ow),
            "wrc": wrc, "w1r": w1r, "b1c": b1c,
            "wor2": wor2, "bor": bor, "gamt": gamt, "bett": bett,
        })
    return in_maps


def kernel(**inputs):
    in_maps = _stage_inputs(inputs)
    nc = build(stage="full", debug=False)
    res = run_bass_kernel_spmd(nc, in_maps, core_ids=list(range(8)))
    out = np.concatenate([np.asarray(r["out"]) for r in res.results], axis=0)
    return out.reshape(16, 2048, 1024).astype(np.float32)


# version 16
# speedup vs baseline: 1.1710x; 1.0626x over previous
import numpy as np

import concourse.bass as bass
import concourse.mybir as mybir
from concourse.bass_utils import run_bass_kernel_spmd

F32 = mybir.dt.float32
F16 = mybir.dt.float16
AF = mybir.ActivationFunctionType
ALU = mybir.AluOpType

D = 1024
S = 2048
NSEQ = 2
T = NSEQ * S
NH = 16
NCH = 8
TG = 512
NTG = S // TG
NE = 4
DFF = 4096
NIC = DFF // 128
SCALE = 1.0 / 32.0
MASKVAL = -30000.0
LN_EPS_OUT = 1e-5
LN_EPS_IN = 1e-12
CCAP = 640
GR = 320
SLOTS = NE * CCAP
TRASH = NSEQ * SLOTS
NXG = TRASH + 128

CAP = 205312


def _round32(x):
    return (x + 31) // 32 * 32


class Arena:

    _ctr = [0]

    def __init__(self, nc, cap):
        self.nc = nc
        self.cap = cap
        self.free = [(0, cap)]
        self.live = {}
        Arena._ctr[0] += 1
        self.backing = nc.alloc_sbuf_tensor(f"arena{Arena._ctr[0]}", [128, cap],
                                            mybir.dt.uint8)
        self.base_addr = nc.lookup_mloc(self.backing).addr

    def alloc(self, name, shape, dtype):
        per_part = int(np.prod(shape[1:])) * mybir.dt.size(dtype)
        need = _round32(per_part)
        for idx, (off, sz) in enumerate(self.free):
            if sz >= need:
                self.free[idx] = (off + need, sz - need)
                if self.free[idx][1] == 0:
                    self.free.pop(idx)
                h = self.nc.alloc_sbuf_tensor_at(name, list(shape), dtype,
                                                 offset=self.base_addr + off)
                self.live[name] = (off, need)
                return h
        raise MemoryError(f"sbuf arena: no room for {name} ({need}B); free={self.free}")

    def freebuf(self, *names):
        for name in names:
            off, sz = self.live.pop(name)
            self.free.append((off, sz))
        self.free.sort()
        out = []
        for off, sz in self.free:
            if out and out[-1][0] + out[-1][1] == off:
                out[-1] = (out[-1][0], out[-1][1] + sz)
            else:
                out.append((off, sz))
        self.free = [tuple(x) for x in out]


class Walk:

    def __init__(self, me, handle, sems):
        self.me = me
        self.E = handle
        self.sems = sems
        self.cnt = {k: 0 for k in sems}
        self.last_wait = {}
        self.marks = {}

    def wait(self, eng, *pairs):
        if self.me == eng:
            for s, v in pairs:
                if v <= 0 or self.last_wait.get(s, -1) >= v:
                    continue
                self.E.wait_ge(self.sems[s], v)
                self.last_wait[s] = v

    def do(self, eng, fn, inc=None, wait=None):
        if self.me == eng:
            if wait:
                self.wait(eng, *wait)
            inst = fn(self.E)
            if inc:
                inst.then_inc(self.sems[inc[0]], inc[1])
        if inc:
            self.cnt[inc[0]] += inc[1]

    def drain(self, eng):
        if self.me == eng:
            self.E.drain()

    def dma(self, eng, out, in_, sem, wait=None):
        self.do(eng, lambda E: E.dma_start(out=out, in_=in_), inc=(sem, 16), wait=wait)

    def dmal(self, eng, out, in_, wait=None):
        pool = self.pools[eng]
        lane = pool[self.rr[eng] % len(pool)]
        self.rr[eng] += 1
        ws = [(lane, self.cnt[lane])]
        if wait:
            ws += list(wait)
        self.do(eng, lambda E: E.dma_start(out=out, in_=in_), inc=(lane, 16), wait=ws)
        return (lane, self.cnt[lane])

    def waitpool(self, eng, pool_of):
        for lane in self.pools[pool_of]:
            self.wait(eng, (lane, self.cnt[lane]))

    def mm(self, out, lhsT, rhs, start, stop, tp=None, wait=None, inc=None):
        self.do("t",
                lambda E: E.matmul(out, lhsT, rhs, start=start, stop=stop,
                                   tile_position=tp),
                inc=inc, wait=wait)

    def mark(self, name, sem):
        self.marks[name] = [(sem, self.cnt[sem])]

    def wmark(self, eng, name):
        for s, v in self.marks.get(name, ()):
            self.wait(eng, (s, v))

    def barrier(self):
        snap = [(s, v) for s, v in self.cnt.items() if v > 0]
        for eng in ("q", "g", "t", "s", "v"):
            self.wait(eng, *snap)


def build(stage="full", debug=False):
    nc = bass.Bass("TRN2", target_bir_lowering=False)

    xT_d = nc.dram_tensor("xT", [D, T], F16, kind="ExternalInput").ap()
    xrows_d = nc.dram_tensor("xrows", [T, D], F16, kind="ExternalInput").ap()
    qwp_d = nc.dram_tensor("qwp", [D, D], F16, kind="ExternalInput").ap()
    kwp_d = nc.dram_tensor("kwp", [D, D], F16, kind="ExternalInput").ap()
    vwp_d = nc.dram_tensor("vwp", [D, 1040], F16, kind="ExternalInput").ap()
    ow_d = nc.dram_tensor("oww", [D, D], F16, kind="ExternalInput").ap()
    wr_d = nc.dram_tensor("wrc", [128, 32], F32, kind="ExternalInput").ap()
    w1_d = nc.dram_tensor("w1r", [NE, NIC, 128, 1024], F32, kind="ExternalInput").ap()
    b1_d = nc.dram_tensor("b1c", [128, 128], F32, kind="ExternalInput").ap()
    wo_d = nc.dram_tensor("wor2", [NCH, 128, NIC, 128], F32, kind="ExternalInput").ap()
    bo_d = nc.dram_tensor("bor", [1, D], F32, kind="ExternalInput").ap()
    gam_d = nc.dram_tensor("gamt", [128, D], F32, kind="ExternalInput").ap()
    bet_d = nc.dram_tensor("bett", [128, D], F32, kind="ExternalInput").ap()
    out_d = nc.dram_tensor("out", [T, D], F32, kind="ExternalOutput").ap()

    x1f_d = nc.dram_tensor("x1f", [T, D], F32, kind="Internal").ap()
    ghi_d = nc.dram_tensor("ghi", [NIC, 128, NSEQ * SLOTS], F16, kind="Internal").ap()
    glo_d = nc.dram_tensor("glo", [NIC, 128, NSEQ * SLOTS], F16, kind="Internal").ap()
    xg_d = nc.dram_tensor("xg", [NXG, D], F32, kind="Internal").ap()
    yg_d = nc.dram_tensor("yg", [NXG, D], F32, kind="Internal").ap()
    slot_d = nc.dram_tensor("slotd", [NSEQ * NTG, TG], F32, kind="Internal").ap()

    dbg = {}
    if debug and stage == "qkv":
        dbg["qT"] = nc.dram_tensor("dbg_qT", [D, S], F16, kind="ExternalOutput").ap()
        dbg["kT"] = nc.dram_tensor("dbg_kT", [D, S], F16, kind="ExternalOutput").ap()
        dbg["vp"] = nc.dram_tensor("dbg_vp", [128, 16 * 1040], F16, kind="ExternalOutput").ap()
    if debug and stage == "attn":
        dbg["at_hi"] = nc.dram_tensor("dbg_athi", [D, TG], F16, kind="ExternalOutput").ap()
        dbg["at_lo"] = nc.dram_tensor("dbg_atlo", [D, TG], F16, kind="ExternalOutput").ap()
    if debug and stage == "ln1":
        dbg["x1"] = nc.dram_tensor("dbg_x1", [T, D], F32, kind="ExternalOutput").ap()
    if debug and stage == "slots":
        dbg["idx"] = nc.dram_tensor("dbg_idx", [128, 32], F32, kind="ExternalOutput").ap()

    pos = np.arange(S, dtype=np.float64)
    inv = 10000.0 ** (-np.arange(32, dtype=np.float64) / 32.0)
    freq = inv[:, None] * pos[None, :]
    cc_d = nc.inline_tensor(np.tile(np.cos(freq), (4, 1)).astype(np.float32), "cc_c")
    ss_d = nc.inline_tensor(np.tile(np.sin(freq), (4, 1)).astype(np.float32), "ss_c")
    masks_np = np.zeros((4, 128, TG), np.float16)
    for di in range(4):
        k_idx = np.arange(128)[:, None] + di * 128
        q_idx = np.arange(TG)[None, :]
        masks_np[di][k_idx > q_idx] = np.float16(MASKVAL)
    mask_d = [nc.inline_tensor(masks_np[i], f"mask{i}_c") for i in range(4)]
    id16_d = nc.inline_tensor(np.eye(128, dtype=np.float16), "id16_c")
    id32_d = nc.inline_tensor(np.eye(128, dtype=np.float32), "id32_c")
    sel64_np = np.zeros((128, 128), np.float32)
    for _r in (0, 32, 64, 96):
        sel64_np[_r, :64] = 1.0
    sel64_d = nc.inline_tensor(sel64_np, "sel64_c")
    sel64b_np = np.zeros((1, 128), np.float32)
    sel64b_np[0, 64:] = 1.0
    sel64b_d = nc.inline_tensor(sel64b_np, "sel64b_c")
    ones128_d = nc.inline_tensor(np.ones((1, 128), np.float16), "ones128_c")
    ones512_d = nc.inline_tensor(np.ones((1, 512), np.float32), "ones512_c")
    trrow_np = (TRASH + (np.arange(512) % 128)).astype(np.float32)[None, :]
    trrow_d = nc.inline_tensor(trrow_np, "trrow_c")

    AP_ = Arena(nc, 40448)
    AR1 = Arena(nc, 63488)
    AR2 = Arena(nc, 98816)

    class Multi:

        def alloc(self, name, shape, dtype):
            try:
                return AR1.alloc(name, shape, dtype)
            except MemoryError:
                return AR2.alloc(name, shape, dtype)

    AC = Multi()

    id16_s = AP_.alloc("id16", [128, 128], F16)
    id32_s = AP_.alloc("id32", [128, 128], F32)
    sel64_s = AP_.alloc("sel64", [128, 128], F32)
    sel64b_s = AP_.alloc("sel64b", [1, 128], F32)
    ones128_s = AP_.alloc("ones128", [1, 128], F16)
    wr32_s = AP_.alloc("wr32", [128, 32], F32)
    wrh_s = AP_.alloc("wrh", [128, 32], F16)
    wrl_s = AP_.alloc("wrl", [128, 32], F16)
    b1_s = AP_.alloc("b1", [128, 128], F32)
    slab32_s = [AP_.alloc(f"slab32_{i}", [128, 1024], F32) for i in range(2)]
    xrow_s = [AP_.alloc(f"xrow{i}", [128, D], F16) for i in range(4)]
    z_s = [AP_.alloc(f"z{i}", [128, D], F32) for i in range(2)]
    sqd_s = AP_.alloc("sqd", [128, D], F32)
    vec_s = AP_.alloc("vecs", [128, 32], F32)
    x1t_s = [AP_.alloc(f"x1t{i}", [128, D], F32) for i in range(2)]
    eps5_s = AP_.alloc("eps5", [128, 1], F32)
    eps12_s = AP_.alloc("eps12", [128, 1], F32)
    idxf_s = AP_.alloc("idxf", [128, 32], F32)
    idxi_s = AP_.alloc("idxi", [128, 32], mybir.dt.int32)

    xts_s = [AR1.alloc(f"xts{c}", [128, S], F16) for c in range(NCH)]
    wch_s = [AR1.alloc(f"wch{c}", [128, 1040], F16) for c in range(NCH)]
    ccs_s = AR1.alloc("ccs", [128, TG], F32)
    sss_s = AR1.alloc("sss", [128, TG], F32)
    rp_s = [AR1.alloc(f"rp{i}", [128, TG], F32) for i in range(4)]
    AR1.freebuf(*([f"xts{c}" for c in range(NCH)] + [f"wch{c}" for c in range(NCH)]
                  + ["ccs", "sss"] + [f"rp{i}" for i in range(4)]))
    e32_s = [AR1.alloc(f"e32_{i}", [128, 1024], F32) for i in range(3)]
    _o0 = AP_.base_addr + AP_.live["slab32_0"][0]
    _o1 = AP_.base_addr + AP_.live["slab32_1"][0]
    ehi_s = [AR1.alloc(f"ehi_{i}", [128, 1024], F16) for i in range(2)] + [
        nc.alloc_sbuf_tensor_at("ehi_2", [128, 1024], F16, offset=_o0),
        nc.alloc_sbuf_tensor_at("ehi_3", [128, 1024], F16, offset=_o0 + 2048)]
    elo_s = [AR1.alloc(f"elo_{i}", [128, 1024], F16) for i in range(2)] + [
        nc.alloc_sbuf_tensor_at("elo_2", [128, 1024], F16, offset=_o1),
        nc.alloc_sbuf_tensor_at("elo_3", [128, 1024], F16, offset=_o1 + 2048)]
    rec_s = AR1.alloc("rec", [128, 1024], F32)

    def rec_ap(hh):
        if hh < 3:
            return rec_s[32 * hh:32 * hh + 1, 0:512]
        return rec_s[0:1, 512:1024]
    bsb_s = AR1.alloc("bsb", [64, 2048], F32)
    t32_s = AR1.alloc("t32", [128, TG], F32)
    athi_s = [AR1.alloc(f"athi{c}", [128, TG], F16) for c in range(NCH)]
    atlo_s = [AR1.alloc(f"atlo{c}", [128, TG], F16) for c in range(NCH)]
    owh_s = [AR1.alloc(f"owh{i}", [128, D], F16) for i in range(4)]
    msk_s = [AR1.alloc(f"msk{i}", [128, TG], F16) for i in range(4)]
    AR1.freebuf(*(["e32_0", "e32_1", "e32_2",
                   "ehi_0", "ehi_1", "elo_0", "elo_1",
                   "rec", "bsb", "t32"]
                  + [f"athi{c}" for c in range(NCH)] + [f"atlo{c}" for c in range(NCH)]
                  + [f"owh{i}" for i in range(4)] + [f"msk{i}" for i in range(4)]))
    lgs_s = AR1.alloc("lgs", [4, TG], F32)
    lrow_s = AR1.alloc("lrow", [1, NE * TG], F32)
    rtmp_s = AR1.alloc("rtmp", [1, 7 * TG], F32)
    onesr_s = AR1.alloc("onesr", [1, TG], F32)
    trrow_s = AR1.alloc("trrow", [1, TG], F32)
    carr_s = AR1.alloc("carr", [1, 32], F32)
    ovi_s = AR1.alloc("ovi", [1, TG], mybir.dt.int32)
    slotr_s = AR1.alloc("slotr", [1, TG], F32)
    AR1.freebuf("lgs", "lrow", "rtmp", "onesr", "trrow", "carr", "ovi", "slotr")
    slabh_s = [[AR1.alloc(f"slabh{i}_{e}", [128, 1024], F16) for e in range(NE)]
               for i in range(2)]
    slabl_s = [[AR1.alloc(f"slabl{i}_{e}", [128, 1024], F16) for e in range(NE)]
               for i in range(2)]
    gf_s = [AR1.alloc(f"gfull{i}", [128, GR], F32) for i in range(2)]
    ghi_sb = [AR1.alloc(f"ghisb{i}", [128, GR], F16) for i in range(2)]
    glo_sb = [AR1.alloc(f"glosb{i}", [128, GR], F16) for i in range(2)]
    AR1.freebuf(*[f"slabh{i}_{e}" for i in range(2) for e in range(NE)],
                *[f"slabl{i}_{e}" for i in range(2) for e in range(NE)],
                "gfull0", "gfull1", "ghisb0", "ghisb1", "glosb0", "glosb1")

    qT_s = [AR2.alloc(f"qT{c}", [128, S], F16) for c in range(NCH)]
    kT_s = [AR2.alloc(f"kT{c}", [128, S], F16) for c in range(NCH)]
    vp_s = AR2.alloc("vp", [128, 16 * 1040], F16)
    AR2.freebuf(*([f"qT{c}" for c in range(NCH)] + [f"kT{c}" for c in range(NCH)]
                  + ["vp"]))
    xgTh_s = [AR2.alloc(f"xgTh{c}", [128, SLOTS], F16) for c in range(NCH)]
    xgTl_s = [AR2.alloc(f"xgTl{c}", [128, SLOTS], F16) for c in range(NCH)]
    x1Tt_s = [AR2.alloc(f"x1Tt{i}", [128, 1024], F16) for i in range(4)]
    AR2.freebuf(*([f"xgTh{c}" for c in range(NCH)] + [f"xgTl{c}" for c in range(NCH)]
                  + [f"x1Tt{i}" for i in range(4)]))
    WB = 512
    gslh_s = [AC.alloc(f"gslh{ic}", [128, WB], F16) for ic in range(NIC)]
    gsll_s = [AC.alloc(f"gsll{ic}", [128, WB], F16) for ic in range(NIC)]
    wjh_s = [AC.alloc(f"wjh{i}", [128, NIC * 128], F16) for i in range(2)]
    wjl_s = [AC.alloc(f"wjl{i}", [128, NIC * 128], F16) for i in range(2)]
    wsp_h = AC.alloc("wsph", [128, 1024], F16)
    wsp_l = AC.alloc("wspl", [128, 1024], F16)
    ysb_s = AC.alloc("ysb", [128, WB], F32)
    yrow_s = [AC.alloc(f"yrow{i}", [128, D], F32) for i in range(4)]
    x1r_s = [AC.alloc(f"x1r{i}", [128, D], F32) for i in range(2)]
    outt_s = [AC.alloc(f"outt{i}", [128, D], F32) for i in range(2)]
    gam_s = AC.alloc("gams", [128, D], F32)
    bet_s = AC.alloc("bets", [128, D], F32)
    bo_s = AC.alloc("bo", [1, D], F32)
    ones512_s = AC.alloc("ones512", [1, 512], F32)

    PA = nc.alloc_psum_tensor("PA", [128, 2048], F32)
    PB = nc.alloc_psum_tensor("PB", [128, 1024], F32)
    PC = nc.alloc_psum_tensor("PC", [128, 1024], F32)

    QL = [f"qa{i}" for i in range(6)]
    GL = [f"gb{i}" for i in range(4)]
    sem_handles = {n: nc.alloc_semaphore(name=f"sem_{n}")
                   for n in tuple(QL) + tuple(GL) + ("t", "s", "v")}

    def walk(me, E):
        w = Walk(me, E, sem_handles)
        w.pools = {"q": QL, "g": GL}
        w.rr = {"q": 0, "g": 0}
        C = w.cnt
        mark, wmark = w.mark, w.wmark

        def qsn():
            return [(l, C[l]) for l in QL]

        def gsn():
            return [(l, C[l]) for l in GL]

        for dst, srcp in ((id16_s, id16_d.ap()), (id32_s, id32_d.ap()),
                          (sel64_s, sel64_d.ap()), (sel64b_s, sel64b_d.ap()),
                          (ones128_s, ones128_d.ap()),
                          (wr32_s, wr_d), (b1_s, b1_d)):
            w.dmal("q", dst[:], srcp)
        const_tok = qsn()
        w.do("v", lambda E: E.memset(eps5_s[:], LN_EPS_OUT))
        w.do("v", lambda E: E.memset(eps12_s[:], LN_EPS_IN))
        w.drain("v")
        w.do("s", lambda E: E.activation(wrh_s[:], wr32_s[:], AF.Copy),
             inc=("s", 1), wait=const_tok)
        sC = C["s"]
        w.do("v", lambda E: E.tensor_tensor(wrl_s[:], wr32_s[:], wrh_s[:], ALU.subtract),
             inc=("v", 1), wait=[("s", sC)])
        w.do("v", lambda E: E.memset(z_s[0][:], 0.0), inc=("v", 1))
        w.drain("v")
        zf_v = C["v"]
        for r0 in range(0, NXG, 128):
            w.dmal("g", xg_d[r0:r0 + 128, :], z_s[0][:], wait=[("v", zf_v)])
            w.dmal("g", yg_d[r0:r0 + 128, :], z_s[0][:])
        zf_tok = gsn()
        w.marks["z0"] = list(zf_tok)

        def qkv_phase(s_idx):
            tok0 = s_idx * S
            for c in range(NCH):
                wmark("q", f"xts{c}")
                w.dmal("q", xts_s[c][:], xT_d[c * 128:(c + 1) * 128, tok0:tok0 + S])
            x_tok = qsn()

            for wsrc, dstT in ((qwp_d, qT_s), (kwp_d, kT_s)):
                wmark("q", "wch")
                for dc in range(NCH):
                    w.dmal("q", wch_s[dc][:, 0:1024], wsrc[dc * 128:(dc + 1) * 128, :])
                w_tok = qsn()
                for tg in range(NTG):
                    wmark("q", "ccs")
                    ct1 = w.dmal("q", ccs_s[:], cc_d.ap()[:, tg * TG:(tg + 1) * TG])
                    ct2 = w.dmal("q", sss_s[:], ss_d.ap()[:, tg * TG:(tg + 1) * TG])
                    for qd in range(4):
                        b = (qd % 2) * 1024
                        pe = PA[:, b:b + 512]
                        po = PA[:, b + 512:b + 1024]
                        wmark("t", f"pab{qd % 2}")
                        for dc in range(NCH):
                            rhs = xts_s[dc][:, tg * TG:(tg + 1) * TG]
                            w.mm(pe, wch_s[dc][:, (2 * qd) * 128:(2 * qd + 1) * 128], rhs,
                                 start=(dc == 0), stop=(dc == NCH - 1),
                                 wait=w_tok + x_tok)
                            w.mm(po, wch_s[dc][:, (2 * qd + 1) * 128:(2 * qd + 2) * 128],
                                 rhs, start=(dc == 0), stop=(dc == NCH - 1),
                                 inc=("t", 1) if dc == NCH - 1 else None)
                        tC = C["t"]
                        wmark("v", "rp")
                        w.do("v", lambda E, pe=pe: E.tensor_tensor(
                            rp_s[0][:], pe, ccs_s[:], ALU.mult),
                             wait=[("t", tC), ct1, ct2])
                        w.do("v", lambda E, po=po: E.tensor_tensor(
                            rp_s[1][:], po, sss_s[:], ALU.mult))
                        w.do("v", lambda E, pe=pe: E.tensor_tensor(
                            rp_s[2][:], pe, sss_s[:], ALU.mult))
                        w.do("v", lambda E, po=po: E.tensor_tensor(
                            rp_s[3][:], po, ccs_s[:], ALU.mult), inc=("v", 1))
                        mark(f"pab{qd % 2}", "v")
                        if qd == 3:
                            mark("ccs", "v")
                        w.drain("v")
                        de = dstT[2 * qd][:, tg * TG:(tg + 1) * TG]
                        do_ = dstT[2 * qd + 1][:, tg * TG:(tg + 1) * TG]
                        w.do("v", lambda E, de=de: E.tensor_tensor(
                            de, rp_s[0][:], rp_s[1][:], ALU.subtract))
                        w.do("v", lambda E, do_=do_: E.tensor_tensor(
                            do_, rp_s[2][:], rp_s[3][:], ALU.add), inc=("v", 1))
                        mark("rp", "v")
                        w.drain("v")
                mark("wch", "t")

            wmark("q", "wch")
            for dc in range(NCH):
                w.dmal("q", wch_s[dc][:, 0:1040], vwp_d[dc * 128:(dc + 1) * 128, :])
            w_tok = qsn()
            pv = 0
            for tt in range(16):
                for j0, jn in ((0, 512), (512, 512), (1024, 16)):
                    ps = PB[:, (pv % 2) * 512:(pv % 2) * 512 + 512]
                    wmark("t", f"pbb{pv % 2}")
                    for dc in range(NCH):
                        w.mm(ps[:, 0:jn], xts_s[dc][:, tt * 128:(tt + 1) * 128],
                             wch_s[dc][:, j0:j0 + jn],
                             start=(dc == 0), stop=(dc == NCH - 1),
                             wait=w_tok + x_tok,
                             inc=("t", 1) if dc == NCH - 1 else None)
                    tC = C["t"]
                    dst = vp_s[:, tt * 1040 + j0: tt * 1040 + j0 + jn]
                    w.do("s", lambda E, dst=dst, ps=ps, jn=jn: E.activation(
                        dst, ps[:, 0:jn], AF.Copy), inc=("s", 1), wait=[("t", tC)])
                    mark(f"pbb{pv % 2}", "s")
                    pv += 1
            sv = C["s"]
            w.do("v", lambda E: E.memset(
                vp_s[:, :].rearrange("p (t c) -> p t c", c=1040)[:, :, 64::65], 1.0),
                 inc=("v", 1), wait=[("s", sv)])
            w.drain("v")
            mark("wch", "t")
            w.barrier()

        def attn_phase(s_idx):
            for i in range(4):
                w.dmal("q", msk_s[i][:], mask_d[i].ap())
            msk_tok = qsn()

            for qg in range(NTG):
                for q in range(4):
                    nkt = 4 * (qg + 1)
                    svtok = {}

                    def emit_scores(kt, q=q, qg=qg, nkt=nkt, svtok=svtok):
                        wmark("t", "bq")
                        wmark("t", "pafV")
                        masked = kt >= qg * 4
                        di = kt - qg * 4
                        toks2 = []
                        for h2 in range(2):
                            wmark("t", f"scq{h2}")
                            for m in (2 * h2, 2 * h2 + 1):
                                dst = PA[:, m * 512:(m + 1) * 512]
                                for half in range(2):
                                    ch = 2 * q + half
                                    last = (half == 1) and not masked
                                    w.mm(dst,
                                         kT_s[ch][32 * m:32 * m + 32,
                                                  kt * 128:(kt + 1) * 128],
                                         qT_s[ch][32 * m:32 * m + 32,
                                                  qg * TG:(qg + 1) * TG],
                                         start=(half == 0), stop=last,
                                         tp=(32 * m, 0), wait=msk_tok,
                                         inc=("t", 1) if (last and m == 2 * h2 + 1)
                                         else None)
                            if masked:
                                for m in (2 * h2, 2 * h2 + 1):
                                    w.mm(PA[:, m * 512:(m + 1) * 512], id16_s[:],
                                         msk_s[di][:], start=False, stop=True,
                                         inc=("t", 1) if m == 2 * h2 + 1 else None)
                            toks2.append(C["t"])
                        svtok[kt] = toks2

                    def emit_expchain(kt, svtok=svtok):
                        tC0, tC1 = svtok[kt]
                        stoks = {}
                        for h2 in range(2):
                            e3 = (2 * kt + h2) % 3
                            wmark("s", f"e32_{e3}")
                            w.do("s", lambda E, h2=h2, e3=e3: E.activation(
                                e32_s[e3][:], PA[:, h2 * 1024:(h2 + 1) * 1024],
                                AF.Exp, scale=SCALE),
                                 inc=("s", 1), wait=[("t", tC0 if h2 == 0 else tC1)])
                            mark(f"scq{h2}", "s")
                            stoks[h2] = C["s"]
                        toks = {}
                        for h2 in range(2):
                            e3 = (2 * kt + h2) % 3
                            e4 = (2 * kt + h2) % 4
                            wmark("v", f"ehi_{e4}")
                            w.do("v", lambda E, e3=e3, e4=e4: E.tensor_copy(
                                ehi_s[e4][:], e32_s[e3][:]), inc=("v", 1),
                                 wait=[("s", stoks[h2])])
                            vCc = C["v"]
                            wmark("v", f"elo_{e4}")
                            w.do("v", lambda E, e3=e3, e4=e4: E.tensor_tensor(
                                elo_s[e4][:], e32_s[e3][:], ehi_s[e4][:], ALU.subtract),
                                 inc=("v", 1), wait=[("v", vCc)])
                            mark(f"e32_{e3}", "v")
                            toks[h2] = (vCc, C["v"])
                        svtok[kt] = toks

                    def emit_vmm(kt, q=q, nkt=nkt, svtok=svtok):
                        toks = svtok[kt]
                        if kt == 0:
                            wmark("t", "atb")
                        for lo in range(2):
                            for hh in range(4):
                                h2 = hh // 2
                                e4 = (2 * kt + h2) % 4
                                h = 4 * q + hh
                                PX = PB if hh < 2 else PC
                                pat = PX[0:65, (hh % 2) * 512:(hh % 2) * 512 + 512]
                                lhs = vp_s[:, kt * 1040 + 65 * h: kt * 1040 + 65 * h + 65]
                                src = (elo_s if lo else ehi_s)[e4]
                                w.mm(pat, lhs,
                                     src[:, (hh % 2) * 512:(hh % 2) * 512 + 512],
                                     start=(kt == 0 and lo == 0),
                                     stop=(kt == nkt - 1 and lo == 1),
                                     wait=[("v", toks[h2][lo])] if hh % 2 == 0 else None,
                                     inc=("t", 1) if hh == 3 else None)
                            nm = "elo" if lo else "ehi"
                            mark(f"{nm}_{(2 * kt) % 4}", "t")
                            mark(f"{nm}_{(2 * kt + 1) % 4}", "t")

                    for kt in range(nkt):
                        emit_scores(kt)
                        if kt > 0:
                            emit_vmm(kt - 1)
                        emit_expchain(kt)
                    emit_vmm(nkt - 1)
                    tC = C["t"]

                    wmark("s", "rec")
                    for hh in range(4):
                        PX = PB if hh < 2 else PC
                        w.do("s", lambda E, hh=hh, PX=PX: E.activation(
                            rec_ap(hh),
                            PX[64:65, (hh % 2) * 512:(hh % 2) * 512 + 512], AF.Copy),
                             inc=("s", 1) if hh == 3 else None,
                             wait=[("t", tC)])
                    sC = C["s"]
                    for hh in range(4):
                        w.do("v", lambda E, hh=hh: E.reciprocal(
                            rec_ap(hh), rec_ap(hh)),
                             inc=("v", 1) if hh == 3 else None,
                             wait=[("s", sC)] if hh == 0 else None)
                    vC = C["v"]
                    mark("rec", "v")
                    wmark("t", "scq0")
                    wmark("t", "scq1")
                    for hh in range(4):
                        _r = 0 if hh == 3 else 32 * hh
                        w.mm(PA[:, hh * 512:hh * 512 + 512],
                             sel64_s[_r:_r + 1, :],
                             rec_ap(hh), start=True, stop=True,
                             wait=[("v", vC)], inc=("t", 1) if hh == 3 else None)
                    tC2 = C["t"]
                    wmark("s", "bsb")
                    for hh in range(4):
                        w.do("s", lambda E, hh=hh: E.activation(
                            bsb_s[0:64, hh * 512:(hh + 1) * 512],
                            PA[0:64, hh * 512:hh * 512 + 512], AF.Copy),
                             inc=("s", 1) if hh == 3 else None, wait=[("t", tC2)])
                    sC = C["s"]
                    mark("bq", "s")
                    for half in range(2):
                        p = 2 * q + half
                        hh0, hh1 = 2 * half, 2 * half + 1
                        PX = PB if half == 0 else PC
                        wmark("v", "t32")
                        w.do("v", lambda E, PX=PX, hh0=hh0: E.tensor_tensor(
                            t32_s[0:64, :], PX[0:64, 0:512],
                            bsb_s[0:64, hh0 * 512:(hh0 + 1) * 512], ALU.mult),
                             wait=[("s", sC)])
                        w.do("v", lambda E, PX=PX, hh1=hh1: E.tensor_tensor(
                            t32_s[64:128, :], PX[0:64, 512:1024],
                            bsb_s[0:64, hh1 * 512:(hh1 + 1) * 512], ALU.mult),
                             inc=("v", 1))
                        vC = C["v"]
                        if half == 1:
                            mark("bsb", "v")
                            mark("atb", "v")
                        wmark("s", f"athi{p}")
                        w.do("s", lambda E, p=p: E.activation(
                            athi_s[p][:], t32_s[:], AF.Copy), inc=("s", 1),
                             wait=[("v", vC)])
                        sC2 = C["s"]
                        wmark("v", f"atlo{p}")
                        w.do("v", lambda E, p=p: E.tensor_tensor(
                            atlo_s[p][:], t32_s[:], athi_s[p][:], ALU.subtract),
                             inc=("v", 1), wait=[("s", sC2)])
                        mark("t32", "v")

                if stage == "attn" and qg == 0:
                    vv, ss_ = C["v"], C["s"]
                    for c in range(NCH):
                        w.dmal("g", dbg["at_hi"][c * 128:(c + 1) * 128, :], athi_s[c][:],
                               wait=[("v", vv), ("s", ss_)])
                        w.dmal("g", dbg["at_lo"][c * 128:(c + 1) * 128, :], atlo_s[c][:])
                    w.waitpool("g", "g")
                    return True

                athi_v = C["v"]
                athi_sc = C["s"]
                pbank = {0: PA[:, 0:1024], 1: PA[:, 1024:2048],
                         2: PB[:, 0:1024], 3: PC[:, 0:1024]}
                xr_toks = {}
                for tt in range(4):
                    tok0 = s_idx * S + qg * TG + tt * 128
                    wmark("q", f"xrow{tt}")
                    xr_toks[tt] = w.dmal("q", xrow_s[tt][:], xrows_d[tok0:tok0 + 128, :])
                for halfw in range(2):
                    wmark("q", "owh")
                    for cc in range(4):
                        c = 4 * halfw + cc
                        w.dmal("q", owh_s[cc][:], ow_d[c * 128:(c + 1) * 128, :])
                    ow_tok = qsn()
                    for tt in range(4):
                        ps = pbank[tt]
                        if halfw == 0:
                            if tt < 2:
                                wmark("t", "pafV")
                            else:
                                wmark("t", "atb")
                        for cc in range(4):
                            c = 4 * halfw + cc
                            for jg in range(2):
                                w.mm(ps[:, jg * 512:(jg + 1) * 512],
                                     athi_s[c][:, tt * 128:(tt + 1) * 128],
                                     owh_s[cc][:, jg * 512:(jg + 1) * 512],
                                     start=(halfw == 0 and cc == 0), stop=False,
                                     wait=[("v", athi_v), ("s", athi_sc)] + ow_tok)
                                w.mm(ps[:, jg * 512:(jg + 1) * 512],
                                     atlo_s[c][:, tt * 128:(tt + 1) * 128],
                                     owh_s[cc][:, jg * 512:(jg + 1) * 512],
                                     start=False, stop=(halfw == 1 and cc == 3),
                                     inc=("t", 1) if (tt == 3 and cc == 3
                                                      and jg == 1) else None)
                    mark("owh", "t")
                opt_t = C["t"]
                for tt in range(4):
                    ti = tt % 2
                    tok0 = s_idx * S + qg * TG + tt * 128
                    ps = pbank[tt]
                    wmark("v", f"z{ti}")
                    w.do("v", lambda E, ps=ps, ti=ti, tt=tt:
                         E.scalar_tensor_tensor(
                             z_s[ti][:], ps[:, :], 1.0, xrow_s[tt][:],
                             ALU.mult, ALU.add, accum_out=vec_s[:, 8:9]),
                         inc=("v", 1), wait=[("t", opt_t), xr_toks[tt]])
                    if tt == 1:
                        mark("pafV", "v")
                    if tt == 3:
                        mark("atb", "v")
                    mark(f"xrow{tt}", "v")
                    vC = C["v"]
                    wmark("s", "sqd")
                    w.do("s", lambda E, ti=ti: E.activation(
                        sqd_s[:], z_s[ti][:], AF.Square, accum_out=vec_s[:, 10:11]),
                         wait=[("v", vC)])
                    w.do("s", lambda E: E.mul(vec_s[:, 1:2], vec_s[:, 8:9], 1.0 / D))
                    w.drain("s")
                    w.do("s", lambda E: E.square(vec_s[:, 2:3], vec_s[:, 1:2]),
                         inc=("s", 1))
                    sC = C["s"]
                    mark("sqd", "s")
                    w.do("v", lambda E: E.scalar_tensor_tensor(
                        vec_s[:, 3:4], vec_s[:, 10:11], 1.0 / D, vec_s[:, 2:3],
                        ALU.mult, ALU.subtract), inc=("v", 1), wait=[("s", sC)])
                    vC = C["v"]
                    w.do("s", lambda E: E.activation(
                        vec_s[:, 4:5], vec_s[:, 3:4], AF.Sqrt, bias=eps5_s[:]),
                         inc=("s", 1), wait=[("v", vC)])
                    sC = C["s"]
                    w.drain("v")
                    w.do("v", lambda E: E.reciprocal(vec_s[:, 5:6], vec_s[:, 4:5]),
                         wait=[("s", sC)])
                    w.drain("v")
                    wmark("v", f"x1t{ti}")
                    w.do("v", lambda E, ti=ti: E.tensor_scalar(
                        x1t_s[ti][:], z_s[ti][:], vec_s[:, 1:2], vec_s[:, 5:6],
                        ALU.subtract, ALU.mult), inc=("v", 1))
                    vC = C["v"]
                    mark(f"z{ti}", "v")
                    x1tok = [w.dmal("g", x1f_d[tok0:tok0 + 128, :], x1t_s[ti][:],
                                    wait=[("v", vC)])]
                    if stage == "ln1":
                        x1tok.append(w.dmal("g", dbg["x1"][tok0:tok0 + 128, :],
                                            x1t_s[ti][:]))
                    w.marks[f"x1t{ti}"] = x1tok
            w.barrier()
            return False

        def slots_phase(s_idx):
            tok0 = s_idx * S
            w.do("v", lambda E: (E.memset(carr_s[:], 0.0), E.memset(onesr_s[:], 1.0),
                                 E.drain())[-1], inc=("v", 1))
            tr_tok = w.dmal("q", trrow_s[:], trrow_d.ap())
            for tg in range(NTG):
                for tl4 in range(4):
                    tile = tg * 4 + tl4
                    ti = tile % 2
                    wmark("q", f"x1t{ti}")
                    xt_tok = w.dmal("q", x1t_s[ti][:],
                                    x1f_d[tok0 + tile * 128: tok0 + (tile + 1) * 128, :])
                    tb = 2 * (tile % 2)
                    for c in range(NCH):
                        sl = PA[:, (c % 4) * 512:(c % 4) * 512 + 128]
                        wmark("t", f"pas{c % 4}")
                        w.do("t", lambda E, sl=sl, ti=ti, c=c: E.transpose(
                            sl, x1t_s[ti][:, c * 128:(c + 1) * 128], id32_s[:]),
                             inc=("t", 1), wait=[xt_tok])
                        tC = C["t"]
                        wmark("s", f"x1Tt{tb}")
                        w.do("s", lambda E, c=c, sl=sl, tb=tb: E.activation(
                            x1Tt_s[tb][:, c * 128:(c + 1) * 128], sl, AF.Copy),
                             inc=("s", 1), wait=[("t", tC)])
                        sC = C["s"]
                        wmark("v", f"x1Tt{tb + 1}")
                        w.do("v", lambda E, c=c, sl=sl, tb=tb: E.tensor_tensor(
                            x1Tt_s[tb + 1][:, c * 128:(c + 1) * 128], sl,
                            x1Tt_s[tb][:, c * 128:(c + 1) * 128], ALU.subtract),
                             inc=("v", 1), wait=[("s", sC)])
                        mark(f"pas{c % 4}", "v")
                    tile_s, tile_v = C["s"], C["v"]
                    if tl4 == 0:
                        wmark("t", "lgbank")
                    first = True
                    for dc in range(NCH):
                        lh = wrh_s[:, dc * 4:(dc + 1) * 4]
                        ll = wrl_s[:, dc * 4:(dc + 1) * 4]
                        xh = x1Tt_s[tb][:, dc * 128:(dc + 1) * 128]
                        xl = x1Tt_s[tb + 1][:, dc * 128:(dc + 1) * 128]
                        dstp = PB[0:4, tl4 * 128:(tl4 + 1) * 128]
                        w.mm(dstp, lh, xh, start=first, stop=False,
                             wait=[("s", tile_s), ("v", tile_v)])
                        first = False
                        w.mm(dstp, lh, xl, start=False, stop=False)
                        w.mm(dstp, ll, xh, start=False, stop=(dc == NCH - 1),
                             inc=("t", 1) if dc == NCH - 1 else None)
                    mark(f"x1Tt{tb}", "t")
                    mark(f"x1Tt{tb + 1}", "t")
                    mark(f"x1t{ti}", "t")
                tC = C["t"]
                wmark("v", "lgs")
                w.do("v", lambda E: E.tensor_copy(lgs_s[:], PB[0:4, 0:512]),
                     inc=("v", 1), wait=[("t", tC)])
                vC = C["v"]
                mark("lgbank", "v")
                wmark("q", "lrow")
                lr_tok = w.dmal("q", lrow_s[0:1, :].rearrange("a (p f) -> a p f", p=4),
                                lgs_s[:], wait=[("v", vC)])
                w.marks["lgs"] = [lr_tok]

                def rowops(E, tg=tg, s_idx=s_idx):
                    L, R = lrow_s, rtmp_s
                    CSUM = R[:, 4 * TG:5 * TG]
                    SLV = R[:, 5 * TG:6 * TG]
                    OV = R[:, 6 * TG:7 * TG]
                    E.tensor_tensor(R[:, 0:TG], L[:, 0:TG], L[:, TG:2 * TG], ALU.max)
                    E.tensor_tensor(R[:, TG:2 * TG], L[:, 2 * TG:3 * TG],
                                    L[:, 3 * TG:4 * TG], ALU.max)
                    E.drain()
                    E.tensor_tensor(R[:, 2 * TG:3 * TG], R[:, 0:TG], R[:, TG:2 * TG],
                                    ALU.max)
                    E.drain()
                    E.memset(R[:, 0:TG], 1.0)
                    E.memset(slotr_s[:], 0.0)
                    E.drain()
                    for e in range(NE):
                        ge = L[:, e * TG:(e + 1) * TG]
                        E.tensor_tensor(ge, ge, R[:, 2 * TG:3 * TG], ALU.is_ge)
                    E.drain()
                    for e in range(NE):
                        ge = L[:, e * TG:(e + 1) * TG]
                        me = R[:, 3 * TG:4 * TG]
                        E.tensor_tensor(me, ge, R[:, 0:TG], ALU.mult)
                        E.drain()
                        E.tensor_tensor_scan(CSUM, onesr_s[:], me,
                                             carr_s[:, e:e + 1], ALU.mult, ALU.add)
                        E.drain()
                        E.tensor_copy(carr_s[:, e:e + 1], CSUM[:, TG - 1:TG])
                        E.tensor_single_scalar(
                            SLV, CSUM, float(s_idx * SLOTS + e * CCAP - 1), ALU.add)
                        E.tensor_single_scalar(ovi_s[:], CSUM, float(CCAP), ALU.is_gt)
                        E.drain()
                        E.copy_predicated(SLV, ovi_s[:], trrow_s[:])
                        E.drain()
                        E.tensor_tensor(SLV, SLV, me, ALU.mult)
                        E.drain()
                        E.tensor_tensor(slotr_s[:], slotr_s[:], SLV, ALU.add)
                        E.drain()
                        if e < NE - 1:
                            E.tensor_tensor(R[:, 0:TG], R[:, 0:TG], me, ALU.subtract)
                            E.drain()
                    return E.drain()
                wmark("v", "slotr")
                w.do("v", rowops, inc=("v", 1), wait=[lr_tok, tr_tok])
                mark("lrow", "v")
                sr_v = C["v"]
                cb = s_idx * 16 + tg * 4
                row = s_idx * NTG + tg
                st_tok = w.dmal("g", slot_d[row:row + 1, :], slotr_s[:],
                                wait=[("v", sr_v)])
                w.marks["slotr"] = [st_tok]
                w.do("q", lambda E, cb=cb, row=row: E.dma_start(
                    out=idxf_s[:, cb:cb + 4],
                    in_=slot_d[row:row + 1, :].rearrange("a (c p) -> (a p) c", p=128),
                    allow_slow_non_contiguous=True),
                     inc=(QL[0], 16), wait=[(QL[0], C[QL[0]]), st_tok])
            idx_tok = (QL[0], C[QL[0]])
            w.do("v", lambda E, s_idx=s_idx: E.tensor_copy(
                idxi_s[:, s_idx * 16:(s_idx + 1) * 16],
                idxf_s[:, s_idx * 16:(s_idx + 1) * 16]),
                 inc=("v", 1), wait=[idx_tok])
            w.drain("v")
            idx_v = C["v"]

            if stage == "slots" and s_idx == 0:
                w.dmal("g", dbg["idx"][:, 0:16], idxf_s[:, 0:16], wait=[("v", idx_v)])
                w.waitpool("g", "g")
                return True

            from concourse.bass_core_rust import IndirectOffsetOnAxis
            for tile in range(16):
                ti = tile % 2
                wmark("q", f"x1t{ti}")
                xt_tok = w.dmal("q", x1t_s[ti][:],
                                x1f_d[tok0 + tile * 128: tok0 + (tile + 1) * 128, :])
                col = s_idx * 16 + tile
                w.do("g", lambda E, ti=ti, col=col: E.indirect_dma_start(
                    xg_d[:, :],
                    IndirectOffsetOnAxis(idxi_s[:, col:col + 1], 0),
                    x1t_s[ti][:], None),
                     inc=(GL[tile % 4], 16),
                     wait=[(GL[tile % 4], C[GL[tile % 4]]), xt_tok, ("v", idx_v)]
                     + zf_tok)
                w.marks[f"x1t{ti}"] = [(GL[tile % 4], C[GL[tile % 4]])]
            sc_tok = gsn()

            for st in range(SLOTS // 128):
                ti = st % 2
                wmark("q", f"x1t{ti}")
                xt_tok = w.dmal("q", x1t_s[ti][:],
                                xg_d[s_idx * SLOTS + st * 128:
                                     s_idx * SLOTS + (st + 1) * 128, :],
                                wait=sc_tok)
                col0 = st * 128
                for c in range(NCH):
                    sl = PA[:, (c % 4) * 512:(c % 4) * 512 + 128]
                    wmark("t", f"pas{c % 4}")
                    w.do("t", lambda E, sl=sl, ti=ti, c=c: E.transpose(
                        sl, x1t_s[ti][:, c * 128:(c + 1) * 128], id32_s[:]),
                         inc=("t", 1), wait=[xt_tok])
                    tC = C["t"]
                    wmark("s", f"xgTh{c}")
                    w.do("s", lambda E, c=c, sl=sl, col0=col0: E.activation(
                        xgTh_s[c][:, col0:col0 + 128], sl, AF.Copy),
                         inc=("s", 1), wait=[("t", tC)])
                    sC = C["s"]
                    wmark("v", f"xgTl{c}")
                    w.do("v", lambda E, c=c, sl=sl, col0=col0: E.tensor_tensor(
                        xgTl_s[c][:, col0:col0 + 128], sl,
                        xgTh_s[c][:, col0:col0 + 128], ALU.subtract),
                         inc=("v", 1), wait=[("s", sC)])
                    mark(f"pas{c % 4}", "v")
                mark(f"x1t{ti}", "t")
            w.barrier()
            return False

        def ffn_phase(s_idx):
            slot0 = s_idx * SLOTS
            sl_tok = {}

            def emit_w1split(ic):
                ip = ic % 2
                for e in range(NE):
                    sb32 = slab32_s[e % 2]
                    wmark("q", f"slab32_{e % 2}")
                    dtok = w.dmal("q", sb32[:], w1_d[e, ic])
                    wmark("s", f"slabh{ip}_{e}")
                    w.do("s", lambda E, e=e, ip=ip, sb32=sb32: E.activation(
                        slabh_s[ip][e][:], sb32[:], AF.Copy), inc=("s", 1),
                         wait=[dtok])
                    sC = C["s"]
                    wmark("v", f"slabl{ip}_{e}")
                    w.do("v", lambda E, e=e, ip=ip, sb32=sb32: E.tensor_tensor(
                        slabl_s[ip][e][:], sb32[:], slabh_s[ip][e][:], ALU.subtract),
                         inc=("v", 1), wait=[("s", sC)])
                    mark(f"slab32_{e % 2}", "v")
                sl_tok[ic] = (C["v"], C["s"])

            emit_w1split(0)
            for ic in range(NIC):
                ip = ic % 2
                if ic + 1 < NIC:
                    emit_w1split(ic + 1)
                slab_v, slab_s_cnt = sl_tok[ic]
                for e in range(NE):
                    for gr in range(CCAP // GR):
                        c0 = e * CCAP + gr * GR
                        ps = PA[:, (2 * e + gr) % 4 * 512:((2 * e + gr) % 4) * 512 + GR]
                        wmark("t", f"fps{(2 * e + gr) % 4}")
                        for dc in range(NCH):
                            lh = slabh_s[ip][e][:, dc * 128:(dc + 1) * 128]
                            ll = slabl_s[ip][e][:, dc * 128:(dc + 1) * 128]
                            xh = xgTh_s[dc][:, c0:c0 + GR]
                            xl = xgTl_s[dc][:, c0:c0 + GR]
                            w.mm(ps, lh, xh, start=(dc == 0), stop=False,
                                 wait=[("v", slab_v), ("s", slab_s_cnt)])
                            w.mm(ps, lh, xl, start=False, stop=False)
                            w.mm(ps, ll, xh, start=False, stop=(dc == NCH - 1),
                                 inc=("t", 1) if dc == NCH - 1 else None)
                        tC = C["t"]
                        gp = (2 * e + gr) % 2
                        wmark("s", f"gfull{gp}")
                        w.do("s", lambda E, ps=ps, e=e, ic=ic, gp=gp: E.activation(
                            gf_s[gp][:], ps, AF.Gelu,
                            bias=b1_s[:, e * 32 + ic:e * 32 + ic + 1]),
                             inc=("s", 1), wait=[("t", tC)])
                        mark(f"fps{(2 * e + gr) % 4}", "s")
                        w.drain("s")
                        wmark("s", f"ghisb{gp}")
                        w.do("s", lambda E, gp=gp: E.activation(
                            ghi_sb[gp][:], gf_s[gp][:], AF.Copy), inc=("s", 1))
                        sC = C["s"]
                        wmark("v", f"glosb{gp}")
                        w.do("v", lambda E, gp=gp: E.tensor_tensor(
                            glo_sb[gp][:], gf_s[gp][:], ghi_sb[gp][:], ALU.subtract),
                             inc=("v", 1), wait=[("s", sC)])
                        vC = C["v"]
                        mark(f"gfull{gp}", "v")
                        gh_tok = w.dmal("g", ghi_d[ic, :, slot0 + c0: slot0 + c0 + GR],
                                        ghi_sb[gp][:], wait=[("s", sC), ("v", vC)])
                        gl_tok = w.dmal("g", glo_d[ic, :, slot0 + c0: slot0 + c0 + GR],
                                        glo_sb[gp][:])
                        w.marks[f"ghisb{gp}"] = [gh_tok]
                        w.marks[f"glosb{gp}"] = [gl_tok]
                for e in range(NE):
                    w.marks[f"slabh{ip}_{e}"] = [("t", C["t"])]
                    w.marks[f"slabl{ip}_{e}"] = [("t", C["t"])]
            w.barrier()

        def wo_phase():
            for dst, srcp in ((gam_s, gam_d), (bet_s, bet_d), (bo_s, bo_d),
                              (ones512_s, ones512_d.ap())):
                w.dmal("q", dst[:], srcp)
            wo_tok = qsn()
            sp_tok = {}
            from concourse.bass_core_rust import IndirectOffsetOnAxis

            NWB = SLOTS // WB
            tl_toks = {}

            def emit_tail_load(s_idx, tile, ygs_tok):
                ti = tile % 2
                col = s_idx * 16 + tile
                tok0 = s_idx * S + tile * 128
                gl = GL[2 + ti]
                wmark("v", f"tlg{ti}")
                w.do("g", lambda E, ti=ti, col=col: E.indirect_dma_start(
                    x1t_s[ti][:], None,
                    yg_d[:, :],
                    IndirectOffsetOnAxis(idxi_s[:, col:col + 1], 0)),
                     inc=(gl, 16),
                     wait=[(gl, C[gl])] + ygs_tok)
                yr_tok = (gl, C[gl])
                wmark("q", f"x1r{ti}")
                x1r_tok = w.dmal("q", x1r_s[ti][:], x1f_d[tok0:tok0 + 128, :])
                tl_toks[(s_idx, tile)] = (yr_tok, x1r_tok)

            def emit_tail_chain(s_idx, tile):
                ti = tile % 2
                tok0 = s_idx * S + tile * 128
                yr_tok, x1r_tok = tl_toks.pop((s_idx, tile))
                wmark("v", f"z{ti}")
                wmark("v", f"outt{ti}")
                w.do("v", lambda E, ti=ti: E.scalar_tensor_tensor(
                    z_s[ti][:], x1t_s[ti][:], 1.0, x1r_s[ti][:],
                    ALU.mult, ALU.add, accum_out=vec_s[:, 12:13]),
                     inc=("v", 1), wait=[yr_tok, x1r_tok])
                vC = C["v"]
                mark(f"tlg{ti}", "v")
                w.do("v", lambda E, ti=ti: E.tensor_tensor(
                    outt_s[ti][:], bet_s[:], x1r_s[ti][:], ALU.add), inc=("v", 1))
                wmark("s", "sqd")
                w.do("s", lambda E, ti=ti: E.activation(
                    sqd_s[:], z_s[ti][:], AF.Square, accum_out=vec_s[:, 13:14]),
                     wait=[("v", vC)])
                w.do("s", lambda E: E.mul(vec_s[:, 1:2], vec_s[:, 12:13], 1.0 / D))
                w.drain("s")
                w.do("s", lambda E: E.square(vec_s[:, 2:3], vec_s[:, 1:2]),
                     inc=("s", 1))
                sC = C["s"]
                mark("sqd", "s")
                w.do("v", lambda E: E.scalar_tensor_tensor(
                    vec_s[:, 3:4], vec_s[:, 13:14], 1.0 / D, vec_s[:, 2:3],
                    ALU.mult, ALU.subtract), inc=("v", 1), wait=[("s", sC)])
                vC = C["v"]
                w.do("s", lambda E: E.activation(
                    vec_s[:, 4:5], vec_s[:, 3:4], AF.Sqrt, bias=eps12_s[:]),
                     inc=("s", 1), wait=[("v", vC)])
                sC = C["s"]
                w.drain("v")
                w.do("v", lambda E: E.reciprocal(vec_s[:, 5:6], vec_s[:, 4:5]),
                     wait=[("s", sC)])
                w.drain("v")
                wmark("v", "sqd")
                w.do("v", lambda E, ti=ti: E.tensor_scalar(
                    sqd_s[:], z_s[ti][:], vec_s[:, 1:2], vec_s[:, 5:6],
                    ALU.subtract, ALU.mult))
                w.drain("v")
                w.do("v", lambda E: E.tensor_tensor(
                    sqd_s[:], sqd_s[:], gam_s[:], ALU.mult))
                w.drain("v")
                w.do("v", lambda E, ti=ti: E.scalar_tensor_tensor(
                    z_s[ti][:], sqd_s[:], 1.0, outt_s[ti][:], ALU.mult, ALU.add,
                    accum_out=vec_s[:, 14:15]), inc=("v", 1))
                vC = C["v"]
                mark(f"x1r{ti}", "v")
                mark("sqd", "v")
                w.do("s", lambda E, ti=ti: E.activation(
                    sqd_s[:], z_s[ti][:], AF.Square, accum_out=vec_s[:, 15:16]),
                     wait=[("v", vC)])
                w.do("s", lambda E: E.mul(vec_s[:, 1:2], vec_s[:, 14:15], 1.0 / D))
                w.drain("s")
                w.do("s", lambda E: E.square(vec_s[:, 2:3], vec_s[:, 1:2]),
                     inc=("s", 1))
                sC = C["s"]
                mark("sqd", "s")
                w.do("v", lambda E: E.scalar_tensor_tensor(
                    vec_s[:, 3:4], vec_s[:, 15:16], 1.0 / D, vec_s[:, 2:3],
                    ALU.mult, ALU.subtract), inc=("v", 1), wait=[("s", sC)])
                vC = C["v"]
                w.do("s", lambda E: E.activation(
                    vec_s[:, 4:5], vec_s[:, 3:4], AF.Sqrt, bias=eps5_s[:]),
                     inc=("s", 1), wait=[("v", vC)])
                sC = C["s"]
                w.drain("v")
                w.do("v", lambda E: E.reciprocal(vec_s[:, 5:6], vec_s[:, 4:5]),
                     wait=[("s", sC)])
                w.drain("v")
                w.do("v", lambda E, ti=ti: E.tensor_scalar(
                    outt_s[ti][:], z_s[ti][:], vec_s[:, 1:2], vec_s[:, 5:6],
                    ALU.subtract, ALU.mult), inc=("v", 1))
                vC = C["v"]
                mark(f"z{ti}", "v")
                o_tok = w.dmal("g", out_d[tok0:tok0 + 128, :], outt_s[ti][:],
                               wait=[("v", vC)])
                w.marks[f"outt{ti}"] = [o_tok]

            tail0n = [0]
            ygs0 = [None]
            for s_idx in range(NSEQ):
              slot0 = s_idx * SLOTS
              for wb in range(NWB):
                g_ic_tok = {}
                for ic in range(NIC):
                    wmark("q", f"gsl{ic}")
                    t1 = w.dmal("q", gslh_s[ic][:],
                                ghi_d[ic, :, slot0 + wb * WB:slot0 + (wb + 1) * WB])
                    t2 = w.dmal("q", gsll_s[ic][:],
                                glo_d[ic, :, slot0 + wb * WB:slot0 + (wb + 1) * WB])
                    g_ic_tok[ic] = [t1, t2]

                def emit_wosplit(jc):
                    jp = jc % 2
                    wmark("s", f"wjh{jp}")
                    wmark("v", f"wjl{jp}")
                    for q4 in range(4):
                        sb32 = slab32_s[q4 % 2]
                        wmark("q", f"slab32_{q4 % 2}")
                        wsl_tok = w.dmal("q", sb32[:],
                                         wo_d[jc, :, q4 * 8:(q4 + 1) * 8, :])
                        w.do("s", lambda E, sb32=sb32, jp=jp, q4=q4: E.activation(
                            wjh_s[jp][:, q4 * 1024:(q4 + 1) * 1024], sb32[:], AF.Copy),
                             inc=("s", 1), wait=[wsl_tok])
                        sC = C["s"]
                        w.do("v", lambda E, sb32=sb32, jp=jp, q4=q4: E.tensor_tensor(
                            wjl_s[jp][:, q4 * 1024:(q4 + 1) * 1024], sb32[:],
                            wjh_s[jp][:, q4 * 1024:(q4 + 1) * 1024], ALU.subtract),
                             inc=("v", 1), wait=[("s", sC)])
                        mark(f"slab32_{q4 % 2}", "v")
                    sp_tok[jc] = (C["v"], C["s"])

                if wb == 0:
                    emit_wosplit(0)
                for jc in range(NCH):
                    jp = jc % 2
                    if not (jc == NCH - 1 and wb == NWB - 1):
                        emit_wosplit((jc + 1) % NCH)
                    wo_v, wo_s_cnt = sp_tok[jc]
                    ps = PB[:, jp * 512:jp * 512 + WB]
                    wmark("t", f"pbb{jp}")
                    for ic in range(NIC):
                        w.mm(ps, wjh_s[jp][:, ic * 128:(ic + 1) * 128], gslh_s[ic][:],
                             start=(ic == 0), stop=False,
                             wait=[("v", wo_v), ("s", wo_s_cnt)] + g_ic_tok[ic]
                                  + wo_tok)
                        w.mm(ps, wjh_s[jp][:, ic * 128:(ic + 1) * 128], gsll_s[ic][:],
                             start=False, stop=False)
                        w.mm(ps, wjl_s[jp][:, ic * 128:(ic + 1) * 128], gslh_s[ic][:],
                             start=False, stop=False)
                    w.mm(ps, bo_s[:, jc * 128:(jc + 1) * 128], ones512_s[:, 0:WB],
                         start=False, stop=True, inc=("t", 1))
                    tC = C["t"]
                    mark(f"wjh{jp}", "t")
                    mark(f"wjl{jp}", "t")
                    wmark("s", "ysb")
                    w.do("s", lambda E, ps=ps: E.activation(ysb_s[:], ps, AF.Copy),
                         inc=("s", 1), wait=[("t", tC)])
                    sC = C["s"]
                    mark(f"pbb{jp}", "s")
                    for quarter in range(4):
                        sl = PC[:, (quarter % 2) * 512:(quarter % 2) * 512 + 128]
                        wmark("t", f"pc{quarter % 2}")
                        w.do("t", lambda E, sl=sl, quarter=quarter: E.transpose(
                            sl, ysb_s[:, quarter * 128:quarter * 128 + 128], id32_s[:]),
                             inc=("t", 1), wait=[("s", sC)])
                        tC = C["t"]
                        wmark("v", f"yrow{quarter}")
                        w.do("v", lambda E, quarter=quarter, jc=jc, sl=sl: E.tensor_copy(
                            yrow_s[quarter][:, jc * 128:(jc + 1) * 128], sl),
                             inc=("v", 1), wait=[("t", tC)])
                        mark(f"pc{quarter % 2}", "v")
                    mark("ysb", "t")
                    if s_idx == 1:
                        gidx = wb * NCH + jc
                        if gidx >= 2 and gidx % 2 == 0 and tail0n[0] < 16:
                            t0 = tail0n[0]
                            if t0 > 0:
                                emit_tail_chain(0, t0 - 1)
                            emit_tail_load(0, t0, ygs0[0])
                            tail0n[0] += 1
                for ic in range(NIC):
                    w.marks[f"gsl{ic}"] = [("t", C["t"])]
                y_v = C["v"]
                for quarter in range(4):
                    r0 = slot0 + wb * WB + quarter * 128
                    yg_tok = w.dmal("g", yg_d[r0:r0 + 128, :], yrow_s[quarter][:],
                                    wait=[("v", y_v)])
                    w.marks[f"yrow{quarter}"] = [yg_tok]
              if s_idx == 0:
                  ygs0[0] = gsn()
            emit_tail_chain(0, 15)
            ygs1 = gsn()
            emit_tail_load(1, 0, ygs1)
            for tile in range(16):
                if tile + 1 < 16:
                    emit_tail_load(1, tile + 1, ygs1)
                emit_tail_chain(1, tile)
            w.waitpool("g", "g")

        for s_idx in range(NSEQ):
            qkv_phase(s_idx)
            if stage == "qkv" and s_idx == 0:
                vv, ss_ = C["v"], C["s"]
                for c in range(NCH):
                    w.dmal("g", dbg["qT"][c * 128:(c + 1) * 128, :], qT_s[c][:],
                           wait=[("v", vv), ("s", ss_)])
                    w.dmal("g", dbg["kT"][c * 128:(c + 1) * 128, :], kT_s[c][:])
                w.dmal("g", dbg["vp"][:], vp_s[:])
                w.waitpool("g", "g")
                return
            if attn_phase(s_idx):
                return
            if slots_phase(s_idx):
                return
            ffn_phase(s_idx)
        if stage == "ffn":
            w.waitpool("g", "g")
            return
        wo_phase()

    with nc.Block() as block:
        @block.sync
        def _(eng):
            walk("q", eng)

        @block.gpsimd
        def _(eng):
            walk("g", eng)

        @block.tensor
        def _(eng):
            walk("t", eng)

        @block.scalar
        def _(eng):
            walk("s", eng)

        @block.vector
        def _(eng):
            walk("v", eng)

    return nc


def _stage_inputs(inputs):
    x = np.asarray(inputs["L0_attn_x"])
    qw = np.asarray(inputs["L0_attn_qw"])
    kw = np.asarray(inputs["L0_attn_kw"])
    vw = np.asarray(inputs["L0_attn_vw"])
    ow = np.asarray(inputs["L0_attn_ow"])
    wr = np.asarray(inputs["L0_ffn_W_router"])
    w1 = np.asarray(inputs["L0_ffn_W1"])
    b1 = np.asarray(inputs["L0_ffn_b1"])
    wo = np.asarray(inputs["L0_ffn_Wo"])
    bo = np.asarray(inputs["L0_ffn_bo"])
    gam = np.asarray(inputs["L0_ffn_ln_gamma"])
    bet = np.asarray(inputs["L0_ffn_ln_beta"])

    perm = []
    for qd in range(4):
        for par in range(2):
            for m in range(4):
                h = 4 * qd + m
                for i in range(32):
                    perm.append(64 * h + 2 * i + par)
    perm = np.array(perm)
    qwp = np.ascontiguousarray(qw[:, perm])
    kwp = np.ascontiguousarray(kw[:, perm])
    vwp = np.zeros((D, 1040), np.float16)
    for h in range(NH):
        vwp[:, 65 * h:65 * h + 64] = vw[:, 64 * h:64 * h + 64]
    wrc = np.ascontiguousarray(
        wr.reshape(8, 128, 4).transpose(1, 0, 2).reshape(128, 32)).astype(np.float32)
    w1r = np.ascontiguousarray(
        w1.reshape(NE, 8, 128, NIC, 128).transpose(0, 3, 2, 1, 4)
        .reshape(NE, NIC, 128, 1024)).astype(np.float32)
    b1c = np.ascontiguousarray(
        b1.reshape(NE, NIC, 128).transpose(2, 0, 1).reshape(128, NE * NIC)
    ).astype(np.float32)
    gamt = np.ascontiguousarray(np.broadcast_to(gam[None, :], (128, D))).astype(np.float32)
    bett = np.ascontiguousarray(np.broadcast_to(bet[None, :], (128, D))).astype(np.float32)
    bor = np.ascontiguousarray(bo[None, :]).astype(np.float32)
    wor2 = np.ascontiguousarray(
        wo.reshape(NIC, 128, NCH, 128).transpose(2, 1, 0, 3)).astype(np.float32)

    in_maps = []
    for c in range(8):
        xc = x[2 * c:2 * c + 2]
        xrows = np.ascontiguousarray(xc.reshape(T, D))
        xT = np.ascontiguousarray(xc.reshape(T, D).T)
        in_maps.append({
            "xT": xT, "xrows": xrows,
            "qwp": qwp, "kwp": kwp, "vwp": vwp, "oww": np.ascontiguousarray(ow),
            "wrc": wrc, "w1r": w1r, "b1c": b1c,
            "wor2": wor2, "bor": bor, "gamt": gamt, "bett": bett,
        })
    return in_maps


def kernel(**inputs):
    in_maps = _stage_inputs(inputs)
    nc = build(stage="full", debug=False)
    res = run_bass_kernel_spmd(nc, in_maps, core_ids=list(range(8)))
    out = np.concatenate([np.asarray(r["out"]) for r in res.results], axis=0)
    return out.reshape(16, 2048, 1024).astype(np.float32)


# version 17 (final)
# speedup vs baseline: 1.1815x; 1.0089x over previous
import numpy as np

import concourse.bass as bass
import concourse.mybir as mybir
from concourse.bass_utils import run_bass_kernel_spmd

F32 = mybir.dt.float32
F16 = mybir.dt.float16
AF = mybir.ActivationFunctionType
ALU = mybir.AluOpType

D = 1024
S = 2048
NSEQ = 2
T = NSEQ * S
NH = 16
NCH = 8
TG = 512
NTG = S // TG
NE = 4
DFF = 4096
NIC = DFF // 128
SCALE = 1.0 / 32.0
MASKVAL = -30000.0
LN_EPS_OUT = 1e-5
LN_EPS_IN = 1e-12
CCAP = 640
GR = 320
SLOTS = NE * CCAP
TRASH = NSEQ * SLOTS
NXG = TRASH + 128

CAP = 205312


def _round32(x):
    return (x + 31) // 32 * 32


class Arena:

    _ctr = [0]

    def __init__(self, nc, cap):
        self.nc = nc
        self.cap = cap
        self.free = [(0, cap)]
        self.live = {}
        Arena._ctr[0] += 1
        self.backing = nc.alloc_sbuf_tensor(f"arena{Arena._ctr[0]}", [128, cap],
                                            mybir.dt.uint8)
        self.base_addr = nc.lookup_mloc(self.backing).addr

    def alloc(self, name, shape, dtype):
        per_part = int(np.prod(shape[1:])) * mybir.dt.size(dtype)
        need = _round32(per_part)
        for idx, (off, sz) in enumerate(self.free):
            if sz >= need:
                self.free[idx] = (off + need, sz - need)
                if self.free[idx][1] == 0:
                    self.free.pop(idx)
                h = self.nc.alloc_sbuf_tensor_at(name, list(shape), dtype,
                                                 offset=self.base_addr + off)
                self.live[name] = (off, need)
                return h
        raise MemoryError(f"sbuf arena: no room for {name} ({need}B); free={self.free}")

    def freebuf(self, *names):
        for name in names:
            off, sz = self.live.pop(name)
            self.free.append((off, sz))
        self.free.sort()
        out = []
        for off, sz in self.free:
            if out and out[-1][0] + out[-1][1] == off:
                out[-1] = (out[-1][0], out[-1][1] + sz)
            else:
                out.append((off, sz))
        self.free = [tuple(x) for x in out]


class Walk:

    def __init__(self, me, handle, sems):
        self.me = me
        self.E = handle
        self.sems = sems
        self.cnt = {k: 0 for k in sems}
        self.last_wait = {}
        self.marks = {}

    def wait(self, eng, *pairs):
        if self.me == eng:
            for s, v in pairs:
                if v <= 0 or self.last_wait.get(s, -1) >= v:
                    continue
                self.E.wait_ge(self.sems[s], v)
                self.last_wait[s] = v

    def do(self, eng, fn, inc=None, wait=None):
        if self.me == eng:
            if wait:
                self.wait(eng, *wait)
            inst = fn(self.E)
            if inc:
                inst.then_inc(self.sems[inc[0]], inc[1])
        if inc:
            self.cnt[inc[0]] += inc[1]

    def drain(self, eng):
        if self.me == eng:
            self.E.drain()

    def dma(self, eng, out, in_, sem, wait=None):
        self.do(eng, lambda E: E.dma_start(out=out, in_=in_), inc=(sem, 16), wait=wait)

    def dmal(self, eng, out, in_, wait=None):
        pool = self.pools[eng]
        lane = pool[self.rr[eng] % len(pool)]
        self.rr[eng] += 1
        ws = [(lane, self.cnt[lane])]
        if wait:
            ws += list(wait)
        self.do(eng, lambda E: E.dma_start(out=out, in_=in_), inc=(lane, 16), wait=ws)
        return (lane, self.cnt[lane])

    def waitpool(self, eng, pool_of):
        for lane in self.pools[pool_of]:
            self.wait(eng, (lane, self.cnt[lane]))

    def mm(self, out, lhsT, rhs, start, stop, tp=None, wait=None, inc=None):
        self.do("t",
                lambda E: E.matmul(out, lhsT, rhs, start=start, stop=stop,
                                   tile_position=tp),
                inc=inc, wait=wait)

    def mark(self, name, sem):
        self.marks[name] = [(sem, self.cnt[sem])]

    def wmark(self, eng, name):
        for s, v in self.marks.get(name, ()):
            self.wait(eng, (s, v))

    def barrier(self):
        snap = [(s, v) for s, v in self.cnt.items() if v > 0]
        for eng in ("q", "g", "t", "s", "v"):
            self.wait(eng, *snap)


def build(stage="full", debug=False):
    nc = bass.Bass("TRN2", target_bir_lowering=False)

    xT_d = nc.dram_tensor("xT", [D, T], F16, kind="ExternalInput").ap()
    xrows_d = nc.dram_tensor("xrows", [T, D], F16, kind="ExternalInput").ap()
    qwp_d = nc.dram_tensor("qwp", [D, D], F16, kind="ExternalInput").ap()
    kwp_d = nc.dram_tensor("kwp", [D, D], F16, kind="ExternalInput").ap()
    vwp_d = nc.dram_tensor("vwp", [D, 1040], F16, kind="ExternalInput").ap()
    ow_d = nc.dram_tensor("oww", [D, D], F16, kind="ExternalInput").ap()
    wr_d = nc.dram_tensor("wrc", [128, 32], F32, kind="ExternalInput").ap()
    w1_d = nc.dram_tensor("w1r", [NE, NIC, 128, 1024], F32, kind="ExternalInput").ap()
    b1_d = nc.dram_tensor("b1c", [128, 128], F32, kind="ExternalInput").ap()
    wo_d = nc.dram_tensor("wor2", [NCH, 128, NIC, 128], F32, kind="ExternalInput").ap()
    bo_d = nc.dram_tensor("bor", [1, D], F32, kind="ExternalInput").ap()
    gam_d = nc.dram_tensor("gamt", [128, D], F32, kind="ExternalInput").ap()
    bet_d = nc.dram_tensor("bett", [128, D], F32, kind="ExternalInput").ap()
    out_d = nc.dram_tensor("out", [T, D], F32, kind="ExternalOutput").ap()

    x1f_d = nc.dram_tensor("x1f", [T, D], F32, kind="Internal").ap()
    ghi_d = nc.dram_tensor("ghi", [NIC, 128, NSEQ * SLOTS], F16, kind="Internal").ap()
    glo_d = nc.dram_tensor("glo", [NIC, 128, NSEQ * SLOTS], F16, kind="Internal").ap()
    xg_d = nc.dram_tensor("xg", [NXG, D], F32, kind="Internal").ap()
    yg_d = nc.dram_tensor("yg", [NXG, D], F32, kind="Internal").ap()
    slot_d = nc.dram_tensor("slotd", [NSEQ * NTG, TG], F32, kind="Internal").ap()

    dbg = {}
    if debug and stage == "qkv":
        dbg["qT"] = nc.dram_tensor("dbg_qT", [D, S], F16, kind="ExternalOutput").ap()
        dbg["kT"] = nc.dram_tensor("dbg_kT", [D, S], F16, kind="ExternalOutput").ap()
        dbg["vp"] = nc.dram_tensor("dbg_vp", [128, 16 * 1040], F16, kind="ExternalOutput").ap()
    if debug and stage == "attn":
        dbg["at_hi"] = nc.dram_tensor("dbg_athi", [D, TG], F16, kind="ExternalOutput").ap()
        dbg["at_lo"] = nc.dram_tensor("dbg_atlo", [D, TG], F16, kind="ExternalOutput").ap()
    if debug and stage == "ln1":
        dbg["x1"] = nc.dram_tensor("dbg_x1", [T, D], F32, kind="ExternalOutput").ap()
    if debug and stage == "slots":
        dbg["idx"] = nc.dram_tensor("dbg_idx", [128, 32], F32, kind="ExternalOutput").ap()

    pos = np.arange(S, dtype=np.float64)
    inv = 10000.0 ** (-np.arange(32, dtype=np.float64) / 32.0)
    freq = inv[:, None] * pos[None, :]
    cc_d = nc.inline_tensor(np.tile(np.cos(freq), (4, 1)).astype(np.float32), "cc_c")
    ss_d = nc.inline_tensor(np.tile(np.sin(freq), (4, 1)).astype(np.float32), "ss_c")
    masks_np = np.zeros((4, 128, TG), np.float16)
    for di in range(4):
        k_idx = np.arange(128)[:, None] + di * 128
        q_idx = np.arange(TG)[None, :]
        masks_np[di][k_idx > q_idx] = np.float16(MASKVAL)
    mask_d = [nc.inline_tensor(masks_np[i], f"mask{i}_c") for i in range(4)]
    id16_d = nc.inline_tensor(np.eye(128, dtype=np.float16), "id16_c")
    id32_d = nc.inline_tensor(np.eye(128, dtype=np.float32), "id32_c")
    sel64_np = np.zeros((128, 128), np.float32)
    for _r in (0, 32, 64, 96):
        sel64_np[_r, :64] = 1.0
    sel64_d = nc.inline_tensor(sel64_np, "sel64_c")
    sel64b_np = np.zeros((1, 128), np.float32)
    sel64b_np[0, 64:] = 1.0
    sel64b_d = nc.inline_tensor(sel64b_np, "sel64b_c")
    ones128_d = nc.inline_tensor(np.ones((1, 128), np.float16), "ones128_c")
    ones512_d = nc.inline_tensor(np.ones((1, 512), np.float32), "ones512_c")
    trrow_np = (TRASH + (np.arange(512) % 128)).astype(np.float32)[None, :]
    trrow_d = nc.inline_tensor(trrow_np, "trrow_c")

    AP_ = Arena(nc, 40448)
    AR1 = Arena(nc, 63488)
    AR2 = Arena(nc, 98816)

    class Multi:

        def alloc(self, name, shape, dtype):
            try:
                return AR1.alloc(name, shape, dtype)
            except MemoryError:
                return AR2.alloc(name, shape, dtype)

    AC = Multi()

    id16_s = AP_.alloc("id16", [128, 128], F16)
    id32_s = AP_.alloc("id32", [128, 128], F32)
    sel64_s = AP_.alloc("sel64", [128, 128], F32)
    sel64b_s = AP_.alloc("sel64b", [1, 128], F32)
    ones128_s = AP_.alloc("ones128", [1, 128], F16)
    wr32_s = AP_.alloc("wr32", [128, 32], F32)
    wrh_s = AP_.alloc("wrh", [128, 32], F16)
    wrl_s = AP_.alloc("wrl", [128, 32], F16)
    b1_s = AP_.alloc("b1", [128, 128], F32)
    slab32_s = [AP_.alloc(f"slab32_{i}", [128, 1024], F32) for i in range(2)]
    xrow_s = [AP_.alloc(f"xrow{i}", [128, D], F16) for i in range(4)]
    z_s = [AP_.alloc(f"z{i}", [128, D], F32) for i in range(2)]
    sqd_s = AP_.alloc("sqd", [128, D], F32)
    vec_s = AP_.alloc("vecs", [128, 32], F32)
    x1t_s = [AP_.alloc(f"x1t{i}", [128, D], F32) for i in range(2)]
    eps5_s = AP_.alloc("eps5", [128, 1], F32)
    eps12_s = AP_.alloc("eps12", [128, 1], F32)
    idxf_s = AP_.alloc("idxf", [128, 32], F32)
    idxi_s = AP_.alloc("idxi", [128, 32], mybir.dt.int32)

    xts_s = [AR1.alloc(f"xts{c}", [128, S], F16) for c in range(NCH)]
    wch_s = [AR1.alloc(f"wch{c}", [128, 1040], F16) for c in range(NCH)]
    ccs_s = AR1.alloc("ccs", [128, TG], F32)
    sss_s = AR1.alloc("sss", [128, TG], F32)
    rp_s = [AR1.alloc(f"rp{i}", [128, TG], F32) for i in range(4)]
    AR1.freebuf(*([f"xts{c}" for c in range(NCH)] + [f"wch{c}" for c in range(NCH)]
                  + ["ccs", "sss"] + [f"rp{i}" for i in range(4)]))
    e32_s = [AR1.alloc(f"e32_{i}", [128, 1024], F32) for i in range(3)]
    _o0 = AP_.base_addr + AP_.live["slab32_0"][0]
    _o1 = AP_.base_addr + AP_.live["slab32_1"][0]
    ehi_s = [AR1.alloc(f"ehi_{i}", [128, 1024], F16) for i in range(2)] + [
        nc.alloc_sbuf_tensor_at("ehi_2", [128, 1024], F16, offset=_o0),
        nc.alloc_sbuf_tensor_at("ehi_3", [128, 1024], F16, offset=_o0 + 2048)]
    elo_s = [AR1.alloc(f"elo_{i}", [128, 1024], F16) for i in range(2)] + [
        nc.alloc_sbuf_tensor_at("elo_2", [128, 1024], F16, offset=_o1),
        nc.alloc_sbuf_tensor_at("elo_3", [128, 1024], F16, offset=_o1 + 2048)]
    rec_s = AR1.alloc("rec", [128, 1024], F32)

    def rec_ap(hh):
        if hh < 3:
            return rec_s[32 * hh:32 * hh + 1, 0:512]
        return rec_s[0:1, 512:1024]
    bsb_s = AR1.alloc("bsb", [64, 2048], F32)
    t32_s = AR1.alloc("t32", [128, TG], F32)
    athi_s = [AR1.alloc(f"athi{c}", [128, TG], F16) for c in range(NCH)]
    atlo_s = [AR1.alloc(f"atlo{c}", [128, TG], F16) for c in range(NCH)]
    owh_s = [AR1.alloc(f"owh{i}", [128, D], F16) for i in range(4)]
    msk_s = [AR1.alloc(f"msk{i}", [128, TG], F16) for i in range(4)]
    AR1.freebuf(*(["e32_0", "e32_1", "e32_2",
                   "ehi_0", "ehi_1", "elo_0", "elo_1",
                   "rec", "bsb", "t32"]
                  + [f"athi{c}" for c in range(NCH)] + [f"atlo{c}" for c in range(NCH)]
                  + [f"owh{i}" for i in range(4)] + [f"msk{i}" for i in range(4)]))
    lgs_s = AR1.alloc("lgs", [4, TG], F32)
    lrow_s = AR1.alloc("lrow", [1, NE * TG], F32)
    rtmp_s = AR1.alloc("rtmp", [1, 7 * TG], F32)
    onesr_s = AR1.alloc("onesr", [1, TG], F32)
    trrow_s = AR1.alloc("trrow", [1, TG], F32)
    carr_s = AR1.alloc("carr", [1, 32], F32)
    ovi_s = AR1.alloc("ovi", [1, TG], mybir.dt.int32)
    slotr_s = AR1.alloc("slotr", [1, TG], F32)
    AR1.freebuf("lgs", "lrow", "rtmp", "onesr", "trrow", "carr", "ovi", "slotr")
    slabh_s = [[AR1.alloc(f"slabh{i}_{e}", [128, 1024], F16) for e in range(NE)]
               for i in range(2)]
    slabl_s = [[AR1.alloc(f"slabl{i}_{e}", [128, 1024], F16) for e in range(NE)]
               for i in range(2)]
    gf_s = [AR1.alloc(f"gfull{i}", [128, GR], F32) for i in range(2)]
    ghi_sb = [AR1.alloc(f"ghisb{i}", [128, GR], F16) for i in range(2)]
    glo_sb = [AR1.alloc(f"glosb{i}", [128, GR], F16) for i in range(2)]
    AR1.freebuf(*[f"slabh{i}_{e}" for i in range(2) for e in range(NE)],
                *[f"slabl{i}_{e}" for i in range(2) for e in range(NE)],
                "gfull0", "gfull1", "ghisb0", "ghisb1", "glosb0", "glosb1")

    qT_s = [AR2.alloc(f"qT{c}", [128, S], F16) for c in range(NCH)]
    kT_s = [AR2.alloc(f"kT{c}", [128, S], F16) for c in range(NCH)]
    vp_s = AR2.alloc("vp", [128, 16 * 1040], F16)
    AR2.freebuf(*([f"qT{c}" for c in range(NCH)] + [f"kT{c}" for c in range(NCH)]
                  + ["vp"]))
    xgTh_s = [AR2.alloc(f"xgTh{c}", [128, SLOTS], F16) for c in range(NCH)]
    xgTl_s = [AR2.alloc(f"xgTl{c}", [128, SLOTS], F16) for c in range(NCH)]
    x1Tt_s = [AR2.alloc(f"x1Tt{i}", [128, 1024], F16) for i in range(4)]
    AR2.freebuf(*([f"xgTh{c}" for c in range(NCH)] + [f"xgTl{c}" for c in range(NCH)]
                  + [f"x1Tt{i}" for i in range(4)]))
    WB = 512
    gslh_s = [AC.alloc(f"gslh{ic}", [128, WB], F16) for ic in range(NIC)]
    gsll_s = [AC.alloc(f"gsll{ic}", [128, WB], F16) for ic in range(NIC)]
    wjh_s = [AC.alloc(f"wjh{i}", [128, NIC * 128], F16) for i in range(2)]
    wjl_s = [AC.alloc(f"wjl{i}", [128, NIC * 128], F16) for i in range(2)]
    wsp_h = AC.alloc("wsph", [128, 1024], F16)
    wsp_l = AC.alloc("wspl", [128, 1024], F16)
    ysb_s = AC.alloc("ysb", [128, WB], F32)
    yrow_s = [AC.alloc(f"yrow{i}", [128, D], F32) for i in range(4)]
    x1r_s = [AC.alloc(f"x1r{i}", [128, D], F32) for i in range(2)]
    outt_s = [AC.alloc(f"outt{i}", [128, D], F32) for i in range(2)]
    gam_s = AC.alloc("gams", [128, D], F32)
    bet_s = AC.alloc("bets", [128, D], F32)
    bo_s = AC.alloc("bo", [1, D], F32)
    ones512_s = AC.alloc("ones512", [1, 512], F32)

    PA = nc.alloc_psum_tensor("PA", [128, 2048], F32)
    PB = nc.alloc_psum_tensor("PB", [128, 1024], F32)
    PC = nc.alloc_psum_tensor("PC", [128, 1024], F32)

    QL = [f"qa{i}" for i in range(6)]
    GL = [f"gb{i}" for i in range(4)]
    sem_handles = {n: nc.alloc_semaphore(name=f"sem_{n}")
                   for n in tuple(QL) + tuple(GL) + ("t", "s", "v")}

    def walk(me, E):
        w = Walk(me, E, sem_handles)
        w.pools = {"q": QL, "g": GL}
        w.rr = {"q": 0, "g": 0}
        C = w.cnt
        mark, wmark = w.mark, w.wmark

        def qsn():
            return [(l, C[l]) for l in QL]

        def gsn():
            return [(l, C[l]) for l in GL]

        for dst, srcp in ((id16_s, id16_d.ap()), (id32_s, id32_d.ap()),
                          (sel64_s, sel64_d.ap()), (sel64b_s, sel64b_d.ap()),
                          (ones128_s, ones128_d.ap()),
                          (wr32_s, wr_d), (b1_s, b1_d)):
            w.dmal("q", dst[:], srcp)
        const_tok = qsn()
        w.do("v", lambda E: E.memset(eps5_s[:], LN_EPS_OUT))
        w.do("v", lambda E: E.memset(eps12_s[:], LN_EPS_IN))
        w.drain("v")
        w.do("s", lambda E: E.activation(wrh_s[:], wr32_s[:], AF.Copy),
             inc=("s", 1), wait=const_tok)
        sC = C["s"]
        w.do("v", lambda E: E.tensor_tensor(wrl_s[:], wr32_s[:], wrh_s[:], ALU.subtract),
             inc=("v", 1), wait=[("s", sC)])
        w.do("v", lambda E: E.memset(z_s[0][:], 0.0), inc=("v", 1))
        w.drain("v")
        zf_v = C["v"]
        for r0 in range(0, NXG, 128):
            w.dmal("g", xg_d[r0:r0 + 128, :], z_s[0][:], wait=[("v", zf_v)])
            w.dmal("g", yg_d[r0:r0 + 128, :], z_s[0][:])
        zf_tok = gsn()
        w.marks["z0"] = list(zf_tok)

        def qkv_phase(s_idx):
            tok0 = s_idx * S
            for c in range(NCH):
                wmark("q", f"xts{c}")
                w.dmal("q", xts_s[c][:], xT_d[c * 128:(c + 1) * 128, tok0:tok0 + S])
            x_tok = qsn()

            for wsrc, dstT in ((qwp_d, qT_s), (kwp_d, kT_s)):
                wmark("q", "wch")
                for dc in range(NCH):
                    w.dmal("q", wch_s[dc][:, 0:1024], wsrc[dc * 128:(dc + 1) * 128, :])
                w_tok = qsn()
                for tg in range(NTG):
                    wmark("q", "ccs")
                    ct1 = w.dmal("q", ccs_s[:], cc_d.ap()[:, tg * TG:(tg + 1) * TG])
                    ct2 = w.dmal("q", sss_s[:], ss_d.ap()[:, tg * TG:(tg + 1) * TG])
                    for qd in range(4):
                        b = (qd % 2) * 1024
                        pe = PA[:, b:b + 512]
                        po = PA[:, b + 512:b + 1024]
                        wmark("t", f"pab{qd % 2}")
                        for dc in range(NCH):
                            rhs = xts_s[dc][:, tg * TG:(tg + 1) * TG]
                            w.mm(pe, wch_s[dc][:, (2 * qd) * 128:(2 * qd + 1) * 128], rhs,
                                 start=(dc == 0), stop=(dc == NCH - 1),
                                 wait=w_tok + x_tok)
                            w.mm(po, wch_s[dc][:, (2 * qd + 1) * 128:(2 * qd + 2) * 128],
                                 rhs, start=(dc == 0), stop=(dc == NCH - 1),
                                 inc=("t", 1) if dc == NCH - 1 else None)
                        tC = C["t"]
                        wmark("v", "rp")
                        w.do("v", lambda E, pe=pe: E.tensor_tensor(
                            rp_s[0][:], pe, ccs_s[:], ALU.mult),
                             wait=[("t", tC), ct1, ct2])
                        w.do("v", lambda E, po=po: E.tensor_tensor(
                            rp_s[1][:], po, sss_s[:], ALU.mult))
                        w.do("v", lambda E, pe=pe: E.tensor_tensor(
                            rp_s[2][:], pe, sss_s[:], ALU.mult))
                        w.do("v", lambda E, po=po: E.tensor_tensor(
                            rp_s[3][:], po, ccs_s[:], ALU.mult), inc=("v", 1))
                        mark(f"pab{qd % 2}", "v")
                        if qd == 3:
                            mark("ccs", "v")
                        w.drain("v")
                        de = dstT[2 * qd][:, tg * TG:(tg + 1) * TG]
                        do_ = dstT[2 * qd + 1][:, tg * TG:(tg + 1) * TG]
                        w.do("v", lambda E, de=de: E.tensor_tensor(
                            de, rp_s[0][:], rp_s[1][:], ALU.subtract))
                        w.do("v", lambda E, do_=do_: E.tensor_tensor(
                            do_, rp_s[2][:], rp_s[3][:], ALU.add), inc=("v", 1))
                        mark("rp", "v")
                        w.drain("v")
                mark("wch", "t")

            wmark("q", "wch")
            for dc in range(NCH):
                w.dmal("q", wch_s[dc][:, 0:1040], vwp_d[dc * 128:(dc + 1) * 128, :])
            w_tok = qsn()
            pv = 0
            for tt in range(16):
                for j0, jn in ((0, 512), (512, 512), (1024, 16)):
                    ps = PB[:, (pv % 2) * 512:(pv % 2) * 512 + 512]
                    wmark("t", f"pbb{pv % 2}")
                    for dc in range(NCH):
                        w.mm(ps[:, 0:jn], xts_s[dc][:, tt * 128:(tt + 1) * 128],
                             wch_s[dc][:, j0:j0 + jn],
                             start=(dc == 0), stop=(dc == NCH - 1),
                             wait=w_tok + x_tok,
                             inc=("t", 1) if dc == NCH - 1 else None)
                    tC = C["t"]
                    dst = vp_s[:, tt * 1040 + j0: tt * 1040 + j0 + jn]
                    w.do("s", lambda E, dst=dst, ps=ps, jn=jn: E.activation(
                        dst, ps[:, 0:jn], AF.Copy), inc=("s", 1), wait=[("t", tC)])
                    mark(f"pbb{pv % 2}", "s")
                    pv += 1
            sv = C["s"]
            w.do("v", lambda E: E.memset(
                vp_s[:, :].rearrange("p (t c) -> p t c", c=1040)[:, :, 64::65], 1.0),
                 inc=("v", 1), wait=[("s", sv)])
            w.drain("v")
            mark("wch", "t")
            w.barrier()

        def attn_phase(s_idx):
            for i in range(4):
                w.dmal("q", msk_s[i][:], mask_d[i].ap())
            msk_tok = qsn()

            for qg in range(NTG):
                for q in range(4):
                    nkt = 4 * (qg + 1)
                    svtok = {}

                    def emit_scores(kt, q=q, qg=qg, nkt=nkt, svtok=svtok):
                        wmark("t", "bq")
                        wmark("t", "pafV")
                        masked = kt >= qg * 4
                        di = kt - qg * 4
                        toks2 = []
                        for h2 in range(2):
                            wmark("t", f"scq{h2}")
                            for m in (2 * h2, 2 * h2 + 1):
                                dst = PA[:, m * 512:(m + 1) * 512]
                                for half in range(2):
                                    ch = 2 * q + half
                                    last = (half == 1) and not masked
                                    w.mm(dst,
                                         kT_s[ch][32 * m:32 * m + 32,
                                                  kt * 128:(kt + 1) * 128],
                                         qT_s[ch][32 * m:32 * m + 32,
                                                  qg * TG:(qg + 1) * TG],
                                         start=(half == 0), stop=last,
                                         tp=(32 * m, 0), wait=msk_tok,
                                         inc=("t", 1) if (last and m == 2 * h2 + 1)
                                         else None)
                            if masked:
                                for m in (2 * h2, 2 * h2 + 1):
                                    w.mm(PA[:, m * 512:(m + 1) * 512], id16_s[:],
                                         msk_s[di][:], start=False, stop=True,
                                         inc=("t", 1) if m == 2 * h2 + 1 else None)
                            toks2.append(C["t"])
                        svtok[kt] = toks2

                    def emit_expchain(kt, svtok=svtok):
                        tC0, tC1 = svtok[kt]
                        stoks = {}
                        for h2 in range(2):
                            e3 = (2 * kt + h2) % 3
                            wmark("s", f"e32_{e3}")
                            w.do("s", lambda E, h2=h2, e3=e3: E.activation(
                                e32_s[e3][:], PA[:, h2 * 1024:(h2 + 1) * 1024],
                                AF.Exp, scale=SCALE),
                                 inc=("s", 1), wait=[("t", tC0 if h2 == 0 else tC1)])
                            mark(f"scq{h2}", "s")
                            stoks[h2] = C["s"]
                        toks = {}
                        for h2 in range(2):
                            e3 = (2 * kt + h2) % 3
                            e4 = (2 * kt + h2) % 4
                            wmark("v", f"ehi_{e4}")
                            w.do("v", lambda E, e3=e3, e4=e4: E.tensor_copy(
                                ehi_s[e4][:], e32_s[e3][:]), inc=("v", 1),
                                 wait=[("s", stoks[h2])])
                            vCc = C["v"]
                            wmark("v", f"elo_{e4}")
                            w.do("v", lambda E, e3=e3, e4=e4: E.tensor_tensor(
                                elo_s[e4][:], e32_s[e3][:], ehi_s[e4][:], ALU.subtract),
                                 inc=("v", 1), wait=[("v", vCc)])
                            mark(f"e32_{e3}", "v")
                            toks[h2] = (vCc, C["v"])
                        svtok[kt] = toks

                    def emit_vmm(kt, q=q, nkt=nkt, svtok=svtok):
                        toks = svtok[kt]
                        if kt == 0:
                            wmark("t", "atb")
                        for lo in range(2):
                            for hh in range(4):
                                h2 = hh // 2
                                e4 = (2 * kt + h2) % 4
                                h = 4 * q + hh
                                PX = PB if hh < 2 else PC
                                pat = PX[0:65, (hh % 2) * 512:(hh % 2) * 512 + 512]
                                lhs = vp_s[:, kt * 1040 + 65 * h: kt * 1040 + 65 * h + 65]
                                src = (elo_s if lo else ehi_s)[e4]
                                w.mm(pat, lhs,
                                     src[:, (hh % 2) * 512:(hh % 2) * 512 + 512],
                                     start=(kt == 0 and lo == 0),
                                     stop=(kt == nkt - 1 and lo == 1),
                                     wait=[("v", toks[h2][lo])] if hh % 2 == 0 else None,
                                     inc=("t", 1) if hh == 3 else None)
                            nm = "elo" if lo else "ehi"
                            mark(f"{nm}_{(2 * kt) % 4}", "t")
                            mark(f"{nm}_{(2 * kt + 1) % 4}", "t")

                    for kt in range(nkt):
                        emit_scores(kt)
                        if kt > 0:
                            emit_vmm(kt - 1)
                        emit_expchain(kt)
                    emit_vmm(nkt - 1)
                    tC = C["t"]

                    wmark("s", "rec")
                    for hh in range(4):
                        PX = PB if hh < 2 else PC
                        w.do("s", lambda E, hh=hh, PX=PX: E.activation(
                            rec_ap(hh),
                            PX[64:65, (hh % 2) * 512:(hh % 2) * 512 + 512], AF.Copy),
                             inc=("s", 1) if hh == 3 else None,
                             wait=[("t", tC)])
                    sC = C["s"]
                    for hh in range(4):
                        w.do("v", lambda E, hh=hh: E.reciprocal(
                            rec_ap(hh), rec_ap(hh)),
                             inc=("v", 1) if hh == 3 else None,
                             wait=[("s", sC)] if hh == 0 else None)
                    vC = C["v"]
                    mark("rec", "v")
                    wmark("t", "scq0")
                    wmark("t", "scq1")
                    for hh in range(4):
                        _r = 0 if hh == 3 else 32 * hh
                        w.mm(PA[:, hh * 512:hh * 512 + 512],
                             sel64_s[_r:_r + 1, :],
                             rec_ap(hh), start=True, stop=True,
                             wait=[("v", vC)], inc=("t", 1) if hh == 3 else None)
                    tC2 = C["t"]
                    wmark("s", "bsb")
                    for hh in range(4):
                        w.do("s", lambda E, hh=hh: E.activation(
                            bsb_s[0:64, hh * 512:(hh + 1) * 512],
                            PA[0:64, hh * 512:hh * 512 + 512], AF.Copy),
                             inc=("s", 1) if hh == 3 else None, wait=[("t", tC2)])
                    sC = C["s"]
                    mark("bq", "s")
                    for half in range(2):
                        p = 2 * q + half
                        hh0, hh1 = 2 * half, 2 * half + 1
                        PX = PB if half == 0 else PC
                        wmark("v", "t32")
                        w.do("v", lambda E, PX=PX, hh0=hh0: E.tensor_tensor(
                            t32_s[0:64, :], PX[0:64, 0:512],
                            bsb_s[0:64, hh0 * 512:(hh0 + 1) * 512], ALU.mult),
                             wait=[("s", sC)])
                        w.do("v", lambda E, PX=PX, hh1=hh1: E.tensor_tensor(
                            t32_s[64:128, :], PX[0:64, 512:1024],
                            bsb_s[0:64, hh1 * 512:(hh1 + 1) * 512], ALU.mult),
                             inc=("v", 1))
                        vC = C["v"]
                        if half == 1:
                            mark("bsb", "v")
                            mark("atb", "v")
                        wmark("s", f"athi{p}")
                        w.do("s", lambda E, p=p: E.activation(
                            athi_s[p][:], t32_s[:], AF.Copy), inc=("s", 1),
                             wait=[("v", vC)])
                        sC2 = C["s"]
                        wmark("v", f"atlo{p}")
                        w.do("v", lambda E, p=p: E.tensor_tensor(
                            atlo_s[p][:], t32_s[:], athi_s[p][:], ALU.subtract),
                             inc=("v", 1), wait=[("s", sC2)])
                        mark("t32", "v")

                if stage == "attn" and qg == 0:
                    vv, ss_ = C["v"], C["s"]
                    for c in range(NCH):
                        w.dmal("g", dbg["at_hi"][c * 128:(c + 1) * 128, :], athi_s[c][:],
                               wait=[("v", vv), ("s", ss_)])
                        w.dmal("g", dbg["at_lo"][c * 128:(c + 1) * 128, :], atlo_s[c][:])
                    w.waitpool("g", "g")
                    return True

                athi_v = C["v"]
                athi_sc = C["s"]
                pbank = {0: PA[:, 0:1024], 1: PA[:, 1024:2048],
                         2: PB[:, 0:1024], 3: PC[:, 0:1024]}
                xr_toks = {}
                for tt in range(4):
                    tok0 = s_idx * S + qg * TG + tt * 128
                    wmark("q", f"xrow{tt}")
                    xr_toks[tt] = w.dmal("q", xrow_s[tt][:], xrows_d[tok0:tok0 + 128, :])
                for halfw in range(2):
                    wmark("q", "owh")
                    for cc in range(4):
                        c = 4 * halfw + cc
                        w.dmal("q", owh_s[cc][:], ow_d[c * 128:(c + 1) * 128, :])
                    ow_tok = qsn()
                    for tt in range(4):
                        ps = pbank[tt]
                        if halfw == 0:
                            if tt < 2:
                                wmark("t", "pafV")
                            else:
                                wmark("t", "atb")
                        for cc in range(4):
                            c = 4 * halfw + cc
                            for jg in range(2):
                                w.mm(ps[:, jg * 512:(jg + 1) * 512],
                                     athi_s[c][:, tt * 128:(tt + 1) * 128],
                                     owh_s[cc][:, jg * 512:(jg + 1) * 512],
                                     start=(halfw == 0 and cc == 0), stop=False,
                                     wait=[("v", athi_v), ("s", athi_sc)] + ow_tok)
                                w.mm(ps[:, jg * 512:(jg + 1) * 512],
                                     atlo_s[c][:, tt * 128:(tt + 1) * 128],
                                     owh_s[cc][:, jg * 512:(jg + 1) * 512],
                                     start=False, stop=(halfw == 1 and cc == 3),
                                     inc=("t", 1) if (tt == 3 and cc == 3
                                                      and jg == 1) else None)
                    mark("owh", "t")
                opt_t = C["t"]
                for tt in range(4):
                    ti = tt % 2
                    tok0 = s_idx * S + qg * TG + tt * 128
                    ps = pbank[tt]
                    wmark("v", f"z{ti}")
                    w.do("v", lambda E, ps=ps, ti=ti, tt=tt:
                         E.scalar_tensor_tensor(
                             z_s[ti][:], ps[:, :], 1.0, xrow_s[tt][:],
                             ALU.mult, ALU.add, accum_out=vec_s[:, 8:9]),
                         inc=("v", 1), wait=[("t", opt_t), xr_toks[tt]])
                    if tt == 1:
                        mark("pafV", "v")
                    if tt == 3:
                        mark("atb", "v")
                    mark(f"xrow{tt}", "v")
                    vC = C["v"]
                    wmark("s", "sqd")
                    w.do("s", lambda E, ti=ti: E.activation(
                        sqd_s[:], z_s[ti][:], AF.Square, accum_out=vec_s[:, 10:11]),
                         wait=[("v", vC)])
                    w.do("s", lambda E: E.mul(vec_s[:, 1:2], vec_s[:, 8:9], 1.0 / D))
                    w.drain("s")
                    w.do("s", lambda E: E.square(vec_s[:, 2:3], vec_s[:, 1:2]),
                         inc=("s", 1))
                    sC = C["s"]
                    mark("sqd", "s")
                    w.do("v", lambda E: E.scalar_tensor_tensor(
                        vec_s[:, 3:4], vec_s[:, 10:11], 1.0 / D, vec_s[:, 2:3],
                        ALU.mult, ALU.subtract), inc=("v", 1), wait=[("s", sC)])
                    vC = C["v"]
                    w.do("s", lambda E: E.activation(
                        vec_s[:, 4:5], vec_s[:, 3:4], AF.Sqrt, bias=eps5_s[:]),
                         inc=("s", 1), wait=[("v", vC)])
                    sC = C["s"]
                    w.drain("v")
                    w.do("v", lambda E: E.reciprocal(vec_s[:, 5:6], vec_s[:, 4:5]),
                         wait=[("s", sC)])
                    w.drain("v")
                    wmark("v", f"x1t{ti}")
                    w.do("v", lambda E, ti=ti: E.tensor_scalar(
                        x1t_s[ti][:], z_s[ti][:], vec_s[:, 1:2], vec_s[:, 5:6],
                        ALU.subtract, ALU.mult), inc=("v", 1))
                    vC = C["v"]
                    mark(f"z{ti}", "v")
                    x1tok = [w.dmal("g", x1f_d[tok0:tok0 + 128, :], x1t_s[ti][:],
                                    wait=[("v", vC)])]
                    if stage == "ln1":
                        x1tok.append(w.dmal("g", dbg["x1"][tok0:tok0 + 128, :],
                                            x1t_s[ti][:]))
                    w.marks[f"x1t{ti}"] = x1tok
            w.barrier()
            return False

        def slots_phase(s_idx):
            tok0 = s_idx * S
            w.do("v", lambda E: (E.memset(carr_s[:], 0.0), E.memset(onesr_s[:], 1.0),
                                 E.drain())[-1], inc=("v", 1))
            tr_tok = w.dmal("q", trrow_s[:], trrow_d.ap())
            for tg in range(NTG):
                for tl4 in range(4):
                    tile = tg * 4 + tl4
                    ti = tile % 2
                    wmark("q", f"x1t{ti}")
                    xt_tok = w.dmal("q", x1t_s[ti][:],
                                    x1f_d[tok0 + tile * 128: tok0 + (tile + 1) * 128, :])
                    tb = 2 * (tile % 2)
                    for c in range(NCH):
                        sl = PA[:, (c % 4) * 512:(c % 4) * 512 + 128]
                        wmark("t", f"pas{c % 4}")
                        w.do("t", lambda E, sl=sl, ti=ti, c=c: E.transpose(
                            sl, x1t_s[ti][:, c * 128:(c + 1) * 128], id32_s[:]),
                             inc=("t", 1), wait=[xt_tok])
                        tC = C["t"]
                        wmark("s", f"x1Tt{tb}")
                        w.do("s", lambda E, c=c, sl=sl, tb=tb: E.activation(
                            x1Tt_s[tb][:, c * 128:(c + 1) * 128], sl, AF.Copy),
                             inc=("s", 1), wait=[("t", tC)])
                        sC = C["s"]
                        wmark("v", f"x1Tt{tb + 1}")
                        w.do("v", lambda E, c=c, sl=sl, tb=tb: E.tensor_tensor(
                            x1Tt_s[tb + 1][:, c * 128:(c + 1) * 128], sl,
                            x1Tt_s[tb][:, c * 128:(c + 1) * 128], ALU.subtract),
                             inc=("v", 1), wait=[("s", sC)])
                        mark(f"pas{c % 4}", "v")
                    tile_s, tile_v = C["s"], C["v"]
                    if tl4 == 0:
                        wmark("t", "lgbank")
                    first = True
                    for dc in range(NCH):
                        lh = wrh_s[:, dc * 4:(dc + 1) * 4]
                        ll = wrl_s[:, dc * 4:(dc + 1) * 4]
                        xh = x1Tt_s[tb][:, dc * 128:(dc + 1) * 128]
                        xl = x1Tt_s[tb + 1][:, dc * 128:(dc + 1) * 128]
                        dstp = PB[0:4, tl4 * 128:(tl4 + 1) * 128]
                        w.mm(dstp, lh, xh, start=first, stop=False,
                             wait=[("s", tile_s), ("v", tile_v)])
                        first = False
                        w.mm(dstp, lh, xl, start=False, stop=False)
                        w.mm(dstp, ll, xh, start=False, stop=(dc == NCH - 1),
                             inc=("t", 1) if dc == NCH - 1 else None)
                    mark(f"x1Tt{tb}", "t")
                    mark(f"x1Tt{tb + 1}", "t")
                    mark(f"x1t{ti}", "t")
                tC = C["t"]
                wmark("v", "lgs")
                w.do("v", lambda E: E.tensor_copy(lgs_s[:], PB[0:4, 0:512]),
                     inc=("v", 1), wait=[("t", tC)])
                vC = C["v"]
                mark("lgbank", "v")
                wmark("q", "lrow")
                lr_tok = w.dmal("q", lrow_s[0:1, :].rearrange("a (p f) -> a p f", p=4),
                                lgs_s[:], wait=[("v", vC)])
                w.marks["lgs"] = [lr_tok]

                def rowops(E, tg=tg, s_idx=s_idx):
                    L, R = lrow_s, rtmp_s
                    CSUM = R[:, 4 * TG:5 * TG]
                    SLV = R[:, 5 * TG:6 * TG]
                    OV = R[:, 6 * TG:7 * TG]
                    E.tensor_tensor(R[:, 0:TG], L[:, 0:TG], L[:, TG:2 * TG], ALU.max)
                    E.tensor_tensor(R[:, TG:2 * TG], L[:, 2 * TG:3 * TG],
                                    L[:, 3 * TG:4 * TG], ALU.max)
                    E.drain()
                    E.tensor_tensor(R[:, 2 * TG:3 * TG], R[:, 0:TG], R[:, TG:2 * TG],
                                    ALU.max)
                    E.drain()
                    E.memset(R[:, 0:TG], 1.0)
                    E.memset(slotr_s[:], 0.0)
                    E.drain()
                    for e in range(NE):
                        ge = L[:, e * TG:(e + 1) * TG]
                        E.tensor_tensor(ge, ge, R[:, 2 * TG:3 * TG], ALU.is_ge)
                    E.drain()
                    for e in range(NE):
                        ge = L[:, e * TG:(e + 1) * TG]
                        me = R[:, 3 * TG:4 * TG]
                        E.tensor_tensor(me, ge, R[:, 0:TG], ALU.mult)
                        E.drain()
                        E.tensor_tensor_scan(CSUM, onesr_s[:], me,
                                             carr_s[:, e:e + 1], ALU.mult, ALU.add)
                        E.drain()
                        E.tensor_copy(carr_s[:, e:e + 1], CSUM[:, TG - 1:TG])
                        E.tensor_single_scalar(
                            SLV, CSUM, float(s_idx * SLOTS + e * CCAP - 1), ALU.add)
                        E.tensor_single_scalar(ovi_s[:], CSUM, float(CCAP), ALU.is_gt)
                        E.drain()
                        E.copy_predicated(SLV, ovi_s[:], trrow_s[:])
                        E.drain()
                        E.tensor_tensor(SLV, SLV, me, ALU.mult)
                        E.drain()
                        E.tensor_tensor(slotr_s[:], slotr_s[:], SLV, ALU.add)
                        E.drain()
                        if e < NE - 1:
                            E.tensor_tensor(R[:, 0:TG], R[:, 0:TG], me, ALU.subtract)
                            E.drain()
                    return E.drain()
                wmark("v", "slotr")
                w.do("v", rowops, inc=("v", 1), wait=[lr_tok, tr_tok])
                mark("lrow", "v")
                sr_v = C["v"]
                cb = s_idx * 16 + tg * 4
                row = s_idx * NTG + tg
                st_tok = w.dmal("g", slot_d[row:row + 1, :], slotr_s[:],
                                wait=[("v", sr_v)])
                w.marks["slotr"] = [st_tok]
                w.do("q", lambda E, cb=cb, row=row: E.dma_start(
                    out=idxf_s[:, cb:cb + 4],
                    in_=slot_d[row:row + 1, :].rearrange("a (c p) -> (a p) c", p=128),
                    allow_slow_non_contiguous=True),
                     inc=(QL[0], 16), wait=[(QL[0], C[QL[0]]), st_tok])
            idx_tok = (QL[0], C[QL[0]])
            w.do("v", lambda E, s_idx=s_idx: E.tensor_copy(
                idxi_s[:, s_idx * 16:(s_idx + 1) * 16],
                idxf_s[:, s_idx * 16:(s_idx + 1) * 16]),
                 inc=("v", 1), wait=[idx_tok])
            w.drain("v")
            idx_v = C["v"]

            if stage == "slots" and s_idx == 0:
                w.dmal("g", dbg["idx"][:, 0:16], idxf_s[:, 0:16], wait=[("v", idx_v)])
                w.waitpool("g", "g")
                return True

            from concourse.bass_core_rust import IndirectOffsetOnAxis
            for tile in range(16):
                ti = tile % 2
                wmark("q", f"x1t{ti}")
                xt_tok = w.dmal("q", x1t_s[ti][:],
                                x1f_d[tok0 + tile * 128: tok0 + (tile + 1) * 128, :])
                col = s_idx * 16 + tile
                w.do("g", lambda E, ti=ti, col=col: E.indirect_dma_start(
                    xg_d[:, :],
                    IndirectOffsetOnAxis(idxi_s[:, col:col + 1], 0),
                    x1t_s[ti][:], None),
                     inc=(GL[tile % 4], 16),
                     wait=[(GL[tile % 4], C[GL[tile % 4]]), xt_tok, ("v", idx_v)]
                     + zf_tok)
                w.marks[f"x1t{ti}"] = [(GL[tile % 4], C[GL[tile % 4]])]
            sc_tok = gsn()

            for st in range(SLOTS // 128):
                ti = st % 2
                wmark("q", f"x1t{ti}")
                xt_tok = w.dmal("q", x1t_s[ti][:],
                                xg_d[s_idx * SLOTS + st * 128:
                                     s_idx * SLOTS + (st + 1) * 128, :],
                                wait=sc_tok)
                col0 = st * 128
                for c in range(NCH):
                    sl = PA[:, (c % 4) * 512:(c % 4) * 512 + 128]
                    wmark("t", f"pas{c % 4}")
                    w.do("t", lambda E, sl=sl, ti=ti, c=c: E.transpose(
                        sl, x1t_s[ti][:, c * 128:(c + 1) * 128], id32_s[:]),
                         inc=("t", 1), wait=[xt_tok])
                    tC = C["t"]
                    wmark("s", f"xgTh{c}")
                    w.do("s", lambda E, c=c, sl=sl, col0=col0: E.activation(
                        xgTh_s[c][:, col0:col0 + 128], sl, AF.Copy),
                         inc=("s", 1), wait=[("t", tC)])
                    sC = C["s"]
                    wmark("v", f"xgTl{c}")
                    w.do("v", lambda E, c=c, sl=sl, col0=col0: E.tensor_tensor(
                        xgTl_s[c][:, col0:col0 + 128], sl,
                        xgTh_s[c][:, col0:col0 + 128], ALU.subtract),
                         inc=("v", 1), wait=[("s", sC)])
                    mark(f"pas{c % 4}", "v")
                mark(f"x1t{ti}", "t")
            w.barrier()
            return False

        def ffn_phase(s_idx):
            slot0 = s_idx * SLOTS
            sl_tok = {}

            def emit_w1split(ic):
                ip = ic % 2
                for e in range(NE):
                    sb32 = slab32_s[e % 2]
                    wmark("q", f"slab32_{e % 2}")
                    dtok = w.dmal("q", sb32[:], w1_d[e, ic])
                    wmark("s", f"slabh{ip}_{e}")
                    w.do("s", lambda E, e=e, ip=ip, sb32=sb32: E.activation(
                        slabh_s[ip][e][:], sb32[:], AF.Copy), inc=("s", 1),
                         wait=[dtok])
                    sC = C["s"]
                    wmark("v", f"slabl{ip}_{e}")
                    w.do("v", lambda E, e=e, ip=ip, sb32=sb32: E.tensor_tensor(
                        slabl_s[ip][e][:], sb32[:], slabh_s[ip][e][:], ALU.subtract),
                         inc=("v", 1), wait=[("s", sC)])
                    mark(f"slab32_{e % 2}", "v")
                sl_tok[ic] = (C["v"], C["s"])

            emit_w1split(0)
            for ic in range(NIC):
                ip = ic % 2
                if ic + 1 < NIC:
                    emit_w1split(ic + 1)
                slab_v, slab_s_cnt = sl_tok[ic]
                for e in range(NE):
                    for gr in range(CCAP // GR):
                        c0 = e * CCAP + gr * GR
                        ps = PA[:, (2 * e + gr) % 4 * 512:((2 * e + gr) % 4) * 512 + GR]
                        wmark("t", f"fps{(2 * e + gr) % 4}")
                        for dc in range(NCH):
                            lh = slabh_s[ip][e][:, dc * 128:(dc + 1) * 128]
                            ll = slabl_s[ip][e][:, dc * 128:(dc + 1) * 128]
                            xh = xgTh_s[dc][:, c0:c0 + GR]
                            xl = xgTl_s[dc][:, c0:c0 + GR]
                            w.mm(ps, lh, xh, start=(dc == 0), stop=False,
                                 wait=[("v", slab_v), ("s", slab_s_cnt)])
                            w.mm(ps, lh, xl, start=False, stop=False)
                            w.mm(ps, ll, xh, start=False, stop=(dc == NCH - 1),
                                 inc=("t", 1) if dc == NCH - 1 else None)
                        tC = C["t"]
                        gp = (2 * e + gr) % 2
                        wmark("s", f"gfull{gp}")
                        w.do("s", lambda E, ps=ps, e=e, ic=ic, gp=gp: E.activation(
                            gf_s[gp][:], ps, AF.Gelu,
                            bias=b1_s[:, e * 32 + ic:e * 32 + ic + 1]),
                             inc=("s", 1), wait=[("t", tC)])
                        mark(f"fps{(2 * e + gr) % 4}", "s")
                        w.drain("s")
                        wmark("s", f"ghisb{gp}")
                        w.do("s", lambda E, gp=gp: E.activation(
                            ghi_sb[gp][:], gf_s[gp][:], AF.Copy), inc=("s", 1))
                        sC = C["s"]
                        wmark("v", f"glosb{gp}")
                        w.do("v", lambda E, gp=gp: E.tensor_tensor(
                            glo_sb[gp][:], gf_s[gp][:], ghi_sb[gp][:], ALU.subtract),
                             inc=("v", 1), wait=[("s", sC)])
                        vC = C["v"]
                        mark(f"gfull{gp}", "v")
                        gh_tok = w.dmal("g", ghi_d[ic, :, slot0 + c0: slot0 + c0 + GR],
                                        ghi_sb[gp][:], wait=[("s", sC), ("v", vC)])
                        gl_tok = w.dmal("g", glo_d[ic, :, slot0 + c0: slot0 + c0 + GR],
                                        glo_sb[gp][:])
                        w.marks[f"ghisb{gp}"] = [gh_tok]
                        w.marks[f"glosb{gp}"] = [gl_tok]
                for e in range(NE):
                    w.marks[f"slabh{ip}_{e}"] = [("t", C["t"])]
                    w.marks[f"slabl{ip}_{e}"] = [("t", C["t"])]
            w.barrier()

        def wo_phase():
            for dst, srcp in ((gam_s, gam_d), (bet_s, bet_d), (bo_s, bo_d),
                              (ones512_s, ones512_d.ap())):
                w.dmal("q", dst[:], srcp)
            wo_tok = qsn()
            sp_tok = {}
            from concourse.bass_core_rust import IndirectOffsetOnAxis

            NWB = SLOTS // WB
            tl_toks = {}

            def emit_tail_load(s_idx, tile, ygs_tok):
                ti = tile % 2
                col = s_idx * 16 + tile
                tok0 = s_idx * S + tile * 128
                gl = GL[2 + ti]
                tlg_w = list(w.marks.get(f"tlg{ti}", []))
                w.do("g", lambda E, ti=ti, col=col: E.indirect_dma_start(
                    x1t_s[ti][:], None,
                    yg_d[:, :],
                    IndirectOffsetOnAxis(idxi_s[:, col:col + 1], 0)),
                     inc=(gl, 16),
                     wait=[(gl, C[gl])] + ygs_tok + tlg_w)
                yr_tok = (gl, C[gl])
                wmark("q", f"x1r{ti}")
                x1r_tok = w.dmal("q", x1r_s[ti][:], x1f_d[tok0:tok0 + 128, :])
                tl_toks[(s_idx, tile)] = (yr_tok, x1r_tok)

            def emit_tail_chain(s_idx, tile):
                ti = tile % 2
                tok0 = s_idx * S + tile * 128
                yr_tok, x1r_tok = tl_toks.pop((s_idx, tile))
                wmark("v", f"z{ti}")
                wmark("v", f"outt{ti}")
                w.do("v", lambda E, ti=ti: E.scalar_tensor_tensor(
                    z_s[ti][:], x1t_s[ti][:], 1.0, x1r_s[ti][:],
                    ALU.mult, ALU.add, accum_out=vec_s[:, 12:13]),
                     inc=("v", 1), wait=[yr_tok, x1r_tok])
                vC = C["v"]
                mark(f"tlg{ti}", "v")
                w.do("v", lambda E, ti=ti: E.tensor_tensor(
                    outt_s[ti][:], bet_s[:], x1r_s[ti][:], ALU.add), inc=("v", 1))
                wmark("s", "sqd")
                w.do("s", lambda E, ti=ti: E.activation(
                    sqd_s[:], z_s[ti][:], AF.Square, accum_out=vec_s[:, 13:14]),
                     wait=[("v", vC)])
                w.do("s", lambda E: E.mul(vec_s[:, 1:2], vec_s[:, 12:13], 1.0 / D))
                w.drain("s")
                w.do("s", lambda E: E.square(vec_s[:, 2:3], vec_s[:, 1:2]),
                     inc=("s", 1))
                sC = C["s"]
                mark("sqd", "s")
                w.do("v", lambda E: E.scalar_tensor_tensor(
                    vec_s[:, 3:4], vec_s[:, 13:14], 1.0 / D, vec_s[:, 2:3],
                    ALU.mult, ALU.subtract), inc=("v", 1), wait=[("s", sC)])
                vC = C["v"]
                w.do("s", lambda E: E.activation(
                    vec_s[:, 4:5], vec_s[:, 3:4], AF.Sqrt, bias=eps12_s[:]),
                     inc=("s", 1), wait=[("v", vC)])
                sC = C["s"]
                w.drain("v")
                w.do("v", lambda E: E.reciprocal(vec_s[:, 5:6], vec_s[:, 4:5]),
                     wait=[("s", sC)])
                w.drain("v")
                wmark("v", "sqd")
                w.do("v", lambda E, ti=ti: E.tensor_scalar(
                    sqd_s[:], z_s[ti][:], vec_s[:, 1:2], vec_s[:, 5:6],
                    ALU.subtract, ALU.mult))
                w.drain("v")
                w.do("v", lambda E: E.tensor_tensor(
                    sqd_s[:], sqd_s[:], gam_s[:], ALU.mult))
                w.drain("v")
                w.do("v", lambda E, ti=ti: E.scalar_tensor_tensor(
                    z_s[ti][:], sqd_s[:], 1.0, outt_s[ti][:], ALU.mult, ALU.add,
                    accum_out=vec_s[:, 14:15]), inc=("v", 1))
                vC = C["v"]
                mark(f"x1r{ti}", "v")
                mark("sqd", "v")
                w.do("s", lambda E, ti=ti: E.activation(
                    sqd_s[:], z_s[ti][:], AF.Square, accum_out=vec_s[:, 15:16]),
                     wait=[("v", vC)])
                w.do("s", lambda E: E.mul(vec_s[:, 1:2], vec_s[:, 14:15], 1.0 / D))
                w.drain("s")
                w.do("s", lambda E: E.square(vec_s[:, 2:3], vec_s[:, 1:2]),
                     inc=("s", 1))
                sC = C["s"]
                mark("sqd", "s")
                w.do("v", lambda E: E.scalar_tensor_tensor(
                    vec_s[:, 3:4], vec_s[:, 15:16], 1.0 / D, vec_s[:, 2:3],
                    ALU.mult, ALU.subtract), inc=("v", 1), wait=[("s", sC)])
                vC = C["v"]
                w.do("s", lambda E: E.activation(
                    vec_s[:, 4:5], vec_s[:, 3:4], AF.Sqrt, bias=eps5_s[:]),
                     inc=("s", 1), wait=[("v", vC)])
                sC = C["s"]
                w.drain("v")
                w.do("v", lambda E: E.reciprocal(vec_s[:, 5:6], vec_s[:, 4:5]),
                     wait=[("s", sC)])
                w.drain("v")
                w.do("v", lambda E, ti=ti: E.tensor_scalar(
                    outt_s[ti][:], z_s[ti][:], vec_s[:, 1:2], vec_s[:, 5:6],
                    ALU.subtract, ALU.mult), inc=("v", 1))
                vC = C["v"]
                mark(f"z{ti}", "v")
                o_tok = w.dmal("g", out_d[tok0:tok0 + 128, :], outt_s[ti][:],
                               wait=[("v", vC)])
                w.marks[f"outt{ti}"] = [o_tok]

            def emit_tail_pair(tA, tB, prefetch=None):
                st = {}
                for ti, tile in ((0, tA), (1, tB)):
                    o = 16 * ti
                    sqdP = sqd_s if ti == 0 else yrow_s[0]
                    st[ti] = dict(tile=tile, o=o, sqdP=sqdP,
                                  tok0=S + tile * 128)

                def S0(ti):
                    c = st[ti]
                    yr_tok, x1r_tok = tl_toks.pop((1, c["tile"]))
                    wmark("v", f"z{ti}")
                    wmark("v", f"zs{ti}")
                    wmark("v", f"outt{ti}")
                    w.do("v", lambda E, ti=ti, c=c: E.scalar_tensor_tensor(
                        z_s[ti][:], x1t_s[ti][:], 1.0, x1r_s[ti][:],
                        ALU.mult, ALU.add, accum_out=vec_s[:, c["o"] + 12:c["o"] + 13]),
                         inc=("v", 1), wait=[yr_tok, x1r_tok])
                    c["vC"] = C["v"]
                    mark(f"tlg{ti}", "v")

                def S1(ti):
                    w.do("v", lambda E, ti=ti: E.tensor_tensor(
                        outt_s[ti][:], bet_s[:], x1r_s[ti][:], ALU.add),
                         inc=("v", 1))
                    mark(f"x1r{ti}", "v")

                def S2(ti):
                    c = st[ti]
                    if ti == 1:
                        wmark("s", "yrow0")
                    w.do("s", lambda E, ti=ti, c=c: E.activation(
                        c["sqdP"][:], z_s[ti][:], AF.Square,
                        accum_out=vec_s[:, c["o"] + 13:c["o"] + 14]),
                         wait=[("v", c["vC"])])
                    w.do("s", lambda E, c=c: E.mul(
                        vec_s[:, c["o"] + 1:c["o"] + 2],
                        vec_s[:, c["o"] + 12:c["o"] + 13], 1.0 / D), inc=("s", 1))
                    c["sC"] = C["s"]

                def S3(ti):
                    c = st[ti]
                    w.do("s", lambda E, c=c: E.square(
                        vec_s[:, c["o"] + 2:c["o"] + 3],
                        vec_s[:, c["o"] + 1:c["o"] + 2]), inc=("s", 1),
                         wait=[("s", c["sC"])])
                    c["sC"] = C["s"]

                def S4(ti, sq):
                    c = st[ti]
                    w.do("v", lambda E, c=c, sq=sq: E.scalar_tensor_tensor(
                        vec_s[:, c["o"] + 3:c["o"] + 4],
                        vec_s[:, c["o"] + sq:c["o"] + sq + 1], 1.0 / D,
                        vec_s[:, c["o"] + 2:c["o"] + 3],
                        ALU.mult, ALU.subtract), inc=("v", 1),
                         wait=[("s", c["sC"])])
                    c["vC"] = C["v"]

                def S5(ti, eps):
                    c = st[ti]
                    w.do("s", lambda E, c=c, eps=eps: E.activation(
                        vec_s[:, c["o"] + 4:c["o"] + 5],
                        vec_s[:, c["o"] + 3:c["o"] + 4], AF.Sqrt, bias=eps[:]),
                         inc=("s", 1), wait=[("v", c["vC"])])
                    c["sC"] = C["s"]

                def S6(ti):
                    c = st[ti]
                    w.do("v", lambda E, c=c: E.reciprocal(
                        vec_s[:, c["o"] + 5:c["o"] + 6],
                        vec_s[:, c["o"] + 4:c["o"] + 5]), inc=("v", 1),
                         wait=[("s", c["sC"])])
                    c["vC"] = C["v"]

                def S7(ti):
                    c = st[ti]
                    w.do("v", lambda E, ti=ti, c=c: E.tensor_scalar(
                        c["sqdP"][:], z_s[ti][:],
                        vec_s[:, c["o"] + 1:c["o"] + 2],
                        vec_s[:, c["o"] + 5:c["o"] + 6],
                        ALU.subtract, ALU.mult), inc=("v", 1),
                         wait=[("v", c["vC"])])
                    c["vC"] = C["v"]

                def S8(ti):
                    c = st[ti]
                    w.do("v", lambda E, c=c: E.tensor_tensor(
                        c["sqdP"][:], c["sqdP"][:], gam_s[:], ALU.mult),
                         inc=("v", 1), wait=[("v", c["vC"])])
                    c["vC"] = C["v"]

                def S9(ti):
                    c = st[ti]
                    w.do("v", lambda E, ti=ti, c=c: E.scalar_tensor_tensor(
                        z_s[ti][:], c["sqdP"][:], 1.0, outt_s[ti][:],
                        ALU.mult, ALU.add,
                        accum_out=vec_s[:, c["o"] + 14:c["o"] + 15]),
                         inc=("v", 1), wait=[("v", c["vC"])])
                    c["vC"] = C["v"]

                def S10(ti):
                    c = st[ti]
                    w.do("s", lambda E, ti=ti, c=c: E.activation(
                        c["sqdP"][:], z_s[ti][:], AF.Square,
                        accum_out=vec_s[:, c["o"] + 15:c["o"] + 16]),
                         wait=[("v", c["vC"])])
                    w.do("s", lambda E, c=c: E.mul(
                        vec_s[:, c["o"] + 1:c["o"] + 2],
                        vec_s[:, c["o"] + 14:c["o"] + 15], 1.0 / D), inc=("s", 1))
                    c["sC"] = C["s"]
                    mark(f"zs{ti}", "s")
                    if ti == 1:
                        mark("yrow0", "s")

                def S15(ti):
                    c = st[ti]
                    w.do("v", lambda E, ti=ti, c=c: E.tensor_scalar(
                        outt_s[ti][:], z_s[ti][:],
                        vec_s[:, c["o"] + 1:c["o"] + 2],
                        vec_s[:, c["o"] + 5:c["o"] + 6],
                        ALU.subtract, ALU.mult), inc=("v", 1),
                         wait=[("v", c["vC"])])
                    c["vC"] = C["v"]
                    mark(f"z{ti}", "v")

                def S16(ti):
                    c = st[ti]
                    o_tok = w.dmal("g", out_d[c["tok0"]:c["tok0"] + 128, :],
                                   outt_s[ti][:], wait=[("v", c["vC"])])
                    w.marks[f"outt{ti}"] = [o_tok]

                S0(0); S0(1); S1(0); S1(1)
                if prefetch is not None:
                    prefetch()
                S2(0); S2(1); S3(0); S3(1); S4(0, 13); S4(1, 13)
                S5(0, eps12_s); S5(1, eps12_s); S6(0); S6(1)
                S7(0); S7(1); S8(0); S8(1); S9(0); S9(1)
                S10(0); S10(1); S3(0); S3(1); S4(0, 15); S4(1, 15)
                S5(0, eps5_s); S5(1, eps5_s); S6(0); S6(1)
                S15(0); S15(1); S16(0); S16(1)

            tail0n = [0]
            ygs0 = [None]
            for s_idx in range(NSEQ):
              slot0 = s_idx * SLOTS
              for wb in range(NWB):
                g_ic_tok = {}
                for ic in range(NIC):
                    wmark("q", f"gsl{ic}")
                    t1 = w.dmal("q", gslh_s[ic][:],
                                ghi_d[ic, :, slot0 + wb * WB:slot0 + (wb + 1) * WB])
                    t2 = w.dmal("q", gsll_s[ic][:],
                                glo_d[ic, :, slot0 + wb * WB:slot0 + (wb + 1) * WB])
                    g_ic_tok[ic] = [t1, t2]

                def emit_wosplit(jc):
                    jp = jc % 2
                    wmark("s", f"wjh{jp}")
                    wmark("v", f"wjl{jp}")
                    for q4 in range(4):
                        sb32 = slab32_s[q4 % 2]
                        wmark("q", f"slab32_{q4 % 2}")
                        wsl_tok = w.dmal("q", sb32[:],
                                         wo_d[jc, :, q4 * 8:(q4 + 1) * 8, :])
                        w.do("s", lambda E, sb32=sb32, jp=jp, q4=q4: E.activation(
                            wjh_s[jp][:, q4 * 1024:(q4 + 1) * 1024], sb32[:], AF.Copy),
                             inc=("s", 1), wait=[wsl_tok])
                        sC = C["s"]
                        w.do("v", lambda E, sb32=sb32, jp=jp, q4=q4: E.tensor_tensor(
                            wjl_s[jp][:, q4 * 1024:(q4 + 1) * 1024], sb32[:],
                            wjh_s[jp][:, q4 * 1024:(q4 + 1) * 1024], ALU.subtract),
                             inc=("v", 1), wait=[("s", sC)])
                        mark(f"slab32_{q4 % 2}", "v")
                    sp_tok[jc] = (C["v"], C["s"])

                if wb == 0:
                    emit_wosplit(0)
                for jc in range(NCH):
                    jp = jc % 2
                    if not (jc == NCH - 1 and wb == NWB - 1):
                        emit_wosplit((jc + 1) % NCH)
                    wo_v, wo_s_cnt = sp_tok[jc]
                    ps = PB[:, jp * 512:jp * 512 + WB]
                    wmark("t", f"pbb{jp}")
                    for ic in range(NIC):
                        w.mm(ps, wjh_s[jp][:, ic * 128:(ic + 1) * 128], gslh_s[ic][:],
                             start=(ic == 0), stop=False,
                             wait=[("v", wo_v), ("s", wo_s_cnt)] + g_ic_tok[ic]
                                  + wo_tok)
                        w.mm(ps, wjh_s[jp][:, ic * 128:(ic + 1) * 128], gsll_s[ic][:],
                             start=False, stop=False)
                        w.mm(ps, wjl_s[jp][:, ic * 128:(ic + 1) * 128], gslh_s[ic][:],
                             start=False, stop=False)
                    w.mm(ps, bo_s[:, jc * 128:(jc + 1) * 128], ones512_s[:, 0:WB],
                         start=False, stop=True, inc=("t", 1))
                    tC = C["t"]
                    mark(f"wjh{jp}", "t")
                    mark(f"wjl{jp}", "t")
                    wmark("s", "ysb")
                    w.do("s", lambda E, ps=ps: E.activation(ysb_s[:], ps, AF.Copy),
                         inc=("s", 1), wait=[("t", tC)])
                    sC = C["s"]
                    mark(f"pbb{jp}", "s")
                    for quarter in range(4):
                        sl = PC[:, (quarter % 2) * 512:(quarter % 2) * 512 + 128]
                        wmark("t", f"pc{quarter % 2}")
                        w.do("t", lambda E, sl=sl, quarter=quarter: E.transpose(
                            sl, ysb_s[:, quarter * 128:quarter * 128 + 128], id32_s[:]),
                             inc=("t", 1), wait=[("s", sC)])
                        tC = C["t"]
                        wmark("v", f"yrow{quarter}")
                        w.do("v", lambda E, quarter=quarter, jc=jc, sl=sl: E.tensor_copy(
                            yrow_s[quarter][:, jc * 128:(jc + 1) * 128], sl),
                             inc=("v", 1), wait=[("t", tC)])
                        mark(f"pc{quarter % 2}", "v")
                    mark("ysb", "t")
                    if s_idx == 1:
                        gidx = wb * NCH + jc
                        if gidx >= 2 and gidx % 2 == 0 and tail0n[0] < 16:
                            t0 = tail0n[0]
                            if t0 > 0:
                                emit_tail_chain(0, t0 - 1)
                            emit_tail_load(0, t0, ygs0[0])
                            tail0n[0] += 1
                for ic in range(NIC):
                    w.marks[f"gsl{ic}"] = [("t", C["t"])]
                y_v = C["v"]
                for quarter in range(4):
                    r0 = slot0 + wb * WB + quarter * 128
                    yg_tok = w.dmal("g", yg_d[r0:r0 + 128, :], yrow_s[quarter][:],
                                    wait=[("v", y_v)])
                    w.marks[f"yrow{quarter}"] = [yg_tok]
              if s_idx == 0:
                  ygs0[0] = gsn()
            emit_tail_chain(0, 15)
            ygs1 = gsn()
            emit_tail_load(1, 0, ygs1)
            emit_tail_load(1, 1, ygs1)
            for pk in range(8):
                pf = None
                if pk < 7:
                    def pf(pk=pk):
                        emit_tail_load(1, 2 * pk + 2, ygs1)
                        emit_tail_load(1, 2 * pk + 3, ygs1)
                emit_tail_pair(2 * pk, 2 * pk + 1, pf)
            w.waitpool("g", "g")

        for s_idx in range(NSEQ):
            qkv_phase(s_idx)
            if stage == "qkv" and s_idx == 0:
                vv, ss_ = C["v"], C["s"]
                for c in range(NCH):
                    w.dmal("g", dbg["qT"][c * 128:(c + 1) * 128, :], qT_s[c][:],
                           wait=[("v", vv), ("s", ss_)])
                    w.dmal("g", dbg["kT"][c * 128:(c + 1) * 128, :], kT_s[c][:])
                w.dmal("g", dbg["vp"][:], vp_s[:])
                w.waitpool("g", "g")
                return
            if attn_phase(s_idx):
                return
            if slots_phase(s_idx):
                return
            ffn_phase(s_idx)
        if stage == "ffn":
            w.waitpool("g", "g")
            return
        wo_phase()

    with nc.Block() as block:
        @block.sync
        def _(eng):
            walk("q", eng)

        @block.gpsimd
        def _(eng):
            walk("g", eng)

        @block.tensor
        def _(eng):
            walk("t", eng)

        @block.scalar
        def _(eng):
            walk("s", eng)

        @block.vector
        def _(eng):
            walk("v", eng)

    return nc


def _stage_inputs(inputs):
    x = np.asarray(inputs["L0_attn_x"])
    qw = np.asarray(inputs["L0_attn_qw"])
    kw = np.asarray(inputs["L0_attn_kw"])
    vw = np.asarray(inputs["L0_attn_vw"])
    ow = np.asarray(inputs["L0_attn_ow"])
    wr = np.asarray(inputs["L0_ffn_W_router"])
    w1 = np.asarray(inputs["L0_ffn_W1"])
    b1 = np.asarray(inputs["L0_ffn_b1"])
    wo = np.asarray(inputs["L0_ffn_Wo"])
    bo = np.asarray(inputs["L0_ffn_bo"])
    gam = np.asarray(inputs["L0_ffn_ln_gamma"])
    bet = np.asarray(inputs["L0_ffn_ln_beta"])

    perm = []
    for qd in range(4):
        for par in range(2):
            for m in range(4):
                h = 4 * qd + m
                for i in range(32):
                    perm.append(64 * h + 2 * i + par)
    perm = np.array(perm)
    qwp = np.ascontiguousarray(qw[:, perm])
    kwp = np.ascontiguousarray(kw[:, perm])
    vwp = np.zeros((D, 1040), np.float16)
    for h in range(NH):
        vwp[:, 65 * h:65 * h + 64] = vw[:, 64 * h:64 * h + 64]
    wrc = np.ascontiguousarray(
        wr.reshape(8, 128, 4).transpose(1, 0, 2).reshape(128, 32)).astype(np.float32)
    w1r = np.ascontiguousarray(
        w1.reshape(NE, 8, 128, NIC, 128).transpose(0, 3, 2, 1, 4)
        .reshape(NE, NIC, 128, 1024)).astype(np.float32)
    b1c = np.ascontiguousarray(
        b1.reshape(NE, NIC, 128).transpose(2, 0, 1).reshape(128, NE * NIC)
    ).astype(np.float32)
    gamt = np.ascontiguousarray(np.broadcast_to(gam[None, :], (128, D))).astype(np.float32)
    bett = np.ascontiguousarray(np.broadcast_to(bet[None, :], (128, D))).astype(np.float32)
    bor = np.ascontiguousarray(bo[None, :]).astype(np.float32)
    wor2 = np.ascontiguousarray(
        wo.reshape(NIC, 128, NCH, 128).transpose(2, 1, 0, 3)).astype(np.float32)

    in_maps = []
    for c in range(8):
        xc = x[2 * c:2 * c + 2]
        xrows = np.ascontiguousarray(xc.reshape(T, D))
        xT = np.ascontiguousarray(xc.reshape(T, D).T)
        in_maps.append({
            "xT": xT, "xrows": xrows,
            "qwp": qwp, "kwp": kwp, "vwp": vwp, "oww": np.ascontiguousarray(ow),
            "wrc": wrc, "w1r": w1r, "b1c": b1c,
            "wor2": wor2, "bor": bor, "gamt": gamt, "bett": bett,
        })
    return in_maps


def kernel(**inputs):
    in_maps = _stage_inputs(inputs)
    nc = build(stage="full", debug=False)
    res = run_bass_kernel_spmd(nc, in_maps, core_ids=list(range(8)))
    out = np.concatenate([np.asarray(r["out"]) for r in res.results], axis=0)
    return out.reshape(16, 2048, 1024).astype(np.float32)


# version 18
# speedup vs baseline: 1.2075x; 1.0220x over previous
import numpy as np

import concourse.bass as bass
import concourse.mybir as mybir
from concourse.bass_utils import run_bass_kernel_spmd

F32 = mybir.dt.float32
F16 = mybir.dt.float16
AF = mybir.ActivationFunctionType
ALU = mybir.AluOpType

D = 1024
S = 2048
NSEQ = 2
T = NSEQ * S
NH = 16
NCH = 8
TG = 512
NTG = S // TG
NE = 4
DFF = 4096
NIC = DFF // 128
SCALE = 1.0 / 32.0
MASKVAL = -30000.0
LN_EPS_OUT = 1e-5
LN_EPS_IN = 1e-12
CCAP = 640
GR = 320
SLOTS = NE * CCAP
TRASH = NSEQ * SLOTS
NXG = TRASH + 128

CAP = 205312


def _round32(x):
    return (x + 31) // 32 * 32


class Arena:

    _ctr = [0]

    def __init__(self, nc, cap):
        self.nc = nc
        self.cap = cap
        self.free = [(0, cap)]
        self.live = {}
        Arena._ctr[0] += 1
        self.backing = nc.alloc_sbuf_tensor(f"arena{Arena._ctr[0]}", [128, cap],
                                            mybir.dt.uint8)
        self.base_addr = nc.lookup_mloc(self.backing).addr

    def alloc(self, name, shape, dtype):
        per_part = int(np.prod(shape[1:])) * mybir.dt.size(dtype)
        need = _round32(per_part)
        for idx, (off, sz) in enumerate(self.free):
            if sz >= need:
                self.free[idx] = (off + need, sz - need)
                if self.free[idx][1] == 0:
                    self.free.pop(idx)
                h = self.nc.alloc_sbuf_tensor_at(name, list(shape), dtype,
                                                 offset=self.base_addr + off)
                self.live[name] = (off, need)
                return h
        raise MemoryError(f"sbuf arena: no room for {name} ({need}B); free={self.free}")

    def freebuf(self, *names):
        for name in names:
            off, sz = self.live.pop(name)
            self.free.append((off, sz))
        self.free.sort()
        out = []
        for off, sz in self.free:
            if out and out[-1][0] + out[-1][1] == off:
                out[-1] = (out[-1][0], out[-1][1] + sz)
            else:
                out.append((off, sz))
        self.free = [tuple(x) for x in out]


class Walk:

    def __init__(self, me, handle, sems):
        self.me = me
        self.E = handle
        self.sems = sems
        self.cnt = {k: 0 for k in sems}
        self.last_wait = {}
        self.marks = {}

    def wait(self, eng, *pairs):
        if self.me == eng:
            for s, v in pairs:
                if v <= 0 or self.last_wait.get(s, -1) >= v:
                    continue
                self.E.wait_ge(self.sems[s], v)
                self.last_wait[s] = v

    def do(self, eng, fn, inc=None, wait=None):
        if self.me == eng:
            if wait:
                self.wait(eng, *wait)
            inst = fn(self.E)
            if inc:
                inst.then_inc(self.sems[inc[0]], inc[1])
        if inc:
            self.cnt[inc[0]] += inc[1]

    def drain(self, eng):
        if self.me == eng:
            self.E.drain()

    def dma(self, eng, out, in_, sem, wait=None):
        self.do(eng, lambda E: E.dma_start(out=out, in_=in_), inc=(sem, 16), wait=wait)

    def dmal(self, eng, out, in_, wait=None):
        pool = self.pools[eng]
        lane = pool[self.rr[eng] % len(pool)]
        self.rr[eng] += 1
        ws = [(lane, self.cnt[lane])]
        if wait:
            ws += list(wait)
        self.do(eng, lambda E: E.dma_start(out=out, in_=in_), inc=(lane, 16), wait=ws)
        return (lane, self.cnt[lane])

    def waitpool(self, eng, pool_of):
        for lane in self.pools[pool_of]:
            self.wait(eng, (lane, self.cnt[lane]))

    def mm(self, out, lhsT, rhs, start, stop, tp=None, wait=None, inc=None):
        self.do("t",
                lambda E: E.matmul(out, lhsT, rhs, start=start, stop=stop,
                                   tile_position=tp),
                inc=inc, wait=wait)

    def mark(self, name, sem):
        self.marks[name] = [(sem, self.cnt[sem])]

    def wmark(self, eng, name):
        for s, v in self.marks.get(name, ()):
            self.wait(eng, (s, v))

    def barrier(self):
        snap = [(s, v) for s, v in self.cnt.items() if v > 0]
        for eng in ("q", "g", "t", "s", "v"):
            self.wait(eng, *snap)


def build(stage="full", debug=False):
    nc = bass.Bass("TRN2", target_bir_lowering=False)

    xT_d = nc.dram_tensor("xT", [D, T], F16, kind="ExternalInput").ap()
    xrows_d = nc.dram_tensor("xrows", [T, D], F16, kind="ExternalInput").ap()
    qwp_d = nc.dram_tensor("qwp", [D, D], F16, kind="ExternalInput").ap()
    kwp_d = nc.dram_tensor("kwp", [D, D], F16, kind="ExternalInput").ap()
    vwp_d = nc.dram_tensor("vwp", [D, 1040], F16, kind="ExternalInput").ap()
    ow_d = nc.dram_tensor("oww", [D, D], F16, kind="ExternalInput").ap()
    wr_d = nc.dram_tensor("wrc", [128, 32], F32, kind="ExternalInput").ap()
    w1_d = nc.dram_tensor("w1r", [NE, NIC, 128, 1024], F32, kind="ExternalInput").ap()
    b1_d = nc.dram_tensor("b1c", [128, 128], F32, kind="ExternalInput").ap()
    wo_d = nc.dram_tensor("wor2", [NCH, 128, NIC, 128], F32, kind="ExternalInput").ap()
    bo_d = nc.dram_tensor("bor", [1, D], F32, kind="ExternalInput").ap()
    gam_d = nc.dram_tensor("gamt", [128, D], F32, kind="ExternalInput").ap()
    bet_d = nc.dram_tensor("bett", [128, D], F32, kind="ExternalInput").ap()
    out_d = nc.dram_tensor("out", [T, D], F32, kind="ExternalOutput").ap()

    x1f_d = nc.dram_tensor("x1f", [T, D], F32, kind="Internal").ap()
    ghi_d = nc.dram_tensor("ghi", [NIC, 128, NSEQ * SLOTS], F16, kind="Internal").ap()
    glo_d = nc.dram_tensor("glo", [NIC, 128, NSEQ * SLOTS], F16, kind="Internal").ap()
    xg_d = nc.dram_tensor("xg", [NXG, D], F32, kind="Internal").ap()
    yg_d = nc.dram_tensor("yg", [NXG, D], F32, kind="Internal").ap()
    slot_d = nc.dram_tensor("slotd", [NSEQ * NTG, TG], F32, kind="Internal").ap()

    dbg = {}
    if debug and stage == "qkv":
        dbg["qT"] = nc.dram_tensor("dbg_qT", [D, S], F16, kind="ExternalOutput").ap()
        dbg["kT"] = nc.dram_tensor("dbg_kT", [D, S], F16, kind="ExternalOutput").ap()
        dbg["vp"] = nc.dram_tensor("dbg_vp", [128, 16 * 1040], F16, kind="ExternalOutput").ap()
    if debug and stage == "attn":
        dbg["at_hi"] = nc.dram_tensor("dbg_athi", [D, TG], F16, kind="ExternalOutput").ap()
        dbg["at_lo"] = nc.dram_tensor("dbg_atlo", [D, TG], F16, kind="ExternalOutput").ap()
    if debug and stage == "ln1":
        dbg["x1"] = nc.dram_tensor("dbg_x1", [T, D], F32, kind="ExternalOutput").ap()
    if debug and stage == "slots":
        dbg["idx"] = nc.dram_tensor("dbg_idx", [128, 32], F32, kind="ExternalOutput").ap()

    pos = np.arange(S, dtype=np.float64)
    inv = 10000.0 ** (-np.arange(32, dtype=np.float64) / 32.0)
    freq = inv[:, None] * pos[None, :]
    cc_d = nc.inline_tensor(np.tile(np.cos(freq), (4, 1)).astype(np.float32), "cc_c")
    ss_d = nc.inline_tensor(np.tile(np.sin(freq), (4, 1)).astype(np.float32), "ss_c")
    masks_np = np.zeros((4, 128, TG), np.float16)
    for di in range(4):
        k_idx = np.arange(128)[:, None] + di * 128
        q_idx = np.arange(TG)[None, :]
        masks_np[di][k_idx > q_idx] = np.float16(MASKVAL)
    mask_d = [nc.inline_tensor(masks_np[i], f"mask{i}_c") for i in range(4)]
    id16_d = nc.inline_tensor(np.eye(128, dtype=np.float16), "id16_c")
    id32_d = nc.inline_tensor(np.eye(128, dtype=np.float32), "id32_c")
    sel64_np = np.zeros((128, 128), np.float32)
    for _r in (0, 32, 64, 96):
        sel64_np[_r, :64] = 1.0
    sel64_d = nc.inline_tensor(sel64_np, "sel64_c")
    sel64b_np = np.zeros((1, 128), np.float32)
    sel64b_np[0, 64:] = 1.0
    sel64b_d = nc.inline_tensor(sel64b_np, "sel64b_c")
    ones128_d = nc.inline_tensor(np.ones((1, 128), np.float16), "ones128_c")
    ones512_d = nc.inline_tensor(np.ones((1, 512), np.float32), "ones512_c")
    trrow_np = (TRASH + (np.arange(512) % 128)).astype(np.float32)[None, :]
    trrow_d = nc.inline_tensor(trrow_np, "trrow_c")

    AP_ = Arena(nc, 40448)
    AR1 = Arena(nc, 63488)
    AR2 = Arena(nc, 98816)

    class Multi:

        def alloc(self, name, shape, dtype):
            try:
                return AR1.alloc(name, shape, dtype)
            except MemoryError:
                return AR2.alloc(name, shape, dtype)

    AC = Multi()

    id16_s = AP_.alloc("id16", [128, 128], F16)
    id32_s = AP_.alloc("id32", [128, 128], F32)
    sel64_s = AP_.alloc("sel64", [128, 128], F32)
    sel64b_s = AP_.alloc("sel64b", [1, 128], F32)
    ones128_s = AP_.alloc("ones128", [1, 128], F16)
    wr32_s = AP_.alloc("wr32", [128, 32], F32)
    wrh_s = AP_.alloc("wrh", [128, 32], F16)
    wrl_s = AP_.alloc("wrl", [128, 32], F16)
    b1_s = AP_.alloc("b1", [128, 128], F32)
    slab32_s = [AP_.alloc(f"slab32_{i}", [128, 1024], F32) for i in range(2)]
    xrow_s = [AP_.alloc(f"xrow{i}", [128, D], F16) for i in range(4)]
    z_s = [AP_.alloc(f"z{i}", [128, D], F32) for i in range(2)]
    sqd_s = AP_.alloc("sqd", [128, D], F32)
    vec_s = AP_.alloc("vecs", [128, 32], F32)
    x1t_s = [AP_.alloc(f"x1t{i}", [128, D], F32) for i in range(2)]
    eps5_s = AP_.alloc("eps5", [128, 1], F32)
    eps12_s = AP_.alloc("eps12", [128, 1], F32)
    idxf_s = AP_.alloc("idxf", [128, 32], F32)
    idxi_s = AP_.alloc("idxi", [128, 32], mybir.dt.int32)

    xts_s = [AR1.alloc(f"xts{c}", [128, S], F16) for c in range(NCH)]
    wch_s = [AR1.alloc(f"wch{c}", [128, 1040], F16) for c in range(NCH)]
    ccs_s = AR1.alloc("ccs", [128, TG], F32)
    sss_s = AR1.alloc("sss", [128, TG], F32)
    rp_s = [AR1.alloc(f"rp{i}", [128, TG], F32) for i in range(4)]
    AR1.freebuf(*([f"xts{c}" for c in range(NCH)] + [f"wch{c}" for c in range(NCH)]
                  + ["ccs", "sss"] + [f"rp{i}" for i in range(4)]))
    e32_s = [AR1.alloc(f"e32_{i}", [128, 1024], F32) for i in range(3)]
    _o0 = AP_.base_addr + AP_.live["slab32_0"][0]
    _o1 = AP_.base_addr + AP_.live["slab32_1"][0]
    ehi_s = [AR1.alloc(f"ehi_{i}", [128, 1024], F16) for i in range(2)] + [
        nc.alloc_sbuf_tensor_at("ehi_2", [128, 1024], F16, offset=_o0),
        nc.alloc_sbuf_tensor_at("ehi_3", [128, 1024], F16, offset=_o0 + 2048)]
    elo_s = [AR1.alloc(f"elo_{i}", [128, 1024], F16) for i in range(2)] + [
        nc.alloc_sbuf_tensor_at("elo_2", [128, 1024], F16, offset=_o1),
        nc.alloc_sbuf_tensor_at("elo_3", [128, 1024], F16, offset=_o1 + 2048)]
    rec_s = AR1.alloc("rec", [128, 1024], F32)

    def rec_ap(hh):
        if hh < 3:
            return rec_s[32 * hh:32 * hh + 1, 0:512]
        return rec_s[0:1, 512:1024]
    bsb_s = AR1.alloc("bsb", [64, 2048], F32)
    t32_s = AR1.alloc("t32", [128, TG], F32)
    athi_s = [AR1.alloc(f"athi{c}", [128, TG], F16) for c in range(NCH)]
    atlo_s = [AR1.alloc(f"atlo{c}", [128, TG], F16) for c in range(NCH)]
    owh_s = [AR1.alloc(f"owh{i}", [128, D], F16) for i in range(4)]
    msk_s = [AR1.alloc(f"msk{i}", [128, TG], F16) for i in range(4)]
    AR1.freebuf(*(["e32_0", "e32_1", "e32_2",
                   "ehi_0", "ehi_1", "elo_0", "elo_1",
                   "rec", "bsb", "t32"]
                  + [f"athi{c}" for c in range(NCH)] + [f"atlo{c}" for c in range(NCH)]
                  + [f"owh{i}" for i in range(4)] + [f"msk{i}" for i in range(4)]))
    lgs_s = AR1.alloc("lgs", [4, TG], F32)
    lrow_s = AR1.alloc("lrow", [1, NE * TG], F32)
    rtmp_s = AR1.alloc("rtmp", [1, 7 * TG], F32)
    onesr_s = AR1.alloc("onesr", [1, TG], F32)
    trrow_s = AR1.alloc("trrow", [1, TG], F32)
    carr_s = AR1.alloc("carr", [1, 32], F32)
    ovi_s = AR1.alloc("ovi", [1, TG], mybir.dt.int32)
    slotr_s = AR1.alloc("slotr", [1, TG], F32)
    AR1.freebuf("lgs", "lrow", "rtmp", "onesr", "trrow", "carr", "ovi", "slotr")
    slabh_s = [[AR1.alloc(f"slabh{i}_{e}", [128, 1024], F16) for e in range(NE)]
               for i in range(2)]
    slabl_s = [[AR1.alloc(f"slabl{i}_{e}", [128, 1024], F16) for e in range(NE)]
               for i in range(2)]
    gf_s = [AR1.alloc(f"gfull{i}", [128, GR], F32) for i in range(2)]
    ghi_sb = [AR1.alloc(f"ghisb{i}", [128, GR], F16) for i in range(2)]
    glo_sb = [AR1.alloc(f"glosb{i}", [128, GR], F16) for i in range(2)]
    AR1.freebuf(*[f"slabh{i}_{e}" for i in range(2) for e in range(NE)],
                *[f"slabl{i}_{e}" for i in range(2) for e in range(NE)],
                "gfull0", "gfull1", "ghisb0", "ghisb1", "glosb0", "glosb1")

    qT_s = [AR2.alloc(f"qT{c}", [128, S], F16) for c in range(NCH)]
    kT_s = [AR2.alloc(f"kT{c}", [128, S], F16) for c in range(NCH)]
    vp_s = AR2.alloc("vp", [128, 16 * 1040], F16)
    AR2.freebuf(*([f"qT{c}" for c in range(NCH)] + [f"kT{c}" for c in range(NCH)]
                  + ["vp"]))
    xgTh_s = [AR2.alloc(f"xgTh{c}", [128, SLOTS], F16) for c in range(NCH)]
    xgTl_s = [AR2.alloc(f"xgTl{c}", [128, SLOTS], F16) for c in range(NCH)]
    x1Tt_s = [AR2.alloc(f"x1Tt{i}", [128, 1024], F16) for i in range(4)]
    AR2.freebuf(*([f"xgTh{c}" for c in range(NCH)] + [f"xgTl{c}" for c in range(NCH)]
                  + [f"x1Tt{i}" for i in range(4)]))
    WB = 512
    gslh_s = [AC.alloc(f"gslh{ic}", [128, WB], F16) for ic in range(NIC)]
    gsll_s = [AC.alloc(f"gsll{ic}", [128, WB], F16) for ic in range(NIC)]
    wjh_s = [AC.alloc(f"wjh{i}", [128, NIC * 128], F16) for i in range(2)]
    wjl_s = [AC.alloc(f"wjl{i}", [128, NIC * 128], F16) for i in range(2)]
    wsp_h = AC.alloc("wsph", [128, 1024], F16)
    wsp_l = AC.alloc("wspl", [128, 1024], F16)
    ysb_s = AC.alloc("ysb", [128, WB], F32)
    yrow_s = [AC.alloc(f"yrow{i}", [128, D], F32) for i in range(4)]
    x1r_s = [AC.alloc(f"x1r{i}", [128, D], F32) for i in range(2)]
    outt_s = [AC.alloc(f"outt{i}", [128, D], F32) for i in range(2)]
    gam_s = AC.alloc("gams", [128, D], F32)
    bet_s = AC.alloc("bets", [128, D], F32)
    bo_s = AC.alloc("bo", [1, D], F32)
    ones512_s = AC.alloc("ones512", [1, 512], F32)

    PA = nc.alloc_psum_tensor("PA", [128, 2048], F32)
    PB = nc.alloc_psum_tensor("PB", [128, 1024], F32)
    PC = nc.alloc_psum_tensor("PC", [128, 1024], F32)

    QL = [f"qa{i}" for i in range(6)]
    GL = [f"gb{i}" for i in range(4)]
    sem_handles = {n: nc.alloc_semaphore(name=f"sem_{n}")
                   for n in tuple(QL) + tuple(GL) + ("t", "s", "v")}

    def walk(me, E):
        w = Walk(me, E, sem_handles)
        w.pools = {"q": QL, "g": GL}
        w.rr = {"q": 0, "g": 0}
        C = w.cnt
        mark, wmark = w.mark, w.wmark

        def qsn():
            return [(l, C[l]) for l in QL]

        def gsn():
            return [(l, C[l]) for l in GL]

        for dst, srcp in ((id16_s, id16_d.ap()), (id32_s, id32_d.ap()),
                          (sel64_s, sel64_d.ap()), (sel64b_s, sel64b_d.ap()),
                          (ones128_s, ones128_d.ap()),
                          (wr32_s, wr_d), (b1_s, b1_d)):
            w.dmal("q", dst[:], srcp)
        const_tok = qsn()
        w.do("v", lambda E: E.memset(eps5_s[:], LN_EPS_OUT))
        w.do("v", lambda E: E.memset(eps12_s[:], LN_EPS_IN))
        w.drain("v")
        w.do("s", lambda E: E.activation(wrh_s[:], wr32_s[:], AF.Copy),
             inc=("s", 1), wait=const_tok)
        sC = C["s"]
        w.do("v", lambda E: E.tensor_tensor(wrl_s[:], wr32_s[:], wrh_s[:], ALU.subtract),
             inc=("v", 1), wait=[("s", sC)])
        w.do("v", lambda E: E.memset(z_s[0][:], 0.0), inc=("v", 1))
        w.drain("v")
        zf_v = C["v"]
        for r0 in range(0, NXG, 128):
            w.dmal("g", xg_d[r0:r0 + 128, :], z_s[0][:], wait=[("v", zf_v)])
            w.dmal("g", yg_d[r0:r0 + 128, :], z_s[0][:])
        zf_tok = gsn()
        w.marks["z0"] = list(zf_tok)

        def qkv_phase(s_idx):
            tok0 = s_idx * S
            for c in range(NCH):
                wmark("q", f"xts{c}")
                w.dmal("q", xts_s[c][:], xT_d[c * 128:(c + 1) * 128, tok0:tok0 + S])
            x_tok = qsn()

            for wsrc, dstT in ((qwp_d, qT_s), (kwp_d, kT_s)):
                wmark("q", "wch")
                for dc in range(NCH):
                    w.dmal("q", wch_s[dc][:, 0:1024], wsrc[dc * 128:(dc + 1) * 128, :])
                w_tok = qsn()
                for tg in range(NTG):
                    wmark("q", "ccs")
                    ct1 = w.dmal("q", ccs_s[:], cc_d.ap()[:, tg * TG:(tg + 1) * TG])
                    ct2 = w.dmal("q", sss_s[:], ss_d.ap()[:, tg * TG:(tg + 1) * TG])
                    for qd in range(4):
                        b = (qd % 2) * 1024
                        pe = PA[:, b:b + 512]
                        po = PA[:, b + 512:b + 1024]
                        wmark("t", f"pab{qd % 2}")
                        for dc in range(NCH):
                            rhs = xts_s[dc][:, tg * TG:(tg + 1) * TG]
                            w.mm(pe, wch_s[dc][:, (2 * qd) * 128:(2 * qd + 1) * 128], rhs,
                                 start=(dc == 0), stop=(dc == NCH - 1),
                                 wait=w_tok + x_tok)
                            w.mm(po, wch_s[dc][:, (2 * qd + 1) * 128:(2 * qd + 2) * 128],
                                 rhs, start=(dc == 0), stop=(dc == NCH - 1),
                                 inc=("t", 1) if dc == NCH - 1 else None)
                        tC = C["t"]
                        wmark("v", "rp")
                        w.do("v", lambda E, pe=pe: E.tensor_tensor(
                            rp_s[0][:], pe, ccs_s[:], ALU.mult),
                             wait=[("t", tC), ct1, ct2])
                        w.do("v", lambda E, po=po: E.tensor_tensor(
                            rp_s[1][:], po, sss_s[:], ALU.mult))
                        w.do("v", lambda E, pe=pe: E.tensor_tensor(
                            rp_s[2][:], pe, sss_s[:], ALU.mult))
                        w.do("v", lambda E, po=po: E.tensor_tensor(
                            rp_s[3][:], po, ccs_s[:], ALU.mult), inc=("v", 1))
                        mark(f"pab{qd % 2}", "v")
                        if qd == 3:
                            mark("ccs", "v")
                        w.drain("v")
                        de = dstT[2 * qd][:, tg * TG:(tg + 1) * TG]
                        do_ = dstT[2 * qd + 1][:, tg * TG:(tg + 1) * TG]
                        w.do("v", lambda E, de=de: E.tensor_tensor(
                            de, rp_s[0][:], rp_s[1][:], ALU.subtract))
                        w.do("v", lambda E, do_=do_: E.tensor_tensor(
                            do_, rp_s[2][:], rp_s[3][:], ALU.add), inc=("v", 1))
                        mark("rp", "v")
                        w.drain("v")
                mark("wch", "t")

            wmark("q", "wch")
            for dc in range(NCH):
                w.dmal("q", wch_s[dc][:, 0:1040], vwp_d[dc * 128:(dc + 1) * 128, :])
            w_tok = qsn()
            pv = 0
            for tt in range(16):
                for j0, jn in ((0, 512), (512, 512), (1024, 16)):
                    ps = PB[:, (pv % 2) * 512:(pv % 2) * 512 + 512]
                    wmark("t", f"pbb{pv % 2}")
                    for dc in range(NCH):
                        w.mm(ps[:, 0:jn], xts_s[dc][:, tt * 128:(tt + 1) * 128],
                             wch_s[dc][:, j0:j0 + jn],
                             start=(dc == 0), stop=(dc == NCH - 1),
                             wait=w_tok + x_tok,
                             inc=("t", 1) if dc == NCH - 1 else None)
                    tC = C["t"]
                    dst = vp_s[:, tt * 1040 + j0: tt * 1040 + j0 + jn]
                    w.do("s", lambda E, dst=dst, ps=ps, jn=jn: E.activation(
                        dst, ps[:, 0:jn], AF.Copy), inc=("s", 1), wait=[("t", tC)])
                    mark(f"pbb{pv % 2}", "s")
                    pv += 1
            sv = C["s"]
            w.do("v", lambda E: E.memset(
                vp_s[:, :].rearrange("p (t c) -> p t c", c=1040)[:, :, 64::65], 1.0),
                 inc=("v", 1), wait=[("s", sv)])
            w.drain("v")
            mark("wch", "t")
            w.barrier()

        def attn_phase(s_idx):
            for i in range(4):
                w.dmal("q", msk_s[i][:], mask_d[i].ap())
            msk_tok = qsn()

            for qg in range(NTG):
                for q in range(4):
                    nkt = 4 * (qg + 1)
                    svtok = {}

                    def emit_scores(kt, q=q, qg=qg, nkt=nkt, svtok=svtok):
                        wmark("t", "bq")
                        wmark("t", "pafV")
                        masked = kt >= qg * 4
                        di = kt - qg * 4
                        toks2 = []
                        for h2 in range(2):
                            wmark("t", f"scq{h2}")
                            for m in (2 * h2, 2 * h2 + 1):
                                dst = PA[:, m * 512:(m + 1) * 512]
                                for half in range(2):
                                    ch = 2 * q + half
                                    last = (half == 1) and not masked
                                    w.mm(dst,
                                         kT_s[ch][32 * m:32 * m + 32,
                                                  kt * 128:(kt + 1) * 128],
                                         qT_s[ch][32 * m:32 * m + 32,
                                                  qg * TG:(qg + 1) * TG],
                                         start=(half == 0), stop=last,
                                         tp=(32 * m, 0), wait=msk_tok,
                                         inc=("t", 1) if (last and m == 2 * h2 + 1)
                                         else None)
                            if masked:
                                for m in (2 * h2, 2 * h2 + 1):
                                    w.mm(PA[:, m * 512:(m + 1) * 512], id16_s[:],
                                         msk_s[di][:], start=False, stop=True,
                                         inc=("t", 1) if m == 2 * h2 + 1 else None)
                            toks2.append(C["t"])
                        svtok[kt] = toks2

                    def emit_expchain(kt, svtok=svtok):
                        tC0, tC1 = svtok[kt]
                        stoks = {}
                        for h2 in range(2):
                            e3 = (2 * kt + h2) % 3
                            wmark("s", f"e32_{e3}")
                            w.do("s", lambda E, h2=h2, e3=e3: E.activation(
                                e32_s[e3][:], PA[:, h2 * 1024:(h2 + 1) * 1024],
                                AF.Exp, scale=SCALE),
                                 inc=("s", 1), wait=[("t", tC0 if h2 == 0 else tC1)])
                            if h2 == 0:
                                mark("scq0", "s")
                            stoks[h2] = C["s"]
                        e3_1 = (2 * kt + 1) % 3
                        e4_1 = (2 * kt + 1) % 4
                        wmark("s", f"ehi_{e4_1}")
                        w.do("s", lambda E, e4=e4_1: E.activation(
                            ehi_s[e4][:], PA[:, 1024:2048], AF.Exp, scale=SCALE),
                             inc=("s", 1))
                        mark("scq1", "s")
                        sCc1 = C["s"]
                        toks = {}
                        e3_0 = (2 * kt) % 3
                        e4_0 = (2 * kt) % 4
                        wmark("v", f"ehi_{e4_0}")
                        w.do("v", lambda E, e3=e3_0, e4=e4_0: E.tensor_copy(
                            ehi_s[e4][:], e32_s[e3][:]), inc=("v", 1),
                             wait=[("s", stoks[0])])
                        vCc0 = C["v"]
                        wmark("v", f"elo_{e4_0}")
                        w.do("v", lambda E, e3=e3_0, e4=e4_0: E.tensor_tensor(
                            elo_s[e4][:], e32_s[e3][:], ehi_s[e4][:], ALU.subtract),
                             inc=("v", 1), wait=[("v", vCc0)])
                        mark(f"e32_{e3_0}", "v")
                        toks[0] = (("v", vCc0), ("v", C["v"]))
                        wmark("v", f"elo_{e4_1}")
                        w.do("v", lambda E, e3=e3_1, e4=e4_1: E.tensor_tensor(
                            elo_s[e4][:], e32_s[e3][:], ehi_s[e4][:], ALU.subtract),
                             inc=("v", 1), wait=[("s", sCc1)])
                        mark(f"e32_{e3_1}", "v")
                        toks[1] = (("s", sCc1), ("v", C["v"]))
                        svtok[kt] = toks

                    def emit_vmm(kt, q=q, nkt=nkt, svtok=svtok):
                        toks = svtok[kt]
                        if kt == 0:
                            wmark("t", "atb")
                        for lo in range(2):
                            for hh in range(4):
                                h2 = hh // 2
                                e4 = (2 * kt + h2) % 4
                                h = 4 * q + hh
                                PX = PB if hh < 2 else PC
                                pat = PX[0:65, (hh % 2) * 512:(hh % 2) * 512 + 512]
                                lhs = vp_s[:, kt * 1040 + 65 * h: kt * 1040 + 65 * h + 65]
                                src = (elo_s if lo else ehi_s)[e4]
                                w.mm(pat, lhs,
                                     src[:, (hh % 2) * 512:(hh % 2) * 512 + 512],
                                     start=(kt == 0 and lo == 0),
                                     stop=(kt == nkt - 1 and lo == 1),
                                     wait=[toks[h2][lo]] if hh % 2 == 0 else None,
                                     inc=("t", 1) if hh == 3 else None)
                            nm = "elo" if lo else "ehi"
                            mark(f"{nm}_{(2 * kt) % 4}", "t")
                            mark(f"{nm}_{(2 * kt + 1) % 4}", "t")

                    for kt in range(nkt):
                        emit_scores(kt)
                        if kt > 0:
                            emit_vmm(kt - 1)
                        emit_expchain(kt)
                    emit_vmm(nkt - 1)
                    tC = C["t"]

                    wmark("s", "rec")
                    for hh in range(4):
                        PX = PB if hh < 2 else PC
                        w.do("s", lambda E, hh=hh, PX=PX: E.activation(
                            rec_ap(hh),
                            PX[64:65, (hh % 2) * 512:(hh % 2) * 512 + 512], AF.Copy),
                             inc=("s", 1) if hh == 3 else None,
                             wait=[("t", tC)])
                    sC = C["s"]
                    for hh in range(4):
                        w.do("v", lambda E, hh=hh: E.reciprocal(
                            rec_ap(hh), rec_ap(hh)),
                             inc=("v", 1) if hh == 3 else None,
                             wait=[("s", sC)] if hh == 0 else None)
                    vC = C["v"]
                    mark("rec", "v")
                    wmark("t", "scq0")
                    wmark("t", "scq1")
                    for hh in range(4):
                        _r = 0 if hh == 3 else 32 * hh
                        w.mm(PA[:, hh * 512:hh * 512 + 512],
                             sel64_s[_r:_r + 1, :],
                             rec_ap(hh), start=True, stop=True,
                             wait=[("v", vC)], inc=("t", 1) if hh == 3 else None)
                    tC2 = C["t"]
                    wmark("s", "bsb")
                    for hh in range(4):
                        w.do("s", lambda E, hh=hh: E.activation(
                            bsb_s[0:64, hh * 512:(hh + 1) * 512],
                            PA[0:64, hh * 512:hh * 512 + 512], AF.Copy),
                             inc=("s", 1) if hh == 3 else None, wait=[("t", tC2)])
                    sC = C["s"]
                    mark("bq", "s")
                    for half in range(2):
                        p = 2 * q + half
                        hh0, hh1 = 2 * half, 2 * half + 1
                        PX = PB if half == 0 else PC
                        wmark("v", "t32")
                        w.do("v", lambda E, PX=PX, hh0=hh0: E.tensor_tensor(
                            t32_s[0:64, :], PX[0:64, 0:512],
                            bsb_s[0:64, hh0 * 512:(hh0 + 1) * 512], ALU.mult),
                             wait=[("s", sC)])
                        w.do("v", lambda E, PX=PX, hh1=hh1: E.tensor_tensor(
                            t32_s[64:128, :], PX[0:64, 512:1024],
                            bsb_s[0:64, hh1 * 512:(hh1 + 1) * 512], ALU.mult),
                             inc=("v", 1))
                        vC = C["v"]
                        if half == 1:
                            mark("bsb", "v")
                            mark("atb", "v")
                        wmark("s", f"athi{p}")
                        w.do("s", lambda E, p=p: E.activation(
                            athi_s[p][:], t32_s[:], AF.Copy), inc=("s", 1),
                             wait=[("v", vC)])
                        sC2 = C["s"]
                        wmark("v", f"atlo{p}")
                        w.do("v", lambda E, p=p: E.tensor_tensor(
                            atlo_s[p][:], t32_s[:], athi_s[p][:], ALU.subtract),
                             inc=("v", 1), wait=[("s", sC2)])
                        mark("t32", "v")

                if stage == "attn" and qg == 0:
                    vv, ss_ = C["v"], C["s"]
                    for c in range(NCH):
                        w.dmal("g", dbg["at_hi"][c * 128:(c + 1) * 128, :], athi_s[c][:],
                               wait=[("v", vv), ("s", ss_)])
                        w.dmal("g", dbg["at_lo"][c * 128:(c + 1) * 128, :], atlo_s[c][:])
                    w.waitpool("g", "g")
                    return True

                athi_v = C["v"]
                athi_sc = C["s"]
                pbank = {0: PA[:, 0:1024], 1: PA[:, 1024:2048],
                         2: PB[:, 0:1024], 3: PC[:, 0:1024]}
                xr_toks = {}
                for tt in range(4):
                    tok0 = s_idx * S + qg * TG + tt * 128
                    wmark("q", f"xrow{tt}")
                    xr_toks[tt] = w.dmal("q", xrow_s[tt][:], xrows_d[tok0:tok0 + 128, :])
                for halfw in range(2):
                    wmark("q", "owh")
                    for cc in range(4):
                        c = 4 * halfw + cc
                        w.dmal("q", owh_s[cc][:], ow_d[c * 128:(c + 1) * 128, :])
                    ow_tok = qsn()
                    for tt in range(4):
                        ps = pbank[tt]
                        if halfw == 0:
                            if tt < 2:
                                wmark("t", "pafV")
                            else:
                                wmark("t", "atb")
                        for cc in range(4):
                            c = 4 * halfw + cc
                            for jg in range(2):
                                w.mm(ps[:, jg * 512:(jg + 1) * 512],
                                     athi_s[c][:, tt * 128:(tt + 1) * 128],
                                     owh_s[cc][:, jg * 512:(jg + 1) * 512],
                                     start=(halfw == 0 and cc == 0), stop=False,
                                     wait=[("v", athi_v), ("s", athi_sc)] + ow_tok)
                                w.mm(ps[:, jg * 512:(jg + 1) * 512],
                                     atlo_s[c][:, tt * 128:(tt + 1) * 128],
                                     owh_s[cc][:, jg * 512:(jg + 1) * 512],
                                     start=False, stop=(halfw == 1 and cc == 3),
                                     inc=("t", 1) if (tt == 3 and cc == 3
                                                      and jg == 1) else None)
                    mark("owh", "t")
                opt_t = C["t"]
                for tt in range(4):
                    ti = tt % 2
                    tok0 = s_idx * S + qg * TG + tt * 128
                    ps = pbank[tt]
                    wmark("v", f"z{ti}")
                    w.do("v", lambda E, ps=ps, ti=ti, tt=tt:
                         E.scalar_tensor_tensor(
                             z_s[ti][:], ps[:, :], 1.0, xrow_s[tt][:],
                             ALU.mult, ALU.add, accum_out=vec_s[:, 8:9]),
                         inc=("v", 1), wait=[("t", opt_t), xr_toks[tt]])
                    if tt == 1:
                        mark("pafV", "v")
                    if tt == 3:
                        mark("atb", "v")
                    mark(f"xrow{tt}", "v")
                    vC = C["v"]
                    wmark("s", "sqd")
                    w.do("s", lambda E, ti=ti: E.activation(
                        sqd_s[:], z_s[ti][:], AF.Square, accum_out=vec_s[:, 10:11]),
                         wait=[("v", vC)])
                    w.do("s", lambda E: E.mul(vec_s[:, 1:2], vec_s[:, 8:9], 1.0 / D))
                    w.drain("s")
                    w.do("s", lambda E: E.square(vec_s[:, 2:3], vec_s[:, 1:2]),
                         inc=("s", 1))
                    sC = C["s"]
                    mark("sqd", "s")
                    w.do("v", lambda E: E.scalar_tensor_tensor(
                        vec_s[:, 3:4], vec_s[:, 10:11], 1.0 / D, vec_s[:, 2:3],
                        ALU.mult, ALU.subtract), inc=("v", 1), wait=[("s", sC)])
                    vC = C["v"]
                    w.do("s", lambda E: E.activation(
                        vec_s[:, 4:5], vec_s[:, 3:4], AF.Sqrt, bias=eps5_s[:]),
                         inc=("s", 1), wait=[("v", vC)])
                    sC = C["s"]
                    w.drain("v")
                    w.do("v", lambda E: E.reciprocal(vec_s[:, 5:6], vec_s[:, 4:5]),
                         wait=[("s", sC)])
                    w.drain("v")
                    wmark("v", f"x1t{ti}")
                    w.do("v", lambda E, ti=ti: E.tensor_scalar(
                        x1t_s[ti][:], z_s[ti][:], vec_s[:, 1:2], vec_s[:, 5:6],
                        ALU.subtract, ALU.mult), inc=("v", 1))
                    vC = C["v"]
                    mark(f"z{ti}", "v")
                    x1tok = [w.dmal("g", x1f_d[tok0:tok0 + 128, :], x1t_s[ti][:],
                                    wait=[("v", vC)])]
                    if stage == "ln1":
                        x1tok.append(w.dmal("g", dbg["x1"][tok0:tok0 + 128, :],
                                            x1t_s[ti][:]))
                    w.marks[f"x1t{ti}"] = x1tok
            w.barrier()
            return False

        def slots_phase(s_idx):
            tok0 = s_idx * S
            w.do("v", lambda E: (E.memset(carr_s[:], 0.0), E.memset(onesr_s[:], 1.0),
                                 E.drain())[-1], inc=("v", 1))
            tr_tok = w.dmal("q", trrow_s[:], trrow_d.ap())
            for tg in range(NTG):
                for tl4 in range(4):
                    tile = tg * 4 + tl4
                    ti = tile % 2
                    wmark("q", f"x1t{ti}")
                    xt_tok = w.dmal("q", x1t_s[ti][:],
                                    x1f_d[tok0 + tile * 128: tok0 + (tile + 1) * 128, :])
                    tb = 2 * (tile % 2)
                    for c in range(NCH):
                        sl = PA[:, (c % 4) * 512:(c % 4) * 512 + 128]
                        wmark("t", f"pas{c % 4}")
                        w.do("t", lambda E, sl=sl, ti=ti, c=c: E.transpose(
                            sl, x1t_s[ti][:, c * 128:(c + 1) * 128], id32_s[:]),
                             inc=("t", 1), wait=[xt_tok])
                        tC = C["t"]
                        wmark("s", f"x1Tt{tb}")
                        w.do("s", lambda E, c=c, sl=sl, tb=tb: E.activation(
                            x1Tt_s[tb][:, c * 128:(c + 1) * 128], sl, AF.Copy),
                             inc=("s", 1), wait=[("t", tC)])
                        sC = C["s"]
                        wmark("v", f"x1Tt{tb + 1}")
                        w.do("v", lambda E, c=c, sl=sl, tb=tb: E.tensor_tensor(
                            x1Tt_s[tb + 1][:, c * 128:(c + 1) * 128], sl,
                            x1Tt_s[tb][:, c * 128:(c + 1) * 128], ALU.subtract),
                             inc=("v", 1), wait=[("s", sC)])
                        mark(f"pas{c % 4}", "v")
                    tile_s, tile_v = C["s"], C["v"]
                    if tl4 == 0:
                        wmark("t", "lgbank")
                    first = True
                    for dc in range(NCH):
                        lh = wrh_s[:, dc * 4:(dc + 1) * 4]
                        ll = wrl_s[:, dc * 4:(dc + 1) * 4]
                        xh = x1Tt_s[tb][:, dc * 128:(dc + 1) * 128]
                        xl = x1Tt_s[tb + 1][:, dc * 128:(dc + 1) * 128]
                        dstp = PB[0:4, tl4 * 128:(tl4 + 1) * 128]
                        w.mm(dstp, lh, xh, start=first, stop=False,
                             wait=[("s", tile_s), ("v", tile_v)])
                        first = False
                        w.mm(dstp, lh, xl, start=False, stop=False)
                        w.mm(dstp, ll, xh, start=False, stop=(dc == NCH - 1),
                             inc=("t", 1) if dc == NCH - 1 else None)
                    mark(f"x1Tt{tb}", "t")
                    mark(f"x1Tt{tb + 1}", "t")
                    mark(f"x1t{ti}", "t")
                tC = C["t"]
                wmark("v", "lgs")
                w.do("v", lambda E: E.tensor_copy(lgs_s[:], PB[0:4, 0:512]),
                     inc=("v", 1), wait=[("t", tC)])
                vC = C["v"]
                mark("lgbank", "v")
                wmark("q", "lrow")
                lr_tok = w.dmal("q", lrow_s[0:1, :].rearrange("a (p f) -> a p f", p=4),
                                lgs_s[:], wait=[("v", vC)])
                w.marks["lgs"] = [lr_tok]

                def rowops(E, tg=tg, s_idx=s_idx):
                    L, R = lrow_s, rtmp_s
                    CSUM = R[:, 4 * TG:5 * TG]
                    SLV = R[:, 5 * TG:6 * TG]
                    OV = R[:, 6 * TG:7 * TG]
                    E.tensor_tensor(R[:, 0:TG], L[:, 0:TG], L[:, TG:2 * TG], ALU.max)
                    E.tensor_tensor(R[:, TG:2 * TG], L[:, 2 * TG:3 * TG],
                                    L[:, 3 * TG:4 * TG], ALU.max)
                    E.drain()
                    E.tensor_tensor(R[:, 2 * TG:3 * TG], R[:, 0:TG], R[:, TG:2 * TG],
                                    ALU.max)
                    E.drain()
                    E.memset(R[:, 0:TG], 1.0)
                    E.memset(slotr_s[:], 0.0)
                    E.drain()
                    for e in range(NE):
                        ge = L[:, e * TG:(e + 1) * TG]
                        E.tensor_tensor(ge, ge, R[:, 2 * TG:3 * TG], ALU.is_ge)
                    E.drain()
                    for e in range(NE):
                        ge = L[:, e * TG:(e + 1) * TG]
                        me = R[:, 3 * TG:4 * TG]
                        E.tensor_tensor(me, ge, R[:, 0:TG], ALU.mult)
                        E.drain()
                        E.tensor_tensor_scan(CSUM, onesr_s[:], me,
                                             carr_s[:, e:e + 1], ALU.mult, ALU.add)
                        E.drain()
                        E.tensor_copy(carr_s[:, e:e + 1], CSUM[:, TG - 1:TG])
                        E.tensor_single_scalar(
                            SLV, CSUM, float(s_idx * SLOTS + e * CCAP - 1), ALU.add)
                        E.tensor_single_scalar(ovi_s[:], CSUM, float(CCAP), ALU.is_gt)
                        E.drain()
                        E.copy_predicated(SLV, ovi_s[:], trrow_s[:])
                        E.drain()
                        E.tensor_tensor(SLV, SLV, me, ALU.mult)
                        E.drain()
                        E.tensor_tensor(slotr_s[:], slotr_s[:], SLV, ALU.add)
                        E.drain()
                        if e < NE - 1:
                            E.tensor_tensor(R[:, 0:TG], R[:, 0:TG], me, ALU.subtract)
                            E.drain()
                    return E.drain()
                wmark("v", "slotr")
                w.do("v", rowops, inc=("v", 1), wait=[lr_tok, tr_tok])
                mark("lrow", "v")
                sr_v = C["v"]
                cb = s_idx * 16 + tg * 4
                row = s_idx * NTG + tg
                st_tok = w.dmal("g", slot_d[row:row + 1, :], slotr_s[:],
                                wait=[("v", sr_v)])
                w.marks["slotr"] = [st_tok]
                w.do("q", lambda E, cb=cb, row=row: E.dma_start(
                    out=idxf_s[:, cb:cb + 4],
                    in_=slot_d[row:row + 1, :].rearrange("a (c p) -> (a p) c", p=128),
                    allow_slow_non_contiguous=True),
                     inc=(QL[0], 16), wait=[(QL[0], C[QL[0]]), st_tok])
            idx_tok = (QL[0], C[QL[0]])
            w.do("v", lambda E, s_idx=s_idx: E.tensor_copy(
                idxi_s[:, s_idx * 16:(s_idx + 1) * 16],
                idxf_s[:, s_idx * 16:(s_idx + 1) * 16]),
                 inc=("v", 1), wait=[idx_tok])
            w.drain("v")
            idx_v = C["v"]

            if stage == "slots" and s_idx == 0:
                w.dmal("g", dbg["idx"][:, 0:16], idxf_s[:, 0:16], wait=[("v", idx_v)])
                w.waitpool("g", "g")
                return True

            from concourse.bass_core_rust import IndirectOffsetOnAxis
            for tile in range(16):
                ti = tile % 2
                wmark("q", f"x1t{ti}")
                xt_tok = w.dmal("q", x1t_s[ti][:],
                                x1f_d[tok0 + tile * 128: tok0 + (tile + 1) * 128, :])
                col = s_idx * 16 + tile
                w.do("g", lambda E, ti=ti, col=col: E.indirect_dma_start(
                    xg_d[:, :],
                    IndirectOffsetOnAxis(idxi_s[:, col:col + 1], 0),
                    x1t_s[ti][:], None),
                     inc=(GL[tile % 4], 16),
                     wait=[(GL[tile % 4], C[GL[tile % 4]]), xt_tok, ("v", idx_v)]
                     + zf_tok)
                w.marks[f"x1t{ti}"] = [(GL[tile % 4], C[GL[tile % 4]])]
            sc_tok = gsn()

            for st in range(SLOTS // 128):
                ti = st % 2
                wmark("q", f"x1t{ti}")
                xt_tok = w.dmal("q", x1t_s[ti][:],
                                xg_d[s_idx * SLOTS + st * 128:
                                     s_idx * SLOTS + (st + 1) * 128, :],
                                wait=sc_tok)
                col0 = st * 128
                for c in range(NCH):
                    sl = PA[:, (c % 4) * 512:(c % 4) * 512 + 128]
                    wmark("t", f"pas{c % 4}")
                    w.do("t", lambda E, sl=sl, ti=ti, c=c: E.transpose(
                        sl, x1t_s[ti][:, c * 128:(c + 1) * 128], id32_s[:]),
                         inc=("t", 1), wait=[xt_tok])
                    tC = C["t"]
                    wmark("s", f"xgTh{c}")
                    w.do("s", lambda E, c=c, sl=sl, col0=col0: E.activation(
                        xgTh_s[c][:, col0:col0 + 128], sl, AF.Copy),
                         inc=("s", 1), wait=[("t", tC)])
                    sC = C["s"]
                    wmark("v", f"xgTl{c}")
                    w.do("v", lambda E, c=c, sl=sl, col0=col0: E.tensor_tensor(
                        xgTl_s[c][:, col0:col0 + 128], sl,
                        xgTh_s[c][:, col0:col0 + 128], ALU.subtract),
                         inc=("v", 1), wait=[("s", sC)])
                    mark(f"pas{c % 4}", "v")
                mark(f"x1t{ti}", "t")
            w.barrier()
            return False

        def ffn_phase(s_idx):
            slot0 = s_idx * SLOTS
            sl_tok = {}

            def emit_w1split(ic):
                ip = ic % 2
                for e in range(NE):
                    sb32 = slab32_s[e % 2]
                    wmark("q", f"slab32_{e % 2}")
                    dtok = w.dmal("q", sb32[:], w1_d[e, ic])
                    wmark("s", f"slabh{ip}_{e}")
                    w.do("s", lambda E, e=e, ip=ip, sb32=sb32: E.activation(
                        slabh_s[ip][e][:], sb32[:], AF.Copy), inc=("s", 1),
                         wait=[dtok])
                    sC = C["s"]
                    wmark("v", f"slabl{ip}_{e}")
                    w.do("v", lambda E, e=e, ip=ip, sb32=sb32: E.tensor_tensor(
                        slabl_s[ip][e][:], sb32[:], slabh_s[ip][e][:], ALU.subtract),
                         inc=("v", 1), wait=[("s", sC)])
                    mark(f"slab32_{e % 2}", "v")
                sl_tok[ic] = (C["v"], C["s"])

            emit_w1split(0)
            for ic in range(NIC):
                ip = ic % 2
                if ic + 1 < NIC:
                    emit_w1split(ic + 1)
                slab_v, slab_s_cnt = sl_tok[ic]
                for e in range(NE):
                    for gr in range(CCAP // GR):
                        c0 = e * CCAP + gr * GR
                        ps = PA[:, (2 * e + gr) % 4 * 512:((2 * e + gr) % 4) * 512 + GR]
                        wmark("t", f"fps{(2 * e + gr) % 4}")
                        for dc in range(NCH):
                            lh = slabh_s[ip][e][:, dc * 128:(dc + 1) * 128]
                            ll = slabl_s[ip][e][:, dc * 128:(dc + 1) * 128]
                            xh = xgTh_s[dc][:, c0:c0 + GR]
                            xl = xgTl_s[dc][:, c0:c0 + GR]
                            w.mm(ps, lh, xh, start=(dc == 0), stop=False,
                                 wait=[("v", slab_v), ("s", slab_s_cnt)])
                            w.mm(ps, lh, xl, start=False, stop=False)
                            w.mm(ps, ll, xh, start=False, stop=(dc == NCH - 1),
                                 inc=("t", 1) if dc == NCH - 1 else None)
                        tC = C["t"]
                        gp = (2 * e + gr) % 2
                        wmark("s", f"gfull{gp}")
                        w.do("s", lambda E, ps=ps, e=e, ic=ic, gp=gp: E.activation(
                            gf_s[gp][:], ps, AF.Gelu,
                            bias=b1_s[:, e * 32 + ic:e * 32 + ic + 1]),
                             inc=("s", 1), wait=[("t", tC)])
                        mark(f"fps{(2 * e + gr) % 4}", "s")
                        w.drain("s")
                        wmark("s", f"ghisb{gp}")
                        w.do("s", lambda E, gp=gp: E.activation(
                            ghi_sb[gp][:], gf_s[gp][:], AF.Copy), inc=("s", 1))
                        sC = C["s"]
                        wmark("v", f"glosb{gp}")
                        w.do("v", lambda E, gp=gp: E.tensor_tensor(
                            glo_sb[gp][:], gf_s[gp][:], ghi_sb[gp][:], ALU.subtract),
                             inc=("v", 1), wait=[("s", sC)])
                        vC = C["v"]
                        mark(f"gfull{gp}", "v")
                        gh_tok = w.dmal("g", ghi_d[ic, :, slot0 + c0: slot0 + c0 + GR],
                                        ghi_sb[gp][:], wait=[("s", sC), ("v", vC)])
                        gl_tok = w.dmal("g", glo_d[ic, :, slot0 + c0: slot0 + c0 + GR],
                                        glo_sb[gp][:])
                        w.marks[f"ghisb{gp}"] = [gh_tok]
                        w.marks[f"glosb{gp}"] = [gl_tok]
                for e in range(NE):
                    w.marks[f"slabh{ip}_{e}"] = [("t", C["t"])]
                    w.marks[f"slabl{ip}_{e}"] = [("t", C["t"])]
            w.barrier()

        def wo_phase():
            for dst, srcp in ((gam_s, gam_d), (bet_s, bet_d), (bo_s, bo_d),
                              (ones512_s, ones512_d.ap())):
                w.dmal("q", dst[:], srcp)
            wo_tok = qsn()
            sp_tok = {}
            from concourse.bass_core_rust import IndirectOffsetOnAxis

            NWB = SLOTS // WB
            tl_toks = {}

            def emit_tail_load(s_idx, tile, ygs_tok):
                ti = tile % 2
                col = s_idx * 16 + tile
                tok0 = s_idx * S + tile * 128
                gl = GL[2 + ti]
                wmark("v", f"tlg{ti}")
                w.do("g", lambda E, ti=ti, col=col: E.indirect_dma_start(
                    x1t_s[ti][:], None,
                    yg_d[:, :],
                    IndirectOffsetOnAxis(idxi_s[:, col:col + 1], 0)),
                     inc=(gl, 16),
                     wait=[(gl, C[gl])] + ygs_tok)
                yr_tok = (gl, C[gl])
                wmark("q", f"x1r{ti}")
                x1r_tok = w.dmal("q", x1r_s[ti][:], x1f_d[tok0:tok0 + 128, :])
                tl_toks[(s_idx, tile)] = (yr_tok, x1r_tok)

            def emit_tail_chain(s_idx, tile):
                ti = tile % 2
                tok0 = s_idx * S + tile * 128
                yr_tok, x1r_tok = tl_toks.pop((s_idx, tile))
                wmark("v", f"z{ti}")
                wmark("v", f"outt{ti}")
                w.do("v", lambda E, ti=ti: E.scalar_tensor_tensor(
                    z_s[ti][:], x1t_s[ti][:], 1.0, x1r_s[ti][:],
                    ALU.mult, ALU.add, accum_out=vec_s[:, 12:13]),
                     inc=("v", 1), wait=[yr_tok, x1r_tok])
                vC = C["v"]
                mark(f"tlg{ti}", "v")
                w.do("v", lambda E, ti=ti: E.tensor_tensor(
                    outt_s[ti][:], bet_s[:], x1r_s[ti][:], ALU.add), inc=("v", 1))
                wmark("s", "sqd")
                w.do("s", lambda E, ti=ti: E.activation(
                    sqd_s[:], z_s[ti][:], AF.Square, accum_out=vec_s[:, 13:14]),
                     wait=[("v", vC)])
                w.do("s", lambda E: E.mul(vec_s[:, 1:2], vec_s[:, 12:13], 1.0 / D))
                w.drain("s")
                w.do("s", lambda E: E.square(vec_s[:, 2:3], vec_s[:, 1:2]),
                     inc=("s", 1))
                sC = C["s"]
                mark("sqd", "s")
                w.do("v", lambda E: E.scalar_tensor_tensor(
                    vec_s[:, 3:4], vec_s[:, 13:14], 1.0 / D, vec_s[:, 2:3],
                    ALU.mult, ALU.subtract), inc=("v", 1), wait=[("s", sC)])
                vC = C["v"]
                w.do("s", lambda E: E.activation(
                    vec_s[:, 4:5], vec_s[:, 3:4], AF.Sqrt, bias=eps12_s[:]),
                     inc=("s", 1), wait=[("v", vC)])
                sC = C["s"]
                w.drain("v")
                w.do("v", lambda E: E.reciprocal(vec_s[:, 5:6], vec_s[:, 4:5]),
                     wait=[("s", sC)])
                w.drain("v")
                wmark("v", "sqd")
                w.do("v", lambda E, ti=ti: E.tensor_scalar(
                    sqd_s[:], z_s[ti][:], vec_s[:, 1:2], vec_s[:, 5:6],
                    ALU.subtract, ALU.mult))
                w.drain("v")
                w.do("v", lambda E: E.tensor_tensor(
                    sqd_s[:], sqd_s[:], gam_s[:], ALU.mult))
                w.drain("v")
                w.do("v", lambda E, ti=ti: E.scalar_tensor_tensor(
                    z_s[ti][:], sqd_s[:], 1.0, outt_s[ti][:], ALU.mult, ALU.add,
                    accum_out=vec_s[:, 14:15]), inc=("v", 1))
                vC = C["v"]
                mark(f"x1r{ti}", "v")
                mark("sqd", "v")
                w.do("s", lambda E, ti=ti: E.activation(
                    sqd_s[:], z_s[ti][:], AF.Square, accum_out=vec_s[:, 15:16]),
                     wait=[("v", vC)])
                w.do("s", lambda E: E.mul(vec_s[:, 1:2], vec_s[:, 14:15], 1.0 / D))
                w.drain("s")
                w.do("s", lambda E: E.square(vec_s[:, 2:3], vec_s[:, 1:2]),
                     inc=("s", 1))
                sC = C["s"]
                mark("sqd", "s")
                w.do("v", lambda E: E.scalar_tensor_tensor(
                    vec_s[:, 3:4], vec_s[:, 15:16], 1.0 / D, vec_s[:, 2:3],
                    ALU.mult, ALU.subtract), inc=("v", 1), wait=[("s", sC)])
                vC = C["v"]
                w.do("s", lambda E: E.activation(
                    vec_s[:, 4:5], vec_s[:, 3:4], AF.Sqrt, bias=eps5_s[:]),
                     inc=("s", 1), wait=[("v", vC)])
                sC = C["s"]
                w.drain("v")
                w.do("v", lambda E: E.reciprocal(vec_s[:, 5:6], vec_s[:, 4:5]),
                     wait=[("s", sC)])
                w.drain("v")
                w.do("v", lambda E, ti=ti: E.tensor_scalar(
                    outt_s[ti][:], z_s[ti][:], vec_s[:, 1:2], vec_s[:, 5:6],
                    ALU.subtract, ALU.mult), inc=("v", 1))
                vC = C["v"]
                mark(f"z{ti}", "v")
                o_tok = w.dmal("g", out_d[tok0:tok0 + 128, :], outt_s[ti][:],
                               wait=[("v", vC)])
                w.marks[f"outt{ti}"] = [o_tok]

            tail0n = [0]
            ygs0 = [None]
            for s_idx in range(NSEQ):
              slot0 = s_idx * SLOTS
              for wb in range(NWB):
                g_ic_tok = {}
                for ic in range(NIC):
                    wmark("q", f"gsl{ic}")
                    t1 = w.dmal("q", gslh_s[ic][:],
                                ghi_d[ic, :, slot0 + wb * WB:slot0 + (wb + 1) * WB])
                    t2 = w.dmal("q", gsll_s[ic][:],
                                glo_d[ic, :, slot0 + wb * WB:slot0 + (wb + 1) * WB])
                    g_ic_tok[ic] = [t1, t2]

                def emit_wosplit(jc):
                    jp = jc % 2
                    wmark("s", f"wjh{jp}")
                    wmark("v", f"wjl{jp}")
                    for q4 in range(4):
                        sb32 = slab32_s[q4 % 2]
                        wmark("q", f"slab32_{q4 % 2}")
                        wsl_tok = w.dmal("q", sb32[:],
                                         wo_d[jc, :, q4 * 8:(q4 + 1) * 8, :])
                        w.do("s", lambda E, sb32=sb32, jp=jp, q4=q4: E.activation(
                            wjh_s[jp][:, q4 * 1024:(q4 + 1) * 1024], sb32[:], AF.Copy),
                             inc=("s", 1), wait=[wsl_tok])
                        sC = C["s"]
                        w.do("v", lambda E, sb32=sb32, jp=jp, q4=q4: E.tensor_tensor(
                            wjl_s[jp][:, q4 * 1024:(q4 + 1) * 1024], sb32[:],
                            wjh_s[jp][:, q4 * 1024:(q4 + 1) * 1024], ALU.subtract),
                             inc=("v", 1), wait=[("s", sC)])
                        mark(f"slab32_{q4 % 2}", "v")
                    sp_tok[jc] = (C["v"], C["s"])

                if wb == 0:
                    emit_wosplit(0)
                for jc in range(NCH):
                    jp = jc % 2
                    if not (jc == NCH - 1 and wb == NWB - 1):
                        emit_wosplit((jc + 1) % NCH)
                    wo_v, wo_s_cnt = sp_tok[jc]
                    ps = PB[:, jp * 512:jp * 512 + WB]
                    wmark("t", f"pbb{jp}")
                    for ic in range(NIC):
                        w.mm(ps, wjh_s[jp][:, ic * 128:(ic + 1) * 128], gslh_s[ic][:],
                             start=(ic == 0), stop=False,
                             wait=[("v", wo_v), ("s", wo_s_cnt)] + g_ic_tok[ic]
                                  + wo_tok)
                        w.mm(ps, wjh_s[jp][:, ic * 128:(ic + 1) * 128], gsll_s[ic][:],
                             start=False, stop=False)
                        w.mm(ps, wjl_s[jp][:, ic * 128:(ic + 1) * 128], gslh_s[ic][:],
                             start=False, stop=False)
                    w.mm(ps, bo_s[:, jc * 128:(jc + 1) * 128], ones512_s[:, 0:WB],
                         start=False, stop=True, inc=("t", 1))
                    tC = C["t"]
                    mark(f"wjh{jp}", "t")
                    mark(f"wjl{jp}", "t")
                    wmark("s", "ysb")
                    w.do("s", lambda E, ps=ps: E.activation(ysb_s[:], ps, AF.Copy),
                         inc=("s", 1), wait=[("t", tC)])
                    sC = C["s"]
                    mark(f"pbb{jp}", "s")
                    for quarter in range(4):
                        sl = PC[:, (quarter % 2) * 512:(quarter % 2) * 512 + 128]
                        wmark("t", f"pc{quarter % 2}")
                        w.do("t", lambda E, sl=sl, quarter=quarter: E.transpose(
                            sl, ysb_s[:, quarter * 128:quarter * 128 + 128], id32_s[:]),
                             inc=("t", 1), wait=[("s", sC)])
                        tC = C["t"]
                        wmark("v", f"yrow{quarter}")
                        w.do("v", lambda E, quarter=quarter, jc=jc, sl=sl: E.tensor_copy(
                            yrow_s[quarter][:, jc * 128:(jc + 1) * 128], sl),
                             inc=("v", 1), wait=[("t", tC)])
                        mark(f"pc{quarter % 2}", "v")
                    mark("ysb", "t")
                    if s_idx == 1:
                        gidx = wb * NCH + jc
                        if gidx >= 2 and gidx % 2 == 0 and tail0n[0] < 16:
                            t0 = tail0n[0]
                            if t0 > 0:
                                emit_tail_chain(0, t0 - 1)
                            emit_tail_load(0, t0, ygs0[0])
                            tail0n[0] += 1
                for ic in range(NIC):
                    w.marks[f"gsl{ic}"] = [("t", C["t"])]
                y_v = C["v"]
                for quarter in range(4):
                    r0 = slot0 + wb * WB + quarter * 128
                    yg_tok = w.dmal("g", yg_d[r0:r0 + 128, :], yrow_s[quarter][:],
                                    wait=[("v", y_v)])
                    w.marks[f"yrow{quarter}"] = [yg_tok]
              if s_idx == 0:
                  ygs0[0] = gsn()
            emit_tail_chain(0, 15)
            ygs1 = gsn()
            emit_tail_load(1, 0, ygs1)
            for tile in range(16):
                if tile + 1 < 16:
                    emit_tail_load(1, tile + 1, ygs1)
                emit_tail_chain(1, tile)
            w.waitpool("g", "g")

        for s_idx in range(NSEQ):
            qkv_phase(s_idx)
            if stage == "qkv" and s_idx == 0:
                vv, ss_ = C["v"], C["s"]
                for c in range(NCH):
                    w.dmal("g", dbg["qT"][c * 128:(c + 1) * 128, :], qT_s[c][:],
                           wait=[("v", vv), ("s", ss_)])
                    w.dmal("g", dbg["kT"][c * 128:(c + 1) * 128, :], kT_s[c][:])
                w.dmal("g", dbg["vp"][:], vp_s[:])
                w.waitpool("g", "g")
                return
            if attn_phase(s_idx):
                return
            if slots_phase(s_idx):
                return
            ffn_phase(s_idx)
        if stage == "ffn":
            w.waitpool("g", "g")
            return
        wo_phase()

    with nc.Block() as block:
        @block.sync
        def _(eng):
            walk("q", eng)

        @block.gpsimd
        def _(eng):
            walk("g", eng)

        @block.tensor
        def _(eng):
            walk("t", eng)

        @block.scalar
        def _(eng):
            walk("s", eng)

        @block.vector
        def _(eng):
            walk("v", eng)

    return nc


def _stage_inputs(inputs):
    x = np.asarray(inputs["L0_attn_x"])
    qw = np.asarray(inputs["L0_attn_qw"])
    kw = np.asarray(inputs["L0_attn_kw"])
    vw = np.asarray(inputs["L0_attn_vw"])
    ow = np.asarray(inputs["L0_attn_ow"])
    wr = np.asarray(inputs["L0_ffn_W_router"])
    w1 = np.asarray(inputs["L0_ffn_W1"])
    b1 = np.asarray(inputs["L0_ffn_b1"])
    wo = np.asarray(inputs["L0_ffn_Wo"])
    bo = np.asarray(inputs["L0_ffn_bo"])
    gam = np.asarray(inputs["L0_ffn_ln_gamma"])
    bet = np.asarray(inputs["L0_ffn_ln_beta"])

    perm = []
    for qd in range(4):
        for par in range(2):
            for m in range(4):
                h = 4 * qd + m
                for i in range(32):
                    perm.append(64 * h + 2 * i + par)
    perm = np.array(perm)
    qwp = np.ascontiguousarray(qw[:, perm])
    kwp = np.ascontiguousarray(kw[:, perm])
    vwp = np.zeros((D, 1040), np.float16)
    for h in range(NH):
        vwp[:, 65 * h:65 * h + 64] = vw[:, 64 * h:64 * h + 64]
    wrc = np.ascontiguousarray(
        wr.reshape(8, 128, 4).transpose(1, 0, 2).reshape(128, 32)).astype(np.float32)
    w1r = np.ascontiguousarray(
        w1.reshape(NE, 8, 128, NIC, 128).transpose(0, 3, 2, 1, 4)
        .reshape(NE, NIC, 128, 1024)).astype(np.float32)
    b1c = np.ascontiguousarray(
        b1.reshape(NE, NIC, 128).transpose(2, 0, 1).reshape(128, NE * NIC)
    ).astype(np.float32)
    gamt = np.ascontiguousarray(np.broadcast_to(gam[None, :], (128, D))).astype(np.float32)
    bett = np.ascontiguousarray(np.broadcast_to(bet[None, :], (128, D))).astype(np.float32)
    bor = np.ascontiguousarray(bo[None, :]).astype(np.float32)
    wor2 = np.ascontiguousarray(
        wo.reshape(NIC, 128, NCH, 128).transpose(2, 1, 0, 3)).astype(np.float32)

    in_maps = []
    for c in range(8):
        xc = x[2 * c:2 * c + 2]
        xrows = np.ascontiguousarray(xc.reshape(T, D))
        xT = np.ascontiguousarray(xc.reshape(T, D).T)
        in_maps.append({
            "xT": xT, "xrows": xrows,
            "qwp": qwp, "kwp": kwp, "vwp": vwp, "oww": np.ascontiguousarray(ow),
            "wrc": wrc, "w1r": w1r, "b1c": b1c,
            "wor2": wor2, "bor": bor, "gamt": gamt, "bett": bett,
        })
    return in_maps


def kernel(**inputs):
    in_maps = _stage_inputs(inputs)
    nc = build(stage="full", debug=False)
    res = run_bass_kernel_spmd(nc, in_maps, core_ids=list(range(8)))
    out = np.concatenate([np.asarray(r["out"]) for r in res.results], axis=0)
    return out.reshape(16, 2048, 1024).astype(np.float32)
